# version 08
# baseline (speedup 1.0000x reference)
import os
import numpy as np
import ml_dtypes
import concourse.bass as bass
import concourse.mybir as mybir
from concourse.bass_utils import run_bass_kernel_spmd
from contextlib import ExitStack

F32 = mybir.dt.float32
BF16 = mybir.dt.bfloat16
AF = mybir.ActivationFunctionType
ALU = mybir.AluOpType
AX = mybir.AxisListType

NCORES = 8
L = 4096
D = 1024
DI = 2048
NST = 16
DTR = 64
KC = 4
NL = 4
IFF = 2816
TC1 = 1024
NCH1 = L // TC1
TC2 = 1024
NCH2 = L // TC2
NKD = D // 128
NDT = DI // 128
NTT = L // 128
NIC = IFF // 128
NSV = 8
MAGIC = float(np.float32(1.5 * 2 ** 23))
EPS_LN = 1e-5
EPS_RMS = 1e-5
HALO = KC - 1

DEBUG_TAPS = bool(int(os.environ.get("MK_DEBUG_TAPS", "0")))

ENGS = ("sync", "scalar", "vector", "tensor", "gpsimd")


class _Pending:
    __slots__ = ("incs",)

    def __init__(self):
        self.incs = []

    def then_inc(self, sem, by=1):
        self.incs.append((sem, by))
        return self


class _Rec:
    def __init__(self, prog, name):
        self._prog = prog
        self._name = name

    def __getattr__(self, op):
        def call(*a, **k):
            p = _Pending()
            self._prog[self._name].append((op, a, k, p))
            return p
        return call


class Ctx:
    def __init__(self, nc):
        self.nc = nc
        self.cnt = {}

    def phase(self, body):
        prog = {n: [] for n in ENGS}

        class E:
            pass
        e = E()
        for n in ENGS:
            setattr(e, n, _Rec(prog, n))
        body(e)
        with self.nc.Block() as blk:
            for n in ENGS:
                items = prog[n]
                if not items:
                    continue

                def make(items):
                    def f(eng):
                        for (op, a, k, p) in items:
                            ins = getattr(eng, op)(*a, **k)
                            for (sem, by) in p.incs:
                                ins.then_inc(sem, by)
                    return f
                getattr(blk, n)(make(items))

    def inc(self, pending, sem, by=1):
        self.cnt[sem] = self.cnt.get(sem, 0) + by
        pending.then_inc(sem, by)

    def dinc(self, pending, sem):
        self.inc(pending, sem, 16)

    def at(self, sem):
        return self.cnt.get(sem, 0)

    def wait(self, rec, sem, val=None):
        v = self.at(sem) if val is None else val
        if v > 0:
            rec.wait_ge(sem, v)

    def wait_back(self, rec, sem, n_back, step=1):
        v = self.at(sem) - n_back * step
        if v > 0:
            rec.wait_ge(sem, v)



_UNIQ = [0]


def _sb(nc, name, shape, dt):
    _UNIQ[0] += 1
    return nc.sbuf_tensor(f"{name}_{_UNIQ[0]}", shape, dt)


def _pt(nc, name, shape, dt):
    _UNIQ[0] += 1
    return nc.psum_tensor(f"{name}_{_UNIQ[0]}", shape, dt)


def build_program():
    nc = bass.Bass("TRN2", target_bir_lowering=False, debug=False,
                   detect_race_conditions=False)
    ctx = Ctx(nc)
    di = {}

    def inp(name, shape, dt=F32):
        di[name] = nc.dram_tensor(name, shape, dt, kind="ExternalInput").ap()

    inp("x_D", [D, L])
    inp("x_T", [L, D])
    inp("Win", [NL, D, 2 * DI])
    inp("Wx", [NL, DI, DTR + 2 * NST])
    inp("Wdt", [NL, DTR, DI])
    inp("Wblk", [NL, DI, D])
    inp("A_logP", [NL, NDT, 128, NST])
    inp("convdiag", [NL, NDT, KC, 128, 128])
    inp("convb_cols", [NL, NDT, 128])
    inp("bdt_cols", [NL, NDT, 128])
    inp("Dp_cols", [NL, NDT, 128])
    inp("normw_cols", [NL, NKD, 128])
    inp("Wout", [D, D])
    inp("bout", [1, D])
    inp("wmod", [D, 6 * D])
    inp("c_cols", [128, NKD])
    inp("wg", [IFF, D])
    inp("wu", [IFF, D])
    inp("wdn", [D, IFF])
    inp("ident", [128, 128], BF16)
    inp("ones_colf", [128, 1])
    inp("ones_rowf", [1, 128])

    out_t = nc.dram_tensor("out", [L, D], F32, kind="ExternalOutput").ap()
    scr = {
        "x_resid": nc.dram_tensor("x_resid", [D, L], F32).ap(),
        "xc_sp": nc.dram_tensor("xc_sp", [DI, L], BF16).ap(),
        "sz_sp": nc.dram_tensor("sz_sp", [DI, L], BF16).ap(),
        "bc_sp": nc.dram_tensor("bc_sp", [2 * NST, L], F32).ap(),
        "x1_sp": nc.dram_tensor("x1_sp", [L, D], F32).ap(),
        "row_sp": nc.dram_tensor("row_sp", [1, 8 * D], F32).ap(),
        "wgqT_sp": nc.dram_tensor("wgqT_sp", [D, IFF], BF16).ap(),
        "wuqT_sp": nc.dram_tensor("wuqT_sp", [D, IFF], BF16).ap(),
        "wdqT_sp": nc.dram_tensor("wdqT_sp", [IFF, D], BF16).ap(),
    }
    taps = {}
    if DEBUG_TAPS:
        for nm, shp in [("t_h0", [D, L]), ("t_dbc0", [96, L]), ("t_res1", [D, L]),
                        ("t_res4", [D, L]), ("t_x1", [L, D]), ("t_del0", [128, L]),
                        ("t_y0", [128, NDT * TC2]), ("t_y1", [128, NDT * TC2]), ("t_w00", [128, TC2]),
                        ("t_h00", [128, TC2]), ("t_dA00", [128, TC2]),
                        ("t_du0", [128, TC2]), ("t_B0", [128, NST * TC2])]:
            taps[nm] = nc.dram_tensor(nm, shp, F32, kind="ExternalOutput").ap()

    with ExitStack() as es:
        E = es.enter_context
        g = {}
        for nm, shape, dt in [
            ("ca1_c", [128, NKD], F32), ("sa_c", [128, NKD], F32),
            ("identb", [128, 128], BF16),
            ("ones_cf", [128, 1], F32), ("ones_rf", [1, 128], F32),
            ("convb_c", [128, NL * NDT], F32), ("bdt_c", [128, NL * NDT], F32),
            ("dp_c", [128, NL * NDT], F32), ("nw_c", [128, NL * NKD], F32),
            ("a_c", [128, NL * NDT * NST], F32),
            ("cst", [128, 4], F32),
        ]:
            g[nm] = E(_sb(nc, nm, shape, dt))

        SN = ["d_x", "d_xG", "d_w", "d_spa", "d_spb", "d_spc", "d_bc", "d_bcH",
              "d_t", "d_q", "d_st", "d_c", "d_spH",
              "sc_sq", "sc_rstd", "sc_sp", "sc_dA", "sc_dAg", "sc_xz", "sc_cv",
              "sc_q", "sc_a", "sc_m2", "sc_ln",
              "v_rb", "v_xn", "v_del", "v_scf", "v_dt", "v_p3", "v_evac",
              "v_halo", "v_q", "v_tr", "v_misc", "v_bm", "v_stat", "v_var",
              "v_du", "v_cm",
              "g_scl", "g_a", "g_b", "g_scf", "g_yd", "g_wb", "g_acc",
              "t_ssq", "t_bc", "t_xz", "t_conv", "t_wx", "t_del", "t_p3",
              "t_tr", "t_mm", "t_gu", "t_dn", "t_yac"]
        S = {n: E(nc.semaphore(n)) for n in SN}

        a = dict(nc=nc, ctx=ctx, S=S, di=di, g=g, scr=scr, out_t=out_t, taps=taps)
        phase_consts(**a)
        phase_mod(**a)
        phase_ln_pro(**a)
        for layer in range(NL):
            with ExitStack() as esl:
                El = esl.enter_context
                lay = {
                    "dt_full": El(_sb(nc, "dt_full", [64, L], BF16)),
                }
                with ExitStack() as es1:
                    E1 = es1.enter_context
                    w1 = {
                        "win_w": E1(_sb(nc, "win_w", [128, NKD * 2 * DI], BF16)),
                        "wx_w": E1(_sb(nc, "wx_w", [128, NDT * 96], BF16)),
                        "cdiag_w": E1(_sb(nc, "cdiag_w",
                                                     [128, NDT * KC * 128], BF16)),
                        "xp_buf": E1(_sb(nc, "xp_buf",
                                                    [128, NDT * (HALO + TC1)], BF16)),
                        "dbc_sb": E1(_sb(nc, "dbc_sb", [96, L], F32)),
                    }
                    phase_pass1(layer=layer, lay=lay, w1=w1, **a)
                with ExitStack() as es2:
                    E2 = es2.enter_context
                    w2 = {
                        "wblk_w": E2(_sb(nc, "wblk_w", [128, NDT * D], BF16)),
                        "wdt_w": E2(_sb(nc, "wdt_w", [64, DI], BF16)),
                        "carry": E2(_sb(nc, "carry", [128, NDT * NST], F32)),
                    }
                    phase_pass23(layer=layer, lay=lay, w2=w2, **a)
        phase_wout(**a)
        phase_ffn_wq(**a)
        for half in range(4):
            phase_ffn_half(half=half, **a)
    return nc


def phase_consts(nc, ctx, S, di, g, **_):
    def body(e):
        sy, sc = e.sync, e.scalar
        for dst, src in [
            (g["identb"][:], di["ident"]), (g["ones_cf"][:], di["ones_colf"]),
            (g["ones_rf"][:], di["ones_rowf"]),
        ]:
            ctx.dinc(sy.dma_start(out=dst, in_=src), S["d_c"])
        for nm, key, w in [("convb_c", "convb_cols", NDT), ("bdt_c", "bdt_cols", NDT),
                           ("dp_c", "Dp_cols", NDT), ("nw_c", "normw_cols", NKD)]:
            ctx.dinc(sy.dma_start(
                out=g[nm][:].rearrange("p (l t) -> p l t", t=w),
                in_=di[key].rearrange("l t p -> p l t"),
                allow_slow_non_contiguous=True), S["d_c"])
        ctx.dinc(sy.dma_start(
            out=g["a_c"][:].rearrange("p (l t n) -> p l t n", t=NDT, n=NST),
            in_=di["A_logP"].rearrange("l t p n -> p l t n"),
            allow_slow_non_contiguous=True), S["d_c"])
        v = e.vector
        v.memset(g["cst"][:, 0:1], EPS_LN)
        v.memset(g["cst"][:, 1:2], -0.5)
        ctx.inc(v.memset(g["cst"][:, 2:3], 1.0 / D), S["v_misc"])
        ctx.wait(sc, S["d_c"])
        ctx.wait(sc, S["v_misc"])
        sc.activation(g["a_c"][:], g["a_c"][:], AF.Exp)
        ctx.inc(sc.mul(g["a_c"][:], g["a_c"][:], -1.0), S["sc_a"])
    ctx.phase(body)


def phase_mod(nc, ctx, S, di, g, scr, **_):
    NMOD = (6 * D) // 512
    with ExitStack() as es:
        E = es.enter_context
        cc = E(_sb(nc, "cc", [128, NKD], F32))
        ccb = E(_sb(nc, "ccb", [128, NKD], BF16))
        wmod_sb = [E(_sb(nc, f"wmod{j}", [128, 512], BF16)) for j in range(2)]
        mod_ps = E(_pt(nc, "mod_ps", [1, 512], F32))
        modv = E(_sb(nc, "modv", [1, 6 * D], F32))
        gm = E(_sb(nc, "gm", [1, 2 * D], F32))
        gsg = E(_sb(nc, "gsg", [1, 2 * D], F32))

        def body(e):
            sy, sc, v, t, gp = e.sync, e.scalar, e.vector, e.tensor, e.gpsimd
            b_dw = ctx.at(S["d_w"])
            b_mm = ctx.at(S["t_mm"])
            b_ve = ctx.at(S["v_misc"])
            ctx.dinc(sy.dma_start(out=cc[:], in_=di["c_cols"]), S["d_x"])
            ctx.wait(sc, S["d_x"])
            ctx.inc(sc.activation(ccb[:], cc[:], AF.Silu), S["sc_a"])
            for nchi in range(NMOD):
                for k in range(NKD):
                    idx = nchi * NKD + k
                    if idx >= 2:
                        gp.wait_ge(S["t_mm"], b_mm + idx - 1)
                    ctx.dinc(gp.dma_start(
                        out=wmod_sb[idx % 2][:],
                        in_=di["wmod"][k * 128:(k + 1) * 128,
                                       nchi * 512:(nchi + 1) * 512]), S["d_w"])
                    if idx == 0:
                        ctx.wait(t, S["sc_a"])
                    t.wait_ge(S["d_w"], b_dw + 16 * (idx + 1))
                    if k == 0 and nchi > 0:
                        t.wait_ge(S["v_misc"], b_ve + nchi)
                    ctx.inc(t.matmul(mod_ps[:], ccb[:, k:k + 1], wmod_sb[idx % 2][:],
                                     start=(k == 0), stop=(k == NKD - 1)), S["t_mm"])
                v.wait_ge(S["t_mm"], b_mm + (nchi + 1) * NKD)
                ctx.inc(v.tensor_copy(modv[0:1, nchi * 512:(nchi + 1) * 512],
                                      mod_ps[0:1, :]), S["v_misc"])
            v.tensor_copy(gm[0:1, 0:D], modv[0:1, 2 * D:3 * D])
            v.tensor_copy(gm[0:1, D:2 * D], modv[0:1, 5 * D:6 * D])
            v.tensor_scalar(gsg[0:1, :], gm[0:1, :], 0.0, None, ALU.is_ge)
            v.tensor_scalar(gsg[0:1, :], gsg[0:1, :], 2.0, -1.0, ALU.mult, ALU.add)
            v.tensor_tensor(gm[0:1, :], gm[0:1, :], gsg[0:1, :], ALU.mult)
            v.tensor_scalar_max(gm[0:1, :], gm[0:1, :], 0.25)
            v.tensor_tensor(gm[0:1, :], gm[0:1, :], gsg[0:1, :], ALU.mult)
            v.tensor_copy(modv[0:1, 2 * D:3 * D], gm[0:1, 0:D])
            v.tensor_copy(modv[0:1, 5 * D:6 * D], gm[0:1, D:2 * D])
            v.tensor_scalar_add(modv[0:1, 1 * D:2 * D], modv[0:1, 1 * D:2 * D], 1.0)
            ctx.inc(v.tensor_scalar_add(modv[0:1, 4 * D:5 * D],
                                        modv[0:1, 4 * D:5 * D], 1.0), S["v_misc"])
            ctx.wait(sy, S["v_misc"])
            ctx.dinc(sy.dma_start(out=scr["row_sp"][0:1, 0:6 * D], in_=modv[0:1, :]),
                     S["d_spH"])
            ctx.dinc(sy.dma_start(out=scr["row_sp"][0:1, 6 * D:7 * D], in_=di["bout"]),
                     S["d_spH"])
            ctx.wait(sy, S["d_spH"])
            ctx.dinc(sy.dma_start(
                out=g["ca1_c"][:].rearrange("p (k o) -> p k o", o=1),
                in_=scr["row_sp"][0:1, D:2 * D].rearrange("o (k p) -> p k o", p=128),
                allow_slow_non_contiguous=True), S["d_bcH"])
            ctx.dinc(sy.dma_start(
                out=g["sa_c"][:].rearrange("p (k o) -> p k o", o=1),
                in_=scr["row_sp"][0:1, 0:D].rearrange("o (k p) -> p k o", p=128),
                allow_slow_non_contiguous=True), S["d_bcH"])
            ctx.wait(sy, S["d_bcH"])
        ctx.phase(body)


def phase_ln_pro(nc, ctx, S, di, g, scr, taps, **_):
    PT = 1024
    NCH = L // PT
    NBC = PT // 512
    spk = 2 if DEBUG_TAPS else 1
    with ExitStack() as es:
        E = es.enter_context
        xt = E(_sb(nc, "lp_x", [128, NKD * PT], F32))
        sqb = [E(_sb(nc, f"lp_sq{j}", [128, PT], F32)) for j in range(2)]
        s1_ps = E(_pt(nc, "lp_s1", [1, PT], F32))
        s2_ps = E(_pt(nc, "lp_s2", [1, PT], F32))
        mrow = E(_sb(nc, "lp_m", [1, PT], F32))
        rrow = E(_sb(nc, "lp_r", [1, PT], F32))
        b_ps = E(_pt(nc, "lp_bps", [128, 512], F32))
        m_b = E(_sb(nc, "lp_mb", [128, PT], F32))
        r_b = E(_sb(nc, "lp_rb", [128, PT], F32))
        ho = [E(_sb(nc, f"lp_h{j}", [128, PT], F32)) for j in range(2)]

        def body(e):
            sy, sc, v, t, gp = e.sync, e.scalar, e.vector, e.tensor, e.gpsimd
            for ch in range(NCH):
                c0 = ch * PT
                if ch > 0:
                    ctx.wait(sy, S["v_misc"])
                for k in range(NKD):
                    ctx.dinc(sy.dma_start(
                        out=xt[:, k * PT:(k + 1) * PT],
                        in_=di["x_D"][k * 128:(k + 1) * 128, c0:c0 + PT]), S["d_x"])
                ctx.wait(sc, S["d_x"])
                ctx.wait(sc, S["d_bcH"])
                for k in range(NKD):
                    if k >= 2:
                        ctx.wait_back(sc, S["t_ssq"], 1)
                    ctx.inc(sc.activation(sqb[k % 2][:], xt[:, k * PT:(k + 1) * PT],
                                          AF.Square), S["sc_sq"])
                    ctx.wait(t, S["sc_sq"])
                    if k == 0 and ch > 0:
                        ctx.wait(t, S["v_misc"])
                    last = None
                    for nn in range(NBC):
                        t.matmul(s2_ps[0:1, nn * 512:(nn + 1) * 512], g["ones_cf"][:],
                                 sqb[k % 2][:, nn * 512:(nn + 1) * 512],
                                 start=(k == 0), stop=(k == NKD - 1))
                        last = t.matmul(
                            s1_ps[0:1, nn * 512:(nn + 1) * 512], g["ones_cf"][:],
                            xt[:, k * PT + nn * 512:k * PT + (nn + 1) * 512],
                            start=(k == 0), stop=(k == NKD - 1))
                    ctx.inc(last, S["t_ssq"])
                ctx.wait(v, S["t_ssq"])
                v.tensor_scalar_mul(mrow[:], s1_ps[0:1, :], 1.0 / D)
                v.tensor_tensor(rrow[:], mrow[:], mrow[:], ALU.mult)
                ctx.inc(v.scalar_tensor_tensor(rrow[:], s2_ps[0:1, :], 1.0 / D,
                                               rrow[:], ALU.mult, ALU.subtract),
                        S["v_misc"])
                ctx.wait(sc, S["v_misc"])
                sc.activation(rrow[:], rrow[:], AF.Ln, bias=g["cst"][0:1, 0:1])
                ctx.inc(sc.activation(rrow[:], rrow[:], AF.Exp,
                                      scale=g["cst"][0:1, 1:2]), S["sc_rstd"])
                ctx.wait(t, S["sc_rstd"])
                for nn in range(NBC):
                    if not (ch == 0 and nn == 0):
                        ctx.wait(t, S["v_rb"])
                    ctx.inc(t.matmul(b_ps[:], g["ones_rf"][:],
                                     mrow[0:1, nn * 512:(nn + 1) * 512],
                                     start=True, stop=True), S["t_bc"])
                    ctx.wait(v, S["t_bc"])
                    ctx.inc(v.tensor_copy(m_b[:, nn * 512:(nn + 1) * 512], b_ps[:]),
                            S["v_rb"])
                    ctx.wait(t, S["v_rb"])
                    ctx.inc(t.matmul(b_ps[:], g["ones_rf"][:],
                                     rrow[0:1, nn * 512:(nn + 1) * 512],
                                     start=True, stop=True), S["t_bc"])
                    ctx.wait(v, S["t_bc"])
                    ctx.inc(v.tensor_copy(r_b[:, nn * 512:(nn + 1) * 512], b_ps[:]),
                            S["v_rb"])
                for k in range(NKD):
                    hb = ho[k % 2]
                    if k >= 2:
                        ctx.wait_back(v, S["d_spa"], spk, step=16)
                    v.tensor_tensor(hb[:], xt[:, k * PT:(k + 1) * PT], m_b[:],
                                    ALU.subtract)
                    v.tensor_tensor(hb[:], hb[:], r_b[:], ALU.mult)
                    ctx.inc(v.tensor_scalar(hb[:], hb[:], g["ca1_c"][:, k:k + 1],
                                            g["sa_c"][:, k:k + 1], ALU.mult, ALU.add),
                            S["v_misc"])
                    ctx.wait(gp, S["v_misc"])
                    ctx.dinc(gp.dma_start(
                        out=scr["x_resid"][k * 128:(k + 1) * 128, c0:c0 + PT],
                        in_=hb[:]), S["d_spa"])
                    if DEBUG_TAPS:
                        ctx.dinc(gp.dma_start(
                            out=taps["t_h0"][k * 128:(k + 1) * 128, c0:c0 + PT],
                            in_=hb[:]), S["d_spa"])
            ctx.wait(gp, S["d_spa"])
        ctx.phase(body)


def phase_pass1(nc, ctx, S, di, g, scr, taps, layer, lay, w1, **_):
    NBC = TC1 // 512
    with ExitStack() as es:
        E = es.enter_context
        xt = E(_sb(nc, "p1_x", [128, NKD * TC1], BF16))
        sqb = [E(_sb(nc, f"p1_sq{j}", [128, TC1], F32)) for j in range(2)]
        s2_ps = E(_pt(nc, "p1_s2", [1, TC1], F32))
        rrow = E(_sb(nc, "p1_r", [1, TC1], F32))
        b_ps = E(_pt(nc, "p1_b", [128, 512], F32))
        r_b = E(_sb(nc, "p1_rb", [128, TC1], F32))
        xn = E(_sb(nc, "p1_xn", [128, NKD * TC1], BF16))
        xz_ps = [E(_pt(nc, f"p1_xz{j}", [128, 512], F32)) for j in range(2)]
        cv_ps = E(_pt(nc, "p1_cv", [128, 512], F32))
        dbc_ps = E(_pt(nc, "p1_dbc", [96, TC1], F32))
        sz_t = [E(_sb(nc, f"p1_sz{j}", [128, 512], BF16)) for j in range(2)]
        xc_t = [E(_sb(nc, f"p1_xc{j}", [128, TC1], BF16)) for j in range(2)]

        def body(e):
            sy, sc, v, t, gp = e.sync, e.scalar, e.vector, e.tensor, e.gpsimd
            b_tcv = ctx.at(S["t_conv"])
            b_vh = ctx.at(S["v_halo"])
            b_txz = ctx.at(S["t_xz"])
            for k in range(NKD):
                ctx.dinc(gp.dma_start(
                    out=w1["win_w"][:, k * (2 * DI):(k + 1) * (2 * DI)],
                    in_=di["Win"][layer, k * 128:(k + 1) * 128, :]), S["d_w"])
            for kd in range(NDT):
                ctx.dinc(gp.dma_start(
                    out=w1["wx_w"][:, kd * 96:(kd + 1) * 96],
                    in_=di["Wx"][layer, kd * 128:(kd + 1) * 128, :]), S["d_w"])
            ctx.dinc(gp.dma_start(
                out=w1["cdiag_w"][:].rearrange("p (t k m) -> p t k m", t=NDT, k=KC),
                in_=di["convdiag"][layer].rearrange("t k p m -> p t k m")), S["d_w"])
            hv = w1["xp_buf"][:].rearrange("p (t c) -> p t c",
                                           c=HALO + TC1)[:, :, 0:HALO]
            ctx.inc(e.vector.memset(hv, 0.0), S["v_halo"])
            for ch in range(NCH1):
                c0 = ch * TC1
                if ch > 0:
                    ctx.wait(gp, S["v_xn"])
                for k in range(NKD):
                    ctx.dinc(gp.dma_start(
                        out=xt[:, k * TC1:(k + 1) * TC1],
                        in_=scr["x_resid"][k * 128:(k + 1) * 128, c0:c0 + TC1]),
                        S["d_xG"])
                ctx.wait(sc, S["d_xG"])
                for k in range(NKD):
                    if k >= 2:
                        ctx.wait_back(sc, S["t_ssq"], 1)
                    ctx.inc(sc.activation(sqb[k % 2][:], xt[:, k * TC1:(k + 1) * TC1],
                                          AF.Square), S["sc_sq"])
                    ctx.wait(t, S["sc_sq"])
                    if k == 0:
                        ctx.wait(t, S["sc_rstd"])
                        ctx.wait(t, S["d_w"])
                    last = None
                    for nn in range(NBC):
                        last = t.matmul(
                            s2_ps[0:1, nn * 512:(nn + 1) * 512], g["ones_cf"][:],
                            sqb[k % 2][:, nn * 512:(nn + 1) * 512],
                            start=(k == 0), stop=(k == NKD - 1))
                    ctx.inc(last, S["t_ssq"])
                ctx.wait(sc, S["t_ssq"])
                sc.activation(rrow[:], s2_ps[0:1, :], AF.Ln,
                              scale=g["cst"][0:1, 2:3], bias=g["cst"][0:1, 0:1])
                ctx.inc(sc.activation(rrow[:], rrow[:], AF.Exp,
                                      scale=g["cst"][0:1, 1:2]), S["sc_rstd"])
                ctx.wait(t, S["sc_rstd"])
                for nn in range(NBC):
                    if not (ch == 0 and layer == 0 and nn == 0):
                        ctx.wait(t, S["v_rb"])
                    ctx.inc(t.matmul(b_ps[:], g["ones_rf"][:],
                                     rrow[0:1, nn * 512:(nn + 1) * 512],
                                     start=True, stop=True), S["t_bc"])
                    ctx.wait(v, S["t_bc"])
                    ctx.inc(v.tensor_copy(r_b[:, nn * 512:(nn + 1) * 512], b_ps[:]),
                            S["v_rb"])
                if ch > 0:
                    v.wait_ge(S["t_xz"], b_txz + ch * 64)
                for k in range(NKD):
                    ctx.inc(v.scalar_tensor_tensor(
                        xn[:, k * TC1:(k + 1) * TC1], xt[:, k * TC1:(k + 1) * TC1],
                        g["nw_c"][:, layer * NKD + k:layer * NKD + k + 1], r_b[:],
                        ALU.mult, ALU.mult), S["v_xn"])
                for mt in range(32):
                    for nn in range(NBC):
                        gl = mt * NBC + nn
                        pj = gl % 2
                        if gl == 0:
                            ctx.wait(t, S["v_xn"])
                        if gl >= 2:
                            ctx.wait_back(t, S["sc_xz"], 1)
                        last = None
                        for k in range(NKD):
                            last = t.matmul(
                                xz_ps[pj][:],
                                w1["win_w"][:, k * 2 * DI + mt * 128:
                                            k * 2 * DI + (mt + 1) * 128],
                                xn[:, k * TC1 + nn * 512:k * TC1 + (nn + 1) * 512],
                                start=(k == 0), stop=(k == NKD - 1))
                        ctx.inc(last, S["t_xz"])
                        ctx.wait(sc, S["t_xz"])
                        if mt < NDT:
                            if ch > 0 and nn == 0:
                                sc.wait_ge(S["t_conv"],
                                           b_tcv + (ch - 1) * 32 + (mt + 1) * 2)
                                sc.wait_ge(S["v_halo"],
                                           b_vh + 1 + (ch - 1) * NDT + mt + 1)
                            dst = w1["xp_buf"][:,
                                               mt * (HALO + TC1) + HALO + nn * 512:
                                               mt * (HALO + TC1) + HALO +
                                               (nn + 1) * 512]
                            ctx.inc(sc.copy(dst, xz_ps[pj][:]), S["sc_xz"])
                        else:
                            zi = (mt - NDT) * NBC + nn
                            if zi >= 2:
                                ctx.wait_back(sc, S["d_spb"], 1, step=16)
                            ctx.inc(sc.activation(sz_t[zi % 2][:], xz_ps[pj][:],
                                                  AF.Silu), S["sc_xz"])
                            ctx.wait(gp, S["sc_xz"])
                            dt_i = mt - NDT
                            ctx.dinc(gp.dma_start(
                                out=scr["sz_sp"][dt_i * 128:(dt_i + 1) * 128,
                                                 c0 + nn * 512:c0 + (nn + 1) * 512],
                                in_=sz_t[zi % 2][:]), S["d_spb"])
                    if mt < NDT:
                        base = mt * (HALO + TC1)
                        ctx.wait(t, S["sc_xz"])
                        for nn in range(NBC):
                            if not (layer == 0 and ch == 0 and mt == 0 and nn == 0):
                                ctx.wait(t, S["sc_cv"])
                            last = None
                            for kk in range(KC):
                                last = t.matmul(
                                    cv_ps[:],
                                    w1["cdiag_w"][:, (mt * KC + kk) * 128:
                                                  (mt * KC + kk + 1) * 128],
                                    w1["xp_buf"][:, base + nn * 512 + kk:
                                                 base + nn * 512 + kk + 512],
                                    start=(kk == 0), stop=(kk == KC - 1))
                            ctx.inc(last, S["t_conv"])
                            ctx.wait(sc, S["t_conv"])
                            if mt >= 2 and nn == 0:
                                ctx.wait_back(sc, S["t_wx"], 1)
                                ctx.wait_back(sc, S["d_spa"], 1, step=16)
                            ctx.inc(sc.activation(
                                xc_t[mt % 2][:, nn * 512:(nn + 1) * 512], cv_ps[:],
                                AF.Silu,
                                bias=g["convb_c"][:, layer * NDT + mt:
                                                  layer * NDT + mt + 1]), S["sc_cv"])
                        ctx.wait(gp, S["sc_cv"])
                        ctx.dinc(gp.dma_start(
                            out=scr["xc_sp"][mt * 128:(mt + 1) * 128, c0:c0 + TC1],
                            in_=xc_t[mt % 2][:]), S["d_spa"])
                        ctx.wait(t, S["sc_cv"])
                        if mt == 0 and not (ch == 0 and layer == 0):
                            ctx.wait(t, S["v_evac"])
                        last = None
                        for nn in range(NBC):
                            last = t.matmul(dbc_ps[:, nn * 512:(nn + 1) * 512],
                                            w1["wx_w"][:, mt * 96:(mt + 1) * 96],
                                            xc_t[mt % 2][:, nn * 512:(nn + 1) * 512],
                                            start=(mt == 0), stop=(mt == NDT - 1))
                        ctx.inc(last, S["t_wx"])
                        ctx.wait(v, S["t_conv"])
                        ctx.inc(v.tensor_copy(
                            w1["xp_buf"][:, base:base + HALO],
                            w1["xp_buf"][:, base + TC1:base + TC1 + HALO]),
                            S["v_halo"])
                ctx.wait(v, S["t_wx"])
                last = None
                for nn in range(NBC):
                    last = v.tensor_copy(
                        w1["dbc_sb"][:, c0 + nn * 512:c0 + (nn + 1) * 512],
                        dbc_ps[:, nn * 512:(nn + 1) * 512])
                ctx.inc(last, S["v_evac"])
            ctx.inc(v.tensor_copy(lay["dt_full"][:], w1["dbc_sb"][0:64, :]),
                    S["v_misc"])
            ctx.wait(gp, S["v_misc"])
            ctx.dinc(gp.dma_start(out=scr["bc_sp"], in_=w1["dbc_sb"][64:96, :]),
                     S["d_spc"])
            if DEBUG_TAPS and layer == 0:
                ctx.dinc(gp.dma_start(out=taps["t_dbc0"], in_=w1["dbc_sb"][:]),
                         S["d_spc"])
            ctx.wait(gp, S["d_spa"])
            ctx.wait(gp, S["d_spb"])
            ctx.wait(gp, S["d_spc"])
        ctx.phase(body)


def phase_pass23(nc, ctx, S, di, g, scr, taps, layer, lay, w2, **_):
    with ExitStack() as es:
        E = es.enter_context
        Ball = E(_sb(nc, "p2_B", [128, NST * TC2], BF16))
        Call = E(_sb(nc, "p2_C", [128, NST * TC2], BF16))
        del_ps = E(_pt(nc, "p2_dps", [128, TC2], F32))
        delta = [E(_sb(nc, f"p2_del{j}", [128, TC2], F32)) for j in range(2)]
        sp_t = E(_sb(nc, "p2_spt", [128, TC2], F32))
        xct = [E(_sb(nc, f"p2_xc{j}", [128, TC2], BF16)) for j in range(2)]
        szt = [E(_sb(nc, f"p2_sz{j}", [128, TC2], BF16)) for j in range(2)]
        du_t = E(_sb(nc, "p2_du", [128, TC2], BF16))
        dA = [E(_sb(nc, f"p2_dA{j}", [128, TC2], F32)) for j in range(2)]
        wbuf = [E(_sb(nc, f"p2_w{j}", [128, TC2], BF16)) for j in range(2)]
        hbuf = [E(_sb(nc, f"p2_h{j}", [128, TC2], BF16)) for j in range(2)]
        yacc = E(_sb(nc, "p2_y", [128, TC2], F32))
        y_ps = E(_pt(nc, "p2_yps", [128, TC2], F32))
        yout = E(_sb(nc, "p2_yo", [128, NDT * TC2], BF16))
        x_ps = [E(_pt(nc, f"p3_x{j}", [128, 512], F32)) for j in range(2)]
        xrt = [E(_sb(nc, f"p3_xr{j}", [128, TC2], F32)) for j in range(2)]
        taps_sb = {}
        if DEBUG_TAPS:
            shared = E(_sb(nc, "tap_shared", [128, TC2], F32))
            for nm2 in ["del", "du", "w", "h", "dA"]:
                taps_sb[nm2] = shared

        def body(e):
            sy, sc, v, t, gp = e.sync, e.scalar, e.vector, e.tensor, e.gpsimd
            b_vdt = ctx.at(S["v_dt"])
            b_tp3 = ctx.at(S["t_p3"])
            svi = [0]
            sgi = [0]
            for kd in range(NDT):
                ctx.dinc(gp.dma_start(
                    out=w2["wblk_w"][:, kd * D:(kd + 1) * D],
                    in_=di["Wblk"][layer, kd * 128:(kd + 1) * 128, :]), S["d_w"])
            ctx.dinc(gp.dma_start(out=w2["wdt_w"][:], in_=di["Wdt"][layer]), S["d_w"])
            for ch in range(NCH2):
                c0 = ch * TC2
                if ch > 0:
                    gp.wait_ge(S["v_dt"], b_vdt + ch * NDT)
                for q in range(4):
                    ctx.dinc(gp.dma_start(
                        out=Ball[:].rearrange("p (n t) -> p n t", n=NST)
                        [:, q * 4:(q + 1) * 4, :],
                        in_=scr["bc_sp"][q * 4:(q + 1) * 4, c0:c0 + TC2]
                        .partition_broadcast(128)), S["d_bc"])
                    ctx.dinc(gp.dma_start(
                        out=Call[:].rearrange("p (n t) -> p n t", n=NST)
                        [:, q * 4:(q + 1) * 4, :],
                        in_=scr["bc_sp"][NST + q * 4:NST + (q + 1) * 4, c0:c0 + TC2]
                        .partition_broadcast(128)), S["d_bc"])
                for dt_i in range(NDT):
                    gd = ch * NDT + dt_i
                    pj = gd % 2
                    if gd == 0:
                        ctx.wait(t, S["d_w"])
                    if gd >= 1:
                        ctx.wait(t, S["sc_sp"])
                    if ch > 0 and dt_i == 0:
                        t.wait_ge(S["t_p3"], b_tp3 + ch * NKD * 2)
                    last = None
                    for nn in range(TC2 // 512):
                        last = t.matmul(
                            del_ps[:, nn * 512:(nn + 1) * 512],
                            w2["wdt_w"][:, dt_i * 128:(dt_i + 1) * 128],
                            lay["dt_full"][:, c0 + nn * 512:c0 + (nn + 1) * 512],
                            start=True, stop=True)
                    ctx.inc(last, S["t_del"])
                    if gd >= 2:
                        ctx.wait_back(sy, S["v_dt"], 1)
                    ctx.dinc(sy.dma_start(
                        out=xct[pj][:],
                        in_=scr["xc_sp"][dt_i * 128:(dt_i + 1) * 128, c0:c0 + TC2]),
                        S["d_t"])
                    ctx.dinc(sy.dma_start(
                        out=szt[pj][:],
                        in_=scr["sz_sp"][dt_i * 128:(dt_i + 1) * 128, c0:c0 + TC2]),
                        S["d_t"])
                    ctx.wait(sc, S["t_del"])
                    if gd >= 1:
                        ctx.wait(sc, S["v_del"])
                    bcol = g["bdt_c"][:, layer * NDT + dt_i:layer * NDT + dt_i + 1]
                    sc.activation(sp_t[:], del_ps[:], AF.Abs, bias=bcol)
                    sc.activation(sp_t[:], sp_t[:], AF.Exp, scale=-1.0)
                    sc.activation(sp_t[:], sp_t[:], AF.Ln, bias=1.0)
                    ctx.inc(sc.activation(delta[pj][:], del_ps[:], AF.Relu,
                                          bias=bcol), S["sc_sp"])
                    ctx.wait(v, S["sc_sp"])
                    ctx.wait(v, S["d_t"])
                    if dt_i == 0:
                        ctx.wait(v, S["d_bc"])
                        if ch > 0:
                            v.wait_ge(S["t_p3"], b_tp3 + ch * NKD * 2)
                    ctx.inc(v.tensor_tensor(delta[pj][:], delta[pj][:], sp_t[:],
                                            ALU.add), S["v_del"])
                    ctx.inc(v.tensor_tensor(du_t[:], delta[pj][:], xct[pj][:],
                                            ALU.mult), S["v_du"])
                    if DEBUG_TAPS and layer == 0 and dt_i == 0:
                        ctx.inc(v.tensor_copy(taps_sb["del"][:], delta[pj][:]),
                                S["v_misc"])
                        ctx.wait(gp, S["v_misc"])
                        ctx.dinc(gp.dma_start(
                            out=taps["t_del0"][:, c0:c0 + TC2],
                            in_=taps_sb["del"][:]), S["d_spc"])
                        ctx.wait(v, S["d_spc"])
                        ctx.inc(v.tensor_copy(taps_sb["du"][:], du_t[:]), S["v_misc"])
                        ctx.wait(gp, S["v_misc"])
                        if ch == 0:
                            ctx.dinc(gp.dma_start(out=taps["t_du0"],
                                                  in_=taps_sb["du"][:]), S["d_spc"])
                            ctx.dinc(gp.dma_start(out=taps["t_B0"], in_=Ball[:]),
                                     S["d_spc"])
                        ctx.wait(v, S["d_spc"])
                    for n in range(NST):
                        j = n % 2
                        if n == 0:
                            ctx.wait(sc, S["v_del"])
                        if svi[0] >= 2:
                            ctx.wait_back(sc, S["v_scf"], 1)
                        acol = g["a_c"][:, (layer * NDT + dt_i) * NST + n:
                                        (layer * NDT + dt_i) * NST + n + 1]
                        ctx.inc(sc.activation(dA[j][:], delta[pj][:], AF.Exp,
                                              scale=acol), S["sc_dA"])
                        if n == 0:
                            ctx.wait(gp, S["v_du"])
                            if dt_i == 0:
                                ctx.wait(gp, S["d_bc"])
                        if svi[0] >= 2:
                            ctx.wait_back(gp, S["v_scf"], 1)
                        ctx.inc(gp.tensor_tensor(wbuf[j][:], du_t[:],
                                                 Ball[:, n * TC2:(n + 1) * TC2],
                                                 ALU.mult), S["g_wb"])
                        ctx.wait(v, S["sc_dA"])
                        ctx.wait(v, S["g_wb"])
                        if svi[0] >= 2:
                            ctx.wait_back(v, S["t_yac"], 1)
                        init = (0.0 if ch == 0 else
                                w2["carry"][:, dt_i * NST + n:
                                            dt_i * NST + n + 1])
                        ctx.inc(v.tensor_tensor_scan(hbuf[j][:], dA[j][:],
                                                     wbuf[j][:], init, ALU.mult,
                                                     ALU.add), S["v_scf"])
                        if DEBUG_TAPS and layer == 0 and ch == 0 \
                                and dt_i == 0 and n == 0:
                            for tnm, tsrc in [("t_w00", wbuf[j]),
                                              ("t_h00", hbuf[j]),
                                              ("t_dA00", dA[j])]:
                                ctx.inc(v.tensor_copy(taps_sb["w"][:],
                                                      tsrc[:]), S["v_misc"])
                                ctx.wait(gp, S["v_misc"])
                                ctx.dinc(gp.dma_start(out=taps[tnm],
                                                      in_=taps_sb["w"][:]),
                                         S["d_spc"])
                                ctx.wait(v, S["d_spc"])
                        if ch < NCH2 - 1:
                            v.tensor_copy(w2["carry"][:, dt_i * NST + n:
                                                      dt_i * NST + n + 1],
                                          hbuf[j][:, TC2 - 1:TC2])
                        ctx.inc(v.tensor_tensor(hbuf[j][:], hbuf[j][:],
                                                Call[:, n * TC2:(n + 1) * TC2],
                                                ALU.mult), S["v_cm"])
                        ctx.wait(t, S["v_cm"])
                        if n == 0:
                            ctx.wait(t, S["v_dt"])
                        last_acc = None
                        for nn in range(TC2 // 512):
                            last_acc = t.matmul(
                                y_ps[:, nn * 512:(nn + 1) * 512], g["identb"][:],
                                hbuf[j][:, nn * 512:(nn + 1) * 512],
                                start=(n == 0), stop=(n == NST - 1))
                        ctx.inc(last_acc, S["t_yac"])
                        svi[0] += 1
                    dcol = g["dp_c"][:, layer * NDT + dt_i:layer * NDT + dt_i + 1]
                    ctx.wait(v, S["t_yac"])
                    v.scalar_tensor_tensor(yacc[:], xct[pj][:], dcol, y_ps[:],
                                           ALU.mult, ALU.add)
                    ctx.inc(v.tensor_tensor(yout[:, dt_i * TC2:(dt_i + 1) * TC2],
                                            yacc[:], szt[pj][:], ALU.mult), S["v_dt"])
                if DEBUG_TAPS and layer == 0 and ch in (0, 3):
                    ctx.wait(gp, S["v_dt"])
                    ctx.dinc(gp.dma_start(out=taps["t_y0" if ch == 0 else "t_y1"],
                                          in_=yout[:]), S["d_spc"])
                for mt in range(NKD):
                    gm3 = ch * NKD + mt
                    if gm3 >= 2:
                        ctx.wait_back(sy, S["d_spc"], 1, step=16)
                    ctx.dinc(sy.dma_start(
                        out=xrt[mt % 2][:],
                        in_=scr["x_resid"][mt * 128:(mt + 1) * 128, c0:c0 + TC2]),
                        S["d_x"])
                    for nn in range(TC2 // 512):
                        if mt == 0 and nn == 0:
                            ctx.wait(t, S["v_dt"])
                        if gm3 * 2 + nn >= 1:
                            ctx.wait_back(t, S["v_p3"], 1)
                        last = None
                        for kd in range(NDT):
                            last = t.matmul(
                                x_ps[nn][:],
                                w2["wblk_w"][:, kd * D + mt * 128:
                                             kd * D + (mt + 1) * 128],
                                yout[:, kd * TC2 + nn * 512:
                                     kd * TC2 + (nn + 1) * 512],
                                start=(kd == 0), stop=(kd == NDT - 1))
                        ctx.inc(last, S["t_p3"])
                        ctx.wait(v, S["t_p3"])
                        ctx.wait(v, S["d_x"])
                        ctx.inc(v.tensor_tensor(
                            xrt[mt % 2][:, nn * 512:(nn + 1) * 512],
                            xrt[mt % 2][:, nn * 512:(nn + 1) * 512], x_ps[nn][:],
                            ALU.add), S["v_p3"])
                    ctx.wait(gp, S["v_p3"])
                    ctx.dinc(gp.dma_start(
                        out=scr["x_resid"][mt * 128:(mt + 1) * 128, c0:c0 + TC2],
                        in_=xrt[mt % 2][:]), S["d_spc"])
            ctx.wait(gp, S["d_spc"])
            if DEBUG_TAPS and layer in (0, NL - 1):
                ctx.wait(sy, S["d_spc"])
                tap = taps["t_res1"] if layer == 0 else taps["t_res4"]
                for k in range(NKD):
                    for cq in range(NCH2):
                        ctx.dinc(sy.dma_start(
                            out=xrt[0][:],
                            in_=scr["x_resid"][k * 128:(k + 1) * 128,
                                               cq * TC2:(cq + 1) * TC2]), S["d_x"])
                        ctx.wait(sy, S["d_x"])
                        ctx.dinc(sy.dma_start(
                            out=tap[k * 128:(k + 1) * 128, cq * TC2:(cq + 1) * TC2],
                            in_=xrt[0][:]), S["d_spH"])
                        ctx.wait(sy, S["d_spH"])
        ctx.phase(body)


def phase_wout(nc, ctx, S, di, g, scr, taps, **_):
    with ExitStack() as es:
        E = es.enter_context
        h4b = E(_sb(nc, "wo_h4", [128, NKD * L], BF16))
        h4f = [E(_sb(nc, f"wo_hf{j}", [128, L], F32)) for j in range(2)]
        wout_w = E(_sb(nc, "wo_w", [128, NKD * D], BF16))
        o_ps = [E(_pt(nc, f"wo_ps{j}", [128, 512], F32)) for j in range(2)]
        xtt = [E(_sb(nc, f"wo_xt{j}", [128, D], F32)) for j in range(2)]
        x1t = [E(_sb(nc, f"wo_x1{j}", [128, D], F32)) for j in range(2)]
        ga_b = E(_sb(nc, "wo_ga", [128, D], F32))
        bout_b = E(_sb(nc, "wo_bo", [128, D], F32))

        def body(e):
            sy, sc, v, t, gp = e.sync, e.scalar, e.vector, e.tensor, e.gpsimd
            for nm, j in [(ga_b, 2), (bout_b, 6)]:
                ctx.dinc(sy.dma_start(
                    out=nm[:],
                    in_=scr["row_sp"][0:1, j * D:(j + 1) * D].partition_broadcast(128)),
                    S["d_bcH"])
            for k in range(NKD):
                ctx.dinc(gp.dma_start(out=wout_w[:, k * D:(k + 1) * D],
                                      in_=di["Wout"][k * 128:(k + 1) * 128, :]),
                         S["d_w"])
                if k >= 2:
                    ctx.wait_back(sy, S["v_misc"], 1)
                ctx.dinc(sy.dma_start(out=h4f[k % 2][:],
                                      in_=scr["x_resid"][k * 128:(k + 1) * 128, :]),
                         S["d_x"])
                ctx.wait(v, S["d_x"])
                ctx.inc(v.tensor_copy(h4b[:, k * L:(k + 1) * L], h4f[k % 2][:]),
                        S["v_misc"])
            for tt in range(NTT):
                if tt >= 2:
                    ctx.wait_back(sy, S["d_spa"], 2 if DEBUG_TAPS else 1, step=16)
                ctx.dinc(sy.dma_start(out=xtt[tt % 2][:],
                                      in_=di["x_T"][tt * 128:(tt + 1) * 128, :]),
                         S["d_x"])
                for nn in range(2):
                    if tt == 0 and nn == 0:
                        ctx.wait(t, S["v_misc"])
                        ctx.wait(t, S["d_w"])
                    if tt * 2 + nn >= 1:
                        ctx.wait_back(t, S["v_p3"], 1)
                    last = None
                    for k in range(NKD):
                        last = t.matmul(
                            o_ps[nn][:],
                            h4b[:, k * L + tt * 128:k * L + (tt + 1) * 128],
                            wout_w[:, k * D + nn * 512:k * D + (nn + 1) * 512],
                            start=(k == 0), stop=(k == NKD - 1))
                    ctx.inc(last, S["t_p3"])
                    ctx.wait(v, S["t_p3"])
                    if nn == 0:
                        ctx.wait(v, S["d_x"])
                        ctx.wait(v, S["d_bcH"])
                    sl = slice(nn * 512, (nn + 1) * 512)
                    v.tensor_tensor(x1t[tt % 2][:, sl], o_ps[nn][:], bout_b[:, sl],
                                    ALU.add)
                    v.tensor_tensor(x1t[tt % 2][:, sl], x1t[tt % 2][:, sl],
                                    ga_b[:, sl], ALU.mult)
                    ctx.inc(v.tensor_tensor(x1t[tt % 2][:, sl], x1t[tt % 2][:, sl],
                                            xtt[tt % 2][:, sl], ALU.add), S["v_p3"])
                ctx.wait(gp, S["v_p3"])
                ctx.dinc(gp.dma_start(out=scr["x1_sp"][tt * 128:(tt + 1) * 128, :],
                                      in_=x1t[tt % 2][:]), S["d_spa"])
                if DEBUG_TAPS:
                    ctx.dinc(gp.dma_start(out=taps["t_x1"][tt * 128:(tt + 1) * 128, :],
                                          in_=x1t[tt % 2][:]), S["d_spa"])
            ctx.wait(gp, S["d_spa"])
        ctx.phase(body)


QTHR = [(4.0, 1.0), (1.0 / 0.75, 1.0), (0.8, 1.0), (1.0 / 1.75, 1.0),
        (0.4, 2.0), (1.0 / 3.5, 2.0), (0.2, 4.0)]


def emit_quant(ctx, S, sc, v, gp, src, dst, q, F, sem_done):
    NB = F // 16
    FW = max(F, 512)
    NBW = FW // 16
    sc.activation(q["a2"][:, 0:F], src, AF.Abs)
    ctx.inc(sc.activation(q["sgn"][:, 0:F], src, AF.Sign), S["sc_q"])
    ctx.inc(v.reduce_max(q["bm"][:, 0:NB],
                         src.rearrange("p (b s) -> p b s", s=16), AX.X,
                         apply_absolute_value=True), S["v_bm"])
    ctx.wait(gp, S["v_bm"])
    ctx.inc(gp.tensor_scalar(q["scl"][:, 0:NB], q["bm"][:, 0:NB], 1.0 / 6.0,
                             1e-12, ALU.mult, ALU.max), S["g_scl"])
    sclb = q["scl"][:, 0:NBW].rearrange("p (b o) -> p b o", o=1) \
        .broadcast_to([128, NBW, 16])
    a2v = q["a2"][:, 0:FW].rearrange("p (b s) -> p b s", s=16)
    r2v = q["r2"][:, 0:FW].rearrange("p (b s) -> p b s", s=16)
    r3v = q["r3"][:, 0:FW].rearrange("p (b s) -> p b s", s=16)
    qav = q["qa"][:, 0:FW].rearrange("p (b s) -> p b s", s=16)
    sgv = q["sgn"][:, 0:FW].rearrange("p (b s) -> p b s", s=16)
    ctx.wait(v, S["sc_q"])
    ctx.wait(v, S["g_scl"])
    v.scalar_tensor_tensor(qav, a2v, QTHR[0][0], sclb, ALU.mult, ALU.is_gt)
    v.scalar_tensor_tensor(r2v, a2v, QTHR[1][0], sclb, ALU.mult, ALU.is_gt)
    v.scalar_tensor_tensor(r3v, a2v, QTHR[2][0], sclb, ALU.mult, ALU.is_gt)
    v.scalar_tensor_tensor(qav, r2v, QTHR[1][1], qav, ALU.mult, ALU.add)
    v.scalar_tensor_tensor(r2v, a2v, QTHR[3][0], sclb, ALU.mult, ALU.is_gt)
    v.scalar_tensor_tensor(qav, r3v, QTHR[2][1], qav, ALU.mult, ALU.add)
    v.scalar_tensor_tensor(r3v, a2v, QTHR[4][0], sclb, ALU.mult, ALU.is_gt)
    v.scalar_tensor_tensor(qav, r2v, QTHR[3][1], qav, ALU.mult, ALU.add)
    v.scalar_tensor_tensor(r2v, a2v, QTHR[5][0], sclb, ALU.mult, ALU.is_gt)
    v.scalar_tensor_tensor(qav, r3v, QTHR[4][1], qav, ALU.mult, ALU.add)
    v.scalar_tensor_tensor(r3v, a2v, QTHR[6][0], sclb, ALU.mult, ALU.is_gt)
    v.scalar_tensor_tensor(qav, r2v, QTHR[5][1], qav, ALU.mult, ALU.add)
    v.scalar_tensor_tensor(sgv, sgv, 0.5, sclb, ALU.mult, ALU.mult)
    v.scalar_tensor_tensor(qav, r3v, QTHR[6][1], qav, ALU.mult, ALU.add)
    ctx.inc(v.tensor_tensor(dst, q["qa"][:, 0:F], q["sgn"][:, 0:F], ALU.mult),
            sem_done)


def phase_ffn_wq(nc, ctx, S, di, g, scr, **_):
    with ExitStack() as es:
        E = es.enter_context
        wrow = [E(_sb(nc, f"fq_w{j}", [128, IFF], F32)) for j in range(2)]
        wq = E(_sb(nc, "fq_q", [128, IFF], BF16))
        tr_ps = E(_pt(nc, "fq_tr", [128, 128], BF16))
        trs = [E(_sb(nc, f"fq_t{j}", [128, 128], BF16)) for j in range(2)]
        q = {k: E(_sb(nc, f"fq_{k}", [128, IFF], F32))
             for k in ["a2", "qa", "r2", "r3", "sgn"]}
        q.update({k: E(_sb(nc, f"fq_{k}", [128, IFF // 16], F32))
                  for k in ["bm", "scl"]})

        def body(e):
            sy, sc, v, t, gp = e.sync, e.scalar, e.vector, e.tensor, e.gpsimd
            jobs = [(di["wg"], IFF // 128, D, scr["wgqT_sp"]),
                    (di["wu"], IFF // 128, D, scr["wuqT_sp"]),
                    (di["wdn"], D // 128, IFF, scr["wdqT_sp"])]
            si = 0
            gi = [0]
            for (src, nsl, rl, dst) in jobs:
                for s in range(nsl):
                    if si >= 2:
                        ctx.wait_back(sy, S["sc_q"], 1)
                        ctx.wait_back(sy, S["v_q"], 1)
                    ctx.dinc(sy.dma_start(out=wrow[si % 2][:, 0:rl],
                                          in_=src[s * 128:(s + 1) * 128, :]), S["d_x"])
                    ctx.wait(sc, S["d_x"])
                    ctx.wait(v, S["d_x"])
                    if si >= 1:
                        ctx.wait(v, S["t_tr"])
                        ctx.wait(sc, S["v_q"])
                    emit_quant(ctx, S, sc, v, gp, wrow[si % 2][:, 0:rl],
                               wq[:, 0:rl], q, rl, S["v_q"])
                    ctx.wait(t, S["v_q"])
                    for kb in range(rl // 128):
                        ctx.wait(t, S["v_tr"])
                        ctx.inc(t.transpose(tr_ps[:], wq[:, kb * 128:(kb + 1) * 128],
                                            g["identb"][:]), S["t_tr"])
                        ctx.wait(v, S["t_tr"])
                        if gi[0] >= 2:
                            ctx.wait_back(v, S["d_spa"], 1, step=16)
                        ctx.inc(v.tensor_copy(trs[gi[0] % 2][:], tr_ps[:]), S["v_tr"])
                        ctx.wait(gp, S["v_tr"])
                        ctx.dinc(gp.dma_start(
                            out=dst[kb * 128:(kb + 1) * 128, s * 128:(s + 1) * 128],
                            in_=trs[gi[0] % 2][:]), S["d_spa"])
                        gi[0] += 1
                    si += 1
            ctx.wait(gp, S["d_spa"])
        ctx.phase(body)


def phase_ffn_half(nc, ctx, S, di, g, scr, out_t, taps, half, **_):
    TH = L // 4
    NTH = TH // 128
    h0 = half * TH
    ICH = [(0, 512), (512, 512), (1024, 512), (1536, 512), (2048, 512), (2560, 256)]
    with ExitStack() as es:
        E = es.enter_context
        x1t = [E(_sb(nc, f"ff_x1{j}", [128, D], F32)) for j in range(2)]
        hf = E(_sb(nc, "ff_hf", [128, D], F32))
        mcol = E(_sb(nc, "ff_m", [128, 4], F32))
        xq = E(_sb(nc, "ff_xq", [128, D], BF16))
        tr_ps = E(_pt(nc, "ff_tr", [128, 128], BF16))
        q = {k: E(_sb(nc, f"ffq_{k}", [128, D], F32))
             for k in ["a2", "qa", "r2", "r3", "sgn"]}
        q.update({k: E(_sb(nc, f"ffq_{k}", [128, D // 16], F32))
                  for k in ["bm", "scl"]})
        h2qT = E(_sb(nc, "ff_h2T", [128, NIC * TH], BF16))
        g_ps = E(_pt(nc, "ff_g", [128, 512], F32))
        u_ps = E(_pt(nc, "ff_u", [128, 512], F32))
        gsb = E(_sb(nc, "ff_gs", [128, 512], F32))
        h2 = E(_sb(nc, "ff_h2", [128, 512], F32))
        h2q = E(_sb(nc, "ff_h2q", [128, 512], BF16))
        d_ps = [E(_pt(nc, f"ff_d{j}", [128, 512], F32)) for j in range(2)]
        osb = [E(_sb(nc, f"ff_o{j}", [128, D], F32)) for j in range(2)]
        cb1_b = E(_sb(nc, "ff_cb1", [128, D], F32))
        sb_b = E(_sb(nc, "ff_sb", [128, D], F32))
        gb_b = E(_sb(nc, "ff_gb", [128, D], F32))

        def phase1(e, xqT, wgc, wuc):
            sy, sc, v, t, gp = e.sync, e.scalar, e.vector, e.tensor, e.gpsimd
            for nm, j in [(cb1_b, 4), (sb_b, 3), (gb_b, 5)]:
                ctx.dinc(sy.dma_start(
                    out=nm[:],
                    in_=scr["row_sp"][0:1, j * D:(j + 1) * D].partition_broadcast(128)),
                    S["d_bcH"])
            for tt in range(NTH):
                r0 = h0 + tt * 128
                if tt >= 2:
                    ctx.wait_back(sy, S["v_q"], 1)
                ctx.dinc(sy.dma_start(out=x1t[tt % 2][:],
                                      in_=scr["x1_sp"][r0:r0 + 128, :]), S["d_x"])
                ctx.wait(sc, S["d_x"])
                if tt >= 1:
                    ctx.wait(sc, S["v_q"])
                ctx.inc(sc.activation(q["a2"][:, 0:D], x1t[tt % 2][:], AF.Square),
                        S["sc_q"])
                ctx.wait(v, S["d_x"])
                if tt == 0:
                    ctx.wait(v, S["d_bcH"])
                v.reduce_sum(mcol[:, 0:1], x1t[tt % 2][:], AX.X)
                ctx.wait(v, S["sc_q"])
                ctx.inc(v.reduce_sum(mcol[:, 1:2], q["a2"][:, 0:D], AX.X),
                        S["v_stat"])
                ctx.wait(gp, S["v_stat"])
                ctx.inc(gp.tensor_scalar_mul(mcol[:, 2:3], mcol[:, 0:1], 1.0 / D),
                        S["g_a"])
                ctx.wait(sc, S["g_a"])
                ctx.inc(sc.activation(mcol[:, 3:4], mcol[:, 2:3], AF.Square),
                        S["sc_m2"])
                ctx.wait(v, S["sc_m2"])
                ctx.inc(v.scalar_tensor_tensor(mcol[:, 1:2], mcol[:, 1:2], 1.0 / D,
                                               mcol[:, 3:4], ALU.mult, ALU.subtract),
                        S["v_var"])
                ctx.wait(sc, S["v_var"])
                ctx.inc(sc.activation(mcol[:, 1:2], mcol[:, 1:2], AF.Ln,
                                      bias=g["cst"][:, 0:1]), S["sc_ln"])
                ctx.wait(gp, S["sc_ln"])
                ctx.inc(gp.tensor_copy(mcol[:, 3:4], mcol[:, 1:2]), S["g_b"])
                ctx.wait(sc, S["g_b"])
                ctx.inc(sc.activation(mcol[:, 1:2], mcol[:, 3:4], AF.Exp,
                                      scale=g["cst"][:, 1:2]), S["sc_rstd"])
                ctx.wait(v, S["sc_rstd"])
                v.tensor_scalar(hf[:], x1t[tt % 2][:], mcol[:, 2:3], mcol[:, 1:2],
                                ALU.subtract, ALU.mult)
                v.tensor_tensor(hf[:], hf[:], cb1_b[:], ALU.mult)
                ctx.inc(v.tensor_tensor(hf[:], hf[:], sb_b[:], ALU.add), S["v_misc"])
                ctx.wait(sc, S["v_misc"])
                ctx.wait(v, S["t_tr"])
                emit_quant(ctx, S, sc, v, gp, hf[:], xq[:], q, D, S["v_q"])
                ctx.wait(t, S["v_q"])
                for kb in range(NKD):
                    ctx.wait(t, S["v_tr"])
                    ctx.inc(t.transpose(tr_ps[:], xq[:, kb * 128:(kb + 1) * 128],
                                        g["identb"][:]), S["t_tr"])
                    ctx.wait(v, S["t_tr"])
                    ctx.inc(v.tensor_copy(
                        xqT[:, kb * TH + tt * 128:kb * TH + (tt + 1) * 128],
                        tr_ps[:]), S["v_tr"])
            for ci, (i0, iw) in enumerate(ICH):
                cj = ci % 2
                if ci >= 2:
                    ctx.wait_back(sy, S["t_gu"], NTH)
                for k in range(NKD):
                    ctx.dinc(sy.dma_start(
                        out=wgc[cj][:, k * 512:k * 512 + iw],
                        in_=scr["wgqT_sp"][k * 128:(k + 1) * 128, i0:i0 + iw]),
                        S["d_q"])
                    ctx.dinc(sy.dma_start(
                        out=wuc[cj][:, k * 512:k * 512 + iw],
                        in_=scr["wuqT_sp"][k * 128:(k + 1) * 128, i0:i0 + iw]),
                        S["d_q"])
                ctx.wait(t, S["d_q"])
                if ci == 0:
                    ctx.wait(t, S["v_tr"])
                for tt in range(NTH):
                    gti = ci * NTH + tt
                    if gti >= 1:
                        ctx.wait(t, S["sc_cv"])
                        ctx.wait(t, S["v_misc"])
                    last = None
                    for k in range(NKD):
                        last = t.matmul(
                            g_ps[:, 0:iw],
                            xqT[:, k * TH + tt * 128:k * TH + (tt + 1) * 128],
                            wgc[cj][:, k * 512:k * 512 + iw],
                            start=(k == 0), stop=(k == NKD - 1))
                    for k in range(NKD):
                        last = t.matmul(
                            u_ps[:, 0:iw],
                            xqT[:, k * TH + tt * 128:k * TH + (tt + 1) * 128],
                            wuc[cj][:, k * 512:k * 512 + iw],
                            start=(k == 0), stop=(k == NKD - 1))
                    ctx.inc(last, S["t_gu"])
                    ctx.wait(sc, S["t_gu"])
                    ctx.inc(sc.activation(gsb[:, 0:iw], g_ps[:, 0:iw], AF.Silu),
                            S["sc_cv"])
                    ctx.wait(v, S["sc_cv"])
                    v.tensor_copy(h2[:, 0:iw], u_ps[:, 0:iw])
                    ctx.inc(v.tensor_tensor(h2[:, 0:iw], h2[:, 0:iw], gsb[:, 0:iw],
                                            ALU.mult), S["v_misc"])
                    ctx.wait(sc, S["v_misc"])
                    ctx.wait(v, S["t_tr"])
                    emit_quant(ctx, S, sc, v, gp, h2[:, 0:iw], h2q[:, 0:iw], q,
                               iw, S["v_q"])
                    ctx.wait(t, S["v_q"])
                    for kb in range(iw // 128):
                        ic = (i0 // 128) + kb
                        ctx.wait(t, S["v_tr"])
                        ctx.inc(t.transpose(tr_ps[:], h2q[:, kb * 128:(kb + 1) * 128],
                                            g["identb"][:]), S["t_tr"])
                        ctx.wait(v, S["t_tr"])
                        ctx.inc(v.tensor_copy(
                            h2qT[:, ic * TH + tt * 128:ic * TH + (tt + 1) * 128],
                            tr_ps[:]), S["v_tr"])

        def phase2(e, wdc):
            sy, sc, v, t, gp = e.sync, e.scalar, e.vector, e.tensor, e.gpsimd
            ctx.wait(sy, S["t_gu"])
            for ic in range(NIC):
                ctx.dinc(sy.dma_start(out=wdc[:, ic * D:(ic + 1) * D],
                                      in_=scr["wdqT_sp"][ic * 128:(ic + 1) * 128, :]),
                         S["d_q"])
            ctx.wait(t, S["d_q"])
            ctx.wait(t, S["v_tr"])
            for tt in range(NTH):
                r0 = h0 + tt * 128
                if tt >= 2:
                    ctx.wait_back(sy, S["d_st"], 1, step=16)
                ctx.dinc(sy.dma_start(out=x1t[tt % 2][:],
                                      in_=scr["x1_sp"][r0:r0 + 128, :]), S["d_x"])
                for nn in range(2):
                    if tt * 2 + nn >= 1:
                        ctx.wait_back(t, S["v_p3"], 1)
                    last = None
                    for ic in range(NIC):
                        last = t.matmul(
                            d_ps[nn][:],
                            h2qT[:, ic * TH + tt * 128:ic * TH + (tt + 1) * 128],
                            wdc[:, ic * D + nn * 512:ic * D + (nn + 1) * 512],
                            start=(ic == 0), stop=(ic == NIC - 1))
                    ctx.inc(last, S["t_dn"])
                    ctx.wait(v, S["t_dn"])
                    if nn == 0:
                        ctx.wait(v, S["d_x"])
                    sl = slice(nn * 512, (nn + 1) * 512)
                    v.tensor_tensor(osb[tt % 2][:, sl], d_ps[nn][:], gb_b[:, sl],
                                    ALU.mult)
                    ctx.inc(v.tensor_tensor(osb[tt % 2][:, sl], osb[tt % 2][:, sl],
                                            x1t[tt % 2][:, sl], ALU.add), S["v_p3"])
                ctx.wait(gp, S["v_p3"])
                ctx.dinc(gp.dma_start(out=out_t[r0:r0 + 128, :], in_=osb[tt % 2][:]),
                         S["d_st"])
            ctx.wait(gp, S["d_st"])

        with ExitStack() as es1:
            E1 = es1.enter_context
            xqT = E1(_sb(nc, "ff_xqT", [128, NKD * TH], BF16))
            wgc = [E1(_sb(nc, f"ff_wg{j}", [128, NKD * 512], BF16))
                   for j in range(2)]
            wuc = [E1(_sb(nc, f"ff_wu{j}", [128, NKD * 512], BF16))
                   for j in range(2)]
            ctx.phase(lambda e: phase1(e, xqT, wgc, wuc))
        with ExitStack() as es2:
            E2 = es2.enter_context
            wdc = E2(_sb(nc, "ff_wd", [128, NIC * D], BF16))
            ctx.phase(lambda e: phase2(e, wdc))


def kernel(L0_attn_x, L0_attn_norm_w, L0_attn_Win, L0_attn_conv_w, L0_attn_conv_b,
           L0_attn_Wx, L0_attn_Wdt, L0_attn_bdt, L0_attn_A_log, L0_attn_Dp,
           L0_attn_Wblk, L0_attn_Wout, L0_attn_bout, L0_ffn_w_gate, L0_ffn_w_up,
           L0_ffn_w_down, L0_c, L0_wmod):
    f32 = np.float32
    x = np.asarray(L0_attn_x, f32)
    cw = np.asarray(L0_attn_conv_w, f32)
    convdiag = np.zeros((NL, NDT, KC, 128, 128), f32)
    idx = np.arange(128)
    for l in range(NL):
        for dt_i in range(NDT):
            for kk in range(KC):
                convdiag[l, dt_i, kk, idx, idx] = cw[l, dt_i * 128:(dt_i + 1) * 128, kk]

    common = {
        "Win": np.ascontiguousarray(np.asarray(L0_attn_Win, f32)),
        "Wx": np.ascontiguousarray(np.asarray(L0_attn_Wx, f32)),
        "Wdt": np.ascontiguousarray(np.asarray(L0_attn_Wdt, f32)),
        "Wblk": np.ascontiguousarray(np.asarray(L0_attn_Wblk, f32)),
        "A_logP": np.ascontiguousarray(
            np.asarray(L0_attn_A_log, f32).reshape(NL, NDT, 128, NST)),
        "convdiag": convdiag,
        "convb_cols": np.ascontiguousarray(
            np.asarray(L0_attn_conv_b, f32).reshape(NL, NDT, 128)),
        "bdt_cols": np.ascontiguousarray(
            np.asarray(L0_attn_bdt, f32).reshape(NL, NDT, 128)),
        "Dp_cols": np.ascontiguousarray(
            np.asarray(L0_attn_Dp, f32).reshape(NL, NDT, 128)),
        "normw_cols": np.ascontiguousarray(
            np.asarray(L0_attn_norm_w, f32).reshape(NL, NKD, 128)),
        "Wout": np.ascontiguousarray(np.asarray(L0_attn_Wout, f32)),
        "bout": np.ascontiguousarray(np.asarray(L0_attn_bout, f32).reshape(1, D)),
        "wmod": np.ascontiguousarray(np.asarray(L0_wmod, f32)),
        "wg": np.ascontiguousarray(np.asarray(L0_ffn_w_gate, f32)),
        "wu": np.ascontiguousarray(np.asarray(L0_ffn_w_up, f32)),
        "wdn": np.ascontiguousarray(np.asarray(L0_ffn_w_down, f32)),
        "ident": np.eye(128, dtype=ml_dtypes.bfloat16),
        "ones_colf": np.ones((128, 1), f32),
        "ones_rowf": np.ones((1, 128), f32),
    }
    cc = np.asarray(L0_c, f32)
    in_maps = []
    for b in range(NCORES):
        m = dict(common)
        xb = np.ascontiguousarray(x[b])
        m["x_T"] = xb
        m["x_D"] = np.ascontiguousarray(xb.T)
        m["c_cols"] = np.ascontiguousarray(cc[b].reshape(NKD, 128).T)
        in_maps.append(m)

    nc = build_program()
    res = run_bass_kernel_spmd(nc, in_maps, core_ids=list(range(NCORES)))
    out = np.stack([np.asarray(res.results[b]["out"], f32) for b in range(NCORES)])
    if DEBUG_TAPS:
        kernel._last = res
    return out


# version 09
# speedup vs baseline: 1.1710x; 1.1710x over previous
import os
import numpy as np
import ml_dtypes
import concourse.bass as bass
import concourse.mybir as mybir
from concourse.bass_utils import run_bass_kernel_spmd
from contextlib import ExitStack

F32 = mybir.dt.float32
BF16 = mybir.dt.bfloat16
AF = mybir.ActivationFunctionType
ALU = mybir.AluOpType
AX = mybir.AxisListType

NCORES = 8
L = 4096
D = 1024
DI = 2048
NST = 16
DTR = 64
KC = 4
NL = 4
IFF = 2816
TC1 = 1024
NCH1 = L // TC1
TC2 = 1024
NCH2 = L // TC2
NKD = D // 128
NDT = DI // 128
NTT = L // 128
NIC = IFF // 128
NSV = 8
MAGIC = float(np.float32(1.5 * 2 ** 23))
EPS_LN = 1e-5
EPS_RMS = 1e-5
HALO = KC - 1

DEBUG_TAPS = bool(int(os.environ.get("MK_DEBUG_TAPS", "0")))

ENGS = ("sync", "scalar", "vector", "tensor", "gpsimd")


class _Pending:
    __slots__ = ("incs",)

    def __init__(self):
        self.incs = []

    def then_inc(self, sem, by=1):
        self.incs.append((sem, by))
        return self


class _Rec:
    def __init__(self, prog, name):
        self._prog = prog
        self._name = name

    def __getattr__(self, op):
        def call(*a, **k):
            p = _Pending()
            self._prog[self._name].append((op, a, k, p))
            return p
        return call


class Ctx:
    def __init__(self, nc):
        self.nc = nc
        self.cnt = {}

    def phase(self, body):
        prog = {n: [] for n in ENGS}

        class E:
            pass
        e = E()
        for n in ENGS:
            setattr(e, n, _Rec(prog, n))
        body(e)
        with self.nc.Block() as blk:
            for n in ENGS:
                items = prog[n]
                if not items:
                    continue

                def make(items):
                    def f(eng):
                        for (op, a, k, p) in items:
                            ins = getattr(eng, op)(*a, **k)
                            for (sem, by) in p.incs:
                                ins.then_inc(sem, by)
                    return f
                getattr(blk, n)(make(items))

    def inc(self, pending, sem, by=1):
        self.cnt[sem] = self.cnt.get(sem, 0) + by
        pending.then_inc(sem, by)

    def dinc(self, pending, sem):
        self.inc(pending, sem, 16)

    def at(self, sem):
        return self.cnt.get(sem, 0)

    def wait(self, rec, sem, val=None):
        v = self.at(sem) if val is None else val
        if v > 0:
            rec.wait_ge(sem, v)

    def wait_back(self, rec, sem, n_back, step=1):
        v = self.at(sem) - n_back * step
        if v > 0:
            rec.wait_ge(sem, v)



_UNIQ = [0]


def _sb(nc, name, shape, dt):
    _UNIQ[0] += 1
    return nc.sbuf_tensor(f"{name}_{_UNIQ[0]}", shape, dt)


def _pt(nc, name, shape, dt):
    _UNIQ[0] += 1
    return nc.psum_tensor(f"{name}_{_UNIQ[0]}", shape, dt)


def build_program():
    nc = bass.Bass("TRN2", target_bir_lowering=False, debug=False,
                   detect_race_conditions=False)
    ctx = Ctx(nc)
    di = {}

    def inp(name, shape, dt=F32):
        di[name] = nc.dram_tensor(name, shape, dt, kind="ExternalInput").ap()

    inp("x_D", [D, L])
    inp("x_T", [L, D])
    inp("Win", [NL, D, 2 * DI])
    inp("Wx", [NL, DI, DTR + 2 * NST])
    inp("Wdt", [NL, DTR, DI])
    inp("Wblk", [NL, DI, D])
    inp("A_logP", [NL, NDT, 128, NST])
    inp("convdiag", [NL, NDT, KC, 128, 128])
    inp("convb_cols", [NL, NDT, 128])
    inp("bdt_cols", [NL, NDT, 128])
    inp("Dp_cols", [NL, NDT, 128])
    inp("normw_cols", [NL, NKD, 128])
    inp("Wout", [D, D])
    inp("bout", [1, D])
    inp("wmod", [D, 6 * D])
    inp("c_cols", [128, NKD])
    inp("wg", [IFF, D])
    inp("wu", [IFF, D])
    inp("wdn", [D, IFF])
    inp("ident", [128, 128], BF16)
    inp("ones_colf", [128, 1])
    inp("ones_rowf", [1, 128])

    out_t = nc.dram_tensor("out", [L, D], F32, kind="ExternalOutput").ap()
    scr = {
        "x_resid": nc.dram_tensor("x_resid", [D, L], F32).ap(),
        "xc_sp": nc.dram_tensor("xc_sp", [DI, L], BF16).ap(),
        "sz_sp": nc.dram_tensor("sz_sp", [DI, L], BF16).ap(),
        "bc_sp": nc.dram_tensor("bc_sp", [2 * NST, L], F32).ap(),
        "x1_sp": nc.dram_tensor("x1_sp", [L, D], F32).ap(),
        "row_sp": nc.dram_tensor("row_sp", [1, 8 * D], F32).ap(),
        "wgqT_sp": nc.dram_tensor("wgqT_sp", [D, IFF], BF16).ap(),
        "wuqT_sp": nc.dram_tensor("wuqT_sp", [D, IFF], BF16).ap(),
        "wdqT_sp": nc.dram_tensor("wdqT_sp", [IFF, D], BF16).ap(),
    }
    taps = {}
    if DEBUG_TAPS:
        for nm, shp in [("t_h0", [D, L]), ("t_dbc0", [96, L]), ("t_res1", [D, L]),
                        ("t_res4", [D, L]), ("t_x1", [L, D]), ("t_del0", [128, L]),
                        ("t_y0", [128, NDT * TC2]), ("t_y1", [128, NDT * TC2]), ("t_w00", [128, TC2]),
                        ("t_h00", [128, TC2]), ("t_dA00", [128, TC2]),
                        ("t_du0", [128, TC2]), ("t_B0", [128, NST * TC2])]:
            taps[nm] = nc.dram_tensor(nm, shp, F32, kind="ExternalOutput").ap()

    with ExitStack() as es:
        E = es.enter_context
        g = {}
        for nm, shape, dt in [
            ("ca1_c", [128, NKD], F32), ("sa_c", [128, NKD], F32),
            ("identb", [128, 128], BF16),
            ("ones_cf", [128, 1], F32), ("ones_rf", [1, 128], F32),
            ("convb_c", [128, NL * NDT], F32), ("bdt_c", [128, NL * NDT], F32),
            ("dp_c", [128, NL * NDT], F32), ("nw_c", [128, NL * NKD], F32),
            ("a_c", [128, NL * NDT * NST], F32),
            ("cst", [128, 4], F32),
        ]:
            g[nm] = E(_sb(nc, nm, shape, dt))

        SN = ["d_x", "d_xG", "d_w", "d_spa", "d_spb", "d_spc", "d_bc", "d_bcH",
              "d_t", "d_q", "d_st", "d_c", "d_spH",
              "sc_sq", "sc_rstd", "sc_sp", "sc_dA", "sc_dAg", "sc_xz", "sc_cv",
              "sc_q", "sc_a", "sc_m2", "sc_ln",
              "v_rb", "v_xn", "v_del", "v_scf", "v_dt", "v_p3", "v_evac",
              "v_halo", "v_q", "v_tr", "v_misc", "v_bm", "v_stat", "v_var",
              "v_du", "v_cm",
              "g_scl", "g_a", "g_b", "g_scf", "g_yd", "g_wb", "g_acc",
              "t_ssq", "t_bc", "t_xz", "t_conv", "t_wx", "t_del", "t_p3",
              "t_tr", "t_mm", "t_gu", "t_dn", "t_yac"]
        S = {n: E(nc.semaphore(n)) for n in SN}

        a = dict(nc=nc, ctx=ctx, S=S, di=di, g=g, scr=scr, out_t=out_t, taps=taps)
        phase_consts(**a)
        phase_mod(**a)
        phase_ln_pro(**a)
        for layer in range(NL):
            with ExitStack() as esl:
                El = esl.enter_context
                lay = {
                    "dt_full": El(_sb(nc, "dt_full", [64, L], BF16)),
                }
                with ExitStack() as es1:
                    E1 = es1.enter_context
                    w1 = {
                        "win_w": E1(_sb(nc, "win_w", [128, NKD * 2 * DI], BF16)),
                        "wx_w": E1(_sb(nc, "wx_w", [128, NDT * 96], BF16)),
                        "cdiag_w": E1(_sb(nc, "cdiag_w",
                                                     [128, NDT * KC * 128], BF16)),
                        "xp_buf": E1(_sb(nc, "xp_buf",
                                                    [128, NDT * (HALO + TC1)], BF16)),
                        "dbc_sb": E1(_sb(nc, "dbc_sb", [96, L], F32)),
                    }
                    phase_pass1(layer=layer, lay=lay, w1=w1, **a)
                with ExitStack() as es2:
                    E2 = es2.enter_context
                    w2 = {
                        "wblk_w": E2(_sb(nc, "wblk_w", [128, NDT * D], BF16)),
                        "wdt_w": E2(_sb(nc, "wdt_w", [64, DI], BF16)),
                        "carry": E2(_sb(nc, "carry", [128, NDT * NST], F32)),
                    }
                    phase_pass23(layer=layer, lay=lay, w2=w2, **a)
        phase_wout(**a)
        phase_ffn_wq(**a)
        for half in range(4):
            phase_ffn_half(half=half, **a)
    return nc


def phase_consts(nc, ctx, S, di, g, **_):
    def body(e):
        sy, sc = e.sync, e.scalar
        for dst, src in [
            (g["identb"][:], di["ident"]), (g["ones_cf"][:], di["ones_colf"]),
            (g["ones_rf"][:], di["ones_rowf"]),
        ]:
            ctx.dinc(sy.dma_start(out=dst, in_=src), S["d_c"])
        for nm, key, w in [("convb_c", "convb_cols", NDT), ("bdt_c", "bdt_cols", NDT),
                           ("dp_c", "Dp_cols", NDT), ("nw_c", "normw_cols", NKD)]:
            ctx.dinc(sy.dma_start(
                out=g[nm][:].rearrange("p (l t) -> p l t", t=w),
                in_=di[key].rearrange("l t p -> p l t"),
                allow_slow_non_contiguous=True), S["d_c"])
        ctx.dinc(sy.dma_start(
            out=g["a_c"][:].rearrange("p (l t n) -> p l t n", t=NDT, n=NST),
            in_=di["A_logP"].rearrange("l t p n -> p l t n"),
            allow_slow_non_contiguous=True), S["d_c"])
        v = e.vector
        v.memset(g["cst"][:, 0:1], EPS_LN)
        v.memset(g["cst"][:, 1:2], -0.5)
        ctx.inc(v.memset(g["cst"][:, 2:3], 1.0 / D), S["v_misc"])
        ctx.wait(sc, S["d_c"])
        ctx.wait(sc, S["v_misc"])
        sc.activation(g["a_c"][:], g["a_c"][:], AF.Exp)
        ctx.inc(sc.mul(g["a_c"][:], g["a_c"][:], -1.0), S["sc_a"])
    ctx.phase(body)


def phase_mod(nc, ctx, S, di, g, scr, **_):
    NMOD = (6 * D) // 512
    with ExitStack() as es:
        E = es.enter_context
        cc = E(_sb(nc, "cc", [128, NKD], F32))
        ccb = E(_sb(nc, "ccb", [128, NKD], BF16))
        wmod_sb = [E(_sb(nc, f"wmod{j}", [128, 512], BF16)) for j in range(2)]
        mod_ps = E(_pt(nc, "mod_ps", [1, 512], F32))
        modv = E(_sb(nc, "modv", [1, 6 * D], F32))
        gm = E(_sb(nc, "gm", [1, 2 * D], F32))
        gsg = E(_sb(nc, "gsg", [1, 2 * D], F32))

        def body(e):
            sy, sc, v, t, gp = e.sync, e.scalar, e.vector, e.tensor, e.gpsimd
            b_dw = ctx.at(S["d_w"])
            b_mm = ctx.at(S["t_mm"])
            b_ve = ctx.at(S["v_misc"])
            ctx.dinc(sy.dma_start(out=cc[:], in_=di["c_cols"]), S["d_x"])
            ctx.wait(sc, S["d_x"])
            ctx.inc(sc.activation(ccb[:], cc[:], AF.Silu), S["sc_a"])
            for nchi in range(NMOD):
                for k in range(NKD):
                    idx = nchi * NKD + k
                    if idx >= 2:
                        gp.wait_ge(S["t_mm"], b_mm + idx - 1)
                    ctx.dinc(gp.dma_start(
                        out=wmod_sb[idx % 2][:],
                        in_=di["wmod"][k * 128:(k + 1) * 128,
                                       nchi * 512:(nchi + 1) * 512]), S["d_w"])
                    if idx == 0:
                        ctx.wait(t, S["sc_a"])
                    t.wait_ge(S["d_w"], b_dw + 16 * (idx + 1))
                    if k == 0 and nchi > 0:
                        t.wait_ge(S["v_misc"], b_ve + nchi)
                    ctx.inc(t.matmul(mod_ps[:], ccb[:, k:k + 1], wmod_sb[idx % 2][:],
                                     start=(k == 0), stop=(k == NKD - 1)), S["t_mm"])
                v.wait_ge(S["t_mm"], b_mm + (nchi + 1) * NKD)
                ctx.inc(v.tensor_copy(modv[0:1, nchi * 512:(nchi + 1) * 512],
                                      mod_ps[0:1, :]), S["v_misc"])
            v.tensor_copy(gm[0:1, 0:D], modv[0:1, 2 * D:3 * D])
            v.tensor_copy(gm[0:1, D:2 * D], modv[0:1, 5 * D:6 * D])
            v.tensor_scalar(gsg[0:1, :], gm[0:1, :], 0.0, None, ALU.is_ge)
            v.tensor_scalar(gsg[0:1, :], gsg[0:1, :], 2.0, -1.0, ALU.mult, ALU.add)
            v.tensor_tensor(gm[0:1, :], gm[0:1, :], gsg[0:1, :], ALU.mult)
            v.tensor_scalar_max(gm[0:1, :], gm[0:1, :], 0.25)
            v.tensor_tensor(gm[0:1, :], gm[0:1, :], gsg[0:1, :], ALU.mult)
            v.tensor_copy(modv[0:1, 2 * D:3 * D], gm[0:1, 0:D])
            v.tensor_copy(modv[0:1, 5 * D:6 * D], gm[0:1, D:2 * D])
            v.tensor_scalar_add(modv[0:1, 1 * D:2 * D], modv[0:1, 1 * D:2 * D], 1.0)
            ctx.inc(v.tensor_scalar_add(modv[0:1, 4 * D:5 * D],
                                        modv[0:1, 4 * D:5 * D], 1.0), S["v_misc"])
            ctx.wait(sy, S["v_misc"])
            ctx.dinc(sy.dma_start(out=scr["row_sp"][0:1, 0:6 * D], in_=modv[0:1, :]),
                     S["d_spH"])
            ctx.dinc(sy.dma_start(out=scr["row_sp"][0:1, 6 * D:7 * D], in_=di["bout"]),
                     S["d_spH"])
            ctx.wait(sy, S["d_spH"])
            ctx.dinc(sy.dma_start(
                out=g["ca1_c"][:].rearrange("p (k o) -> p k o", o=1),
                in_=scr["row_sp"][0:1, D:2 * D].rearrange("o (k p) -> p k o", p=128),
                allow_slow_non_contiguous=True), S["d_bcH"])
            ctx.dinc(sy.dma_start(
                out=g["sa_c"][:].rearrange("p (k o) -> p k o", o=1),
                in_=scr["row_sp"][0:1, 0:D].rearrange("o (k p) -> p k o", p=128),
                allow_slow_non_contiguous=True), S["d_bcH"])
            ctx.wait(sy, S["d_bcH"])
        ctx.phase(body)


def phase_ln_pro(nc, ctx, S, di, g, scr, taps, **_):
    PT = 1024
    NCH = L // PT
    NBC = PT // 512
    spk = 2 if DEBUG_TAPS else 1
    with ExitStack() as es:
        E = es.enter_context
        xt = E(_sb(nc, "lp_x", [128, NKD * PT], F32))
        sqb = [E(_sb(nc, f"lp_sq{j}", [128, PT], F32)) for j in range(2)]
        s1_ps = E(_pt(nc, "lp_s1", [1, PT], F32))
        s2_ps = E(_pt(nc, "lp_s2", [1, PT], F32))
        mrow = E(_sb(nc, "lp_m", [1, PT], F32))
        rrow = E(_sb(nc, "lp_r", [1, PT], F32))
        b_ps = E(_pt(nc, "lp_bps", [128, 512], F32))
        m_b = E(_sb(nc, "lp_mb", [128, PT], F32))
        r_b = E(_sb(nc, "lp_rb", [128, PT], F32))
        ho = [E(_sb(nc, f"lp_h{j}", [128, PT], F32)) for j in range(2)]

        def body(e):
            sy, sc, v, t, gp = e.sync, e.scalar, e.vector, e.tensor, e.gpsimd
            for ch in range(NCH):
                c0 = ch * PT
                if ch > 0:
                    ctx.wait(sy, S["v_misc"])
                for k in range(NKD):
                    ctx.dinc(sy.dma_start(
                        out=xt[:, k * PT:(k + 1) * PT],
                        in_=di["x_D"][k * 128:(k + 1) * 128, c0:c0 + PT]), S["d_x"])
                ctx.wait(sc, S["d_x"])
                ctx.wait(sc, S["d_bcH"])
                for k in range(NKD):
                    if k >= 2:
                        ctx.wait_back(sc, S["t_ssq"], 1)
                    ctx.inc(sc.activation(sqb[k % 2][:], xt[:, k * PT:(k + 1) * PT],
                                          AF.Square), S["sc_sq"])
                    ctx.wait(t, S["sc_sq"])
                    if k == 0 and ch > 0:
                        ctx.wait(t, S["v_misc"])
                    last = None
                    for nn in range(NBC):
                        t.matmul(s2_ps[0:1, nn * 512:(nn + 1) * 512], g["ones_cf"][:],
                                 sqb[k % 2][:, nn * 512:(nn + 1) * 512],
                                 start=(k == 0), stop=(k == NKD - 1))
                        last = t.matmul(
                            s1_ps[0:1, nn * 512:(nn + 1) * 512], g["ones_cf"][:],
                            xt[:, k * PT + nn * 512:k * PT + (nn + 1) * 512],
                            start=(k == 0), stop=(k == NKD - 1))
                    ctx.inc(last, S["t_ssq"])
                ctx.wait(v, S["t_ssq"])
                v.tensor_scalar_mul(mrow[:], s1_ps[0:1, :], 1.0 / D)
                v.tensor_tensor(rrow[:], mrow[:], mrow[:], ALU.mult)
                ctx.inc(v.scalar_tensor_tensor(rrow[:], s2_ps[0:1, :], 1.0 / D,
                                               rrow[:], ALU.mult, ALU.subtract),
                        S["v_misc"])
                ctx.wait(sc, S["v_misc"])
                sc.activation(rrow[:], rrow[:], AF.Ln, bias=g["cst"][0:1, 0:1])
                ctx.inc(sc.activation(rrow[:], rrow[:], AF.Exp,
                                      scale=g["cst"][0:1, 1:2]), S["sc_rstd"])
                ctx.wait(t, S["sc_rstd"])
                for nn in range(NBC):
                    if not (ch == 0 and nn == 0):
                        ctx.wait(t, S["v_rb"])
                    ctx.inc(t.matmul(b_ps[:], g["ones_rf"][:],
                                     mrow[0:1, nn * 512:(nn + 1) * 512],
                                     start=True, stop=True), S["t_bc"])
                    ctx.wait(v, S["t_bc"])
                    ctx.inc(v.tensor_copy(m_b[:, nn * 512:(nn + 1) * 512], b_ps[:]),
                            S["v_rb"])
                    ctx.wait(t, S["v_rb"])
                    ctx.inc(t.matmul(b_ps[:], g["ones_rf"][:],
                                     rrow[0:1, nn * 512:(nn + 1) * 512],
                                     start=True, stop=True), S["t_bc"])
                    ctx.wait(v, S["t_bc"])
                    ctx.inc(v.tensor_copy(r_b[:, nn * 512:(nn + 1) * 512], b_ps[:]),
                            S["v_rb"])
                for k in range(NKD):
                    hb = ho[k % 2]
                    if k >= 2:
                        ctx.wait_back(v, S["d_spa"], spk, step=16)
                    v.tensor_tensor(hb[:], xt[:, k * PT:(k + 1) * PT], m_b[:],
                                    ALU.subtract)
                    v.tensor_tensor(hb[:], hb[:], r_b[:], ALU.mult)
                    ctx.inc(v.tensor_scalar(hb[:], hb[:], g["ca1_c"][:, k:k + 1],
                                            g["sa_c"][:, k:k + 1], ALU.mult, ALU.add),
                            S["v_misc"])
                    ctx.wait(gp, S["v_misc"])
                    ctx.dinc(gp.dma_start(
                        out=scr["x_resid"][k * 128:(k + 1) * 128, c0:c0 + PT],
                        in_=hb[:]), S["d_spa"])
                    if DEBUG_TAPS:
                        ctx.dinc(gp.dma_start(
                            out=taps["t_h0"][k * 128:(k + 1) * 128, c0:c0 + PT],
                            in_=hb[:]), S["d_spa"])
            ctx.wait(gp, S["d_spa"])
        ctx.phase(body)


def phase_pass1(nc, ctx, S, di, g, scr, taps, layer, lay, w1, **_):
    NBC = TC1 // 512
    with ExitStack() as es:
        E = es.enter_context
        xt = E(_sb(nc, "p1_x", [128, NKD * TC1], BF16))
        sqb = [E(_sb(nc, f"p1_sq{j}", [128, TC1], F32)) for j in range(2)]
        s2_ps = E(_pt(nc, "p1_s2", [1, TC1], F32))
        rrow = E(_sb(nc, "p1_r", [1, TC1], F32))
        b_ps = E(_pt(nc, "p1_b", [128, 512], F32))
        r_b = E(_sb(nc, "p1_rb", [128, TC1], F32))
        xn = E(_sb(nc, "p1_xn", [128, NKD * TC1], BF16))
        xz_ps = [E(_pt(nc, f"p1_xz{j}", [128, 512], F32)) for j in range(2)]
        cv_ps = E(_pt(nc, "p1_cv", [128, 512], F32))
        dbc_ps = E(_pt(nc, "p1_dbc", [96, TC1], F32))
        sz_t = [E(_sb(nc, f"p1_sz{j}", [128, 512], BF16)) for j in range(2)]
        xc_t = [E(_sb(nc, f"p1_xc{j}", [128, TC1], BF16)) for j in range(2)]

        def body(e):
            sy, sc, v, t, gp = e.sync, e.scalar, e.vector, e.tensor, e.gpsimd
            b_tcv = ctx.at(S["t_conv"])
            b_vh = ctx.at(S["v_halo"])
            b_txz = ctx.at(S["t_xz"])
            for k in range(NKD):
                ctx.dinc(gp.dma_start(
                    out=w1["win_w"][:, k * (2 * DI):(k + 1) * (2 * DI)],
                    in_=di["Win"][layer, k * 128:(k + 1) * 128, :]), S["d_w"])
            for kd in range(NDT):
                ctx.dinc(gp.dma_start(
                    out=w1["wx_w"][:, kd * 96:(kd + 1) * 96],
                    in_=di["Wx"][layer, kd * 128:(kd + 1) * 128, :]), S["d_w"])
            ctx.dinc(gp.dma_start(
                out=w1["cdiag_w"][:].rearrange("p (t k m) -> p t k m", t=NDT, k=KC),
                in_=di["convdiag"][layer].rearrange("t k p m -> p t k m")), S["d_w"])
            hv = w1["xp_buf"][:].rearrange("p (t c) -> p t c",
                                           c=HALO + TC1)[:, :, 0:HALO]
            ctx.inc(e.vector.memset(hv, 0.0), S["v_halo"])
            for ch in range(NCH1):
                c0 = ch * TC1
                if ch > 0:
                    ctx.wait(gp, S["v_xn"])
                for k in range(NKD):
                    ctx.dinc(gp.dma_start(
                        out=xt[:, k * TC1:(k + 1) * TC1],
                        in_=scr["x_resid"][k * 128:(k + 1) * 128, c0:c0 + TC1]),
                        S["d_xG"])
                ctx.wait(sc, S["d_xG"])
                for k in range(NKD):
                    if k >= 2:
                        ctx.wait_back(sc, S["t_ssq"], 1)
                    ctx.inc(sc.activation(sqb[k % 2][:], xt[:, k * TC1:(k + 1) * TC1],
                                          AF.Square), S["sc_sq"])
                    ctx.wait(t, S["sc_sq"])
                    if k == 0:
                        ctx.wait(t, S["sc_rstd"])
                        ctx.wait(t, S["d_w"])
                    last = None
                    for nn in range(NBC):
                        last = t.matmul(
                            s2_ps[0:1, nn * 512:(nn + 1) * 512], g["ones_cf"][:],
                            sqb[k % 2][:, nn * 512:(nn + 1) * 512],
                            start=(k == 0), stop=(k == NKD - 1))
                    ctx.inc(last, S["t_ssq"])
                ctx.wait(sc, S["t_ssq"])
                sc.activation(rrow[:], s2_ps[0:1, :], AF.Ln,
                              scale=g["cst"][0:1, 2:3], bias=g["cst"][0:1, 0:1])
                ctx.inc(sc.activation(rrow[:], rrow[:], AF.Exp,
                                      scale=g["cst"][0:1, 1:2]), S["sc_rstd"])
                ctx.wait(t, S["sc_rstd"])
                for nn in range(NBC):
                    if not (ch == 0 and layer == 0 and nn == 0):
                        ctx.wait(t, S["v_rb"])
                    ctx.inc(t.matmul(b_ps[:], g["ones_rf"][:],
                                     rrow[0:1, nn * 512:(nn + 1) * 512],
                                     start=True, stop=True), S["t_bc"])
                    ctx.wait(v, S["t_bc"])
                    ctx.inc(v.tensor_copy(r_b[:, nn * 512:(nn + 1) * 512], b_ps[:]),
                            S["v_rb"])
                if ch > 0:
                    v.wait_ge(S["t_xz"], b_txz + ch * 64)
                for k in range(NKD):
                    ctx.inc(v.scalar_tensor_tensor(
                        xn[:, k * TC1:(k + 1) * TC1], xt[:, k * TC1:(k + 1) * TC1],
                        g["nw_c"][:, layer * NKD + k:layer * NKD + k + 1], r_b[:],
                        ALU.mult, ALU.mult), S["v_xn"])
                for mt in range(32):
                    for nn in range(NBC):
                        gl = mt * NBC + nn
                        pj = gl % 2
                        if gl == 0:
                            ctx.wait(t, S["v_xn"])
                        if gl >= 2:
                            ctx.wait_back(t, S["sc_xz"], 1)
                        last = None
                        for k in range(NKD):
                            last = t.matmul(
                                xz_ps[pj][:],
                                w1["win_w"][:, k * 2 * DI + mt * 128:
                                            k * 2 * DI + (mt + 1) * 128],
                                xn[:, k * TC1 + nn * 512:k * TC1 + (nn + 1) * 512],
                                start=(k == 0), stop=(k == NKD - 1))
                        ctx.inc(last, S["t_xz"])
                        ctx.wait(sc, S["t_xz"])
                        if mt < NDT:
                            if ch > 0 and nn == 0:
                                sc.wait_ge(S["t_conv"],
                                           b_tcv + (ch - 1) * 32 + (mt + 1) * 2)
                                sc.wait_ge(S["v_halo"],
                                           b_vh + 1 + (ch - 1) * NDT + mt + 1)
                            dst = w1["xp_buf"][:,
                                               mt * (HALO + TC1) + HALO + nn * 512:
                                               mt * (HALO + TC1) + HALO +
                                               (nn + 1) * 512]
                            ctx.inc(sc.copy(dst, xz_ps[pj][:]), S["sc_xz"])
                        else:
                            zi = (mt - NDT) * NBC + nn
                            if zi >= 2:
                                ctx.wait_back(sc, S["d_spb"], 1, step=16)
                            ctx.inc(sc.activation(sz_t[zi % 2][:], xz_ps[pj][:],
                                                  AF.Silu), S["sc_xz"])
                            ctx.wait(gp, S["sc_xz"])
                            dt_i = mt - NDT
                            ctx.dinc(gp.dma_start(
                                out=scr["sz_sp"][dt_i * 128:(dt_i + 1) * 128,
                                                 c0 + nn * 512:c0 + (nn + 1) * 512],
                                in_=sz_t[zi % 2][:]), S["d_spb"])
                    if mt < NDT:
                        base = mt * (HALO + TC1)
                        ctx.wait(t, S["sc_xz"])
                        for nn in range(NBC):
                            if not (layer == 0 and ch == 0 and mt == 0 and nn == 0):
                                ctx.wait(t, S["sc_cv"])
                            last = None
                            for kk in range(KC):
                                last = t.matmul(
                                    cv_ps[:],
                                    w1["cdiag_w"][:, (mt * KC + kk) * 128:
                                                  (mt * KC + kk + 1) * 128],
                                    w1["xp_buf"][:, base + nn * 512 + kk:
                                                 base + nn * 512 + kk + 512],
                                    start=(kk == 0), stop=(kk == KC - 1))
                            ctx.inc(last, S["t_conv"])
                            ctx.wait(sc, S["t_conv"])
                            if mt >= 2 and nn == 0:
                                ctx.wait_back(sc, S["t_wx"], 1)
                                ctx.wait_back(sc, S["d_spa"], 1, step=16)
                            ctx.inc(sc.activation(
                                xc_t[mt % 2][:, nn * 512:(nn + 1) * 512], cv_ps[:],
                                AF.Silu,
                                bias=g["convb_c"][:, layer * NDT + mt:
                                                  layer * NDT + mt + 1]), S["sc_cv"])
                        ctx.wait(gp, S["sc_cv"])
                        ctx.dinc(gp.dma_start(
                            out=scr["xc_sp"][mt * 128:(mt + 1) * 128, c0:c0 + TC1],
                            in_=xc_t[mt % 2][:]), S["d_spa"])
                        ctx.wait(t, S["sc_cv"])
                        if mt == 0 and not (ch == 0 and layer == 0):
                            ctx.wait(t, S["v_evac"])
                        last = None
                        for nn in range(NBC):
                            last = t.matmul(dbc_ps[:, nn * 512:(nn + 1) * 512],
                                            w1["wx_w"][:, mt * 96:(mt + 1) * 96],
                                            xc_t[mt % 2][:, nn * 512:(nn + 1) * 512],
                                            start=(mt == 0), stop=(mt == NDT - 1))
                        ctx.inc(last, S["t_wx"])
                        ctx.wait(v, S["t_conv"])
                        ctx.inc(v.tensor_copy(
                            w1["xp_buf"][:, base:base + HALO],
                            w1["xp_buf"][:, base + TC1:base + TC1 + HALO]),
                            S["v_halo"])
                ctx.wait(v, S["t_wx"])
                last = None
                for nn in range(NBC):
                    last = v.tensor_copy(
                        w1["dbc_sb"][:, c0 + nn * 512:c0 + (nn + 1) * 512],
                        dbc_ps[:, nn * 512:(nn + 1) * 512])
                ctx.inc(last, S["v_evac"])
            ctx.inc(v.tensor_copy(lay["dt_full"][:], w1["dbc_sb"][0:64, :]),
                    S["v_misc"])
            ctx.wait(gp, S["v_misc"])
            ctx.dinc(gp.dma_start(out=scr["bc_sp"], in_=w1["dbc_sb"][64:96, :]),
                     S["d_spc"])
            if DEBUG_TAPS and layer == 0:
                ctx.dinc(gp.dma_start(out=taps["t_dbc0"], in_=w1["dbc_sb"][:]),
                         S["d_spc"])
            ctx.wait(gp, S["d_spa"])
            ctx.wait(gp, S["d_spb"])
            ctx.wait(gp, S["d_spc"])
        ctx.phase(body)


def phase_pass23(nc, ctx, S, di, g, scr, taps, layer, lay, w2, **_):
    with ExitStack() as es:
        E = es.enter_context
        Ball = E(_sb(nc, "p2_B", [128, NST * TC2], BF16))
        Call = E(_sb(nc, "p2_C", [128, NST * TC2], BF16))
        del_ps = E(_pt(nc, "p2_dps", [128, TC2], F32))
        delta = [E(_sb(nc, f"p2_del{j}", [128, TC2], F32)) for j in range(2)]
        sp_t = E(_sb(nc, "p2_spt", [128, TC2], F32))
        xct = [E(_sb(nc, f"p2_xc{j}", [128, TC2], BF16)) for j in range(2)]
        szt = [E(_sb(nc, f"p2_sz{j}", [128, TC2], BF16)) for j in range(2)]
        du_t = E(_sb(nc, "p2_du", [128, TC2], BF16))
        dA = [E(_sb(nc, f"p2_dA{j}", [128, TC2], F32)) for j in range(2)]
        wbuf = [E(_sb(nc, f"p2_w{j}", [128, TC2], BF16)) for j in range(2)]
        hbuf = [E(_sb(nc, f"p2_h{j}", [128, TC2], BF16)) for j in range(2)]
        yacc = E(_sb(nc, "p2_y", [128, TC2], F32))
        y_ps = E(_pt(nc, "p2_yps", [128, TC2], F32))
        yout = E(_sb(nc, "p2_yo", [128, NDT * TC2], BF16))
        x_ps = [E(_pt(nc, f"p3_x{j}", [128, 512], F32)) for j in range(2)]
        xrt = [E(_sb(nc, f"p3_xr{j}", [128, TC2], F32)) for j in range(2)]
        taps_sb = {}
        if DEBUG_TAPS:
            shared = E(_sb(nc, "tap_shared", [128, TC2], F32))
            for nm2 in ["del", "du", "w", "h", "dA"]:
                taps_sb[nm2] = shared

        def body(e):
            sy, sc, v, t, gp = e.sync, e.scalar, e.vector, e.tensor, e.gpsimd
            b_vdt = ctx.at(S["v_dt"])
            b_tp3 = ctx.at(S["t_p3"])
            svi = [0]
            sgi = [0]
            for kd in range(NDT):
                ctx.dinc(gp.dma_start(
                    out=w2["wblk_w"][:, kd * D:(kd + 1) * D],
                    in_=di["Wblk"][layer, kd * 128:(kd + 1) * 128, :]), S["d_w"])
            ctx.dinc(gp.dma_start(out=w2["wdt_w"][:], in_=di["Wdt"][layer]), S["d_w"])
            for ch in range(NCH2):
                c0 = ch * TC2
                if ch > 0:
                    gp.wait_ge(S["v_dt"], b_vdt + ch * NDT)
                for q in range(4):
                    ctx.dinc(gp.dma_start(
                        out=Ball[:].rearrange("p (n t) -> p n t", n=NST)
                        [:, q * 4:(q + 1) * 4, :],
                        in_=scr["bc_sp"][q * 4:(q + 1) * 4, c0:c0 + TC2]
                        .partition_broadcast(128)), S["d_bc"])
                    ctx.dinc(gp.dma_start(
                        out=Call[:].rearrange("p (n t) -> p n t", n=NST)
                        [:, q * 4:(q + 1) * 4, :],
                        in_=scr["bc_sp"][NST + q * 4:NST + (q + 1) * 4, c0:c0 + TC2]
                        .partition_broadcast(128)), S["d_bc"])
                for dt_i in range(NDT):
                    gd = ch * NDT + dt_i
                    pj = gd % 2
                    if gd == 0:
                        ctx.wait(t, S["d_w"])
                    if gd >= 1:
                        ctx.wait(t, S["sc_sp"])
                    if ch > 0 and dt_i == 0:
                        t.wait_ge(S["t_p3"], b_tp3 + ch * NKD * 2)
                    last = None
                    for nn in range(TC2 // 512):
                        last = t.matmul(
                            del_ps[:, nn * 512:(nn + 1) * 512],
                            w2["wdt_w"][:, dt_i * 128:(dt_i + 1) * 128],
                            lay["dt_full"][:, c0 + nn * 512:c0 + (nn + 1) * 512],
                            start=True, stop=True)
                    ctx.inc(last, S["t_del"])
                    if gd >= 2:
                        ctx.wait_back(sy, S["v_dt"], 1)
                    ctx.dinc(sy.dma_start(
                        out=xct[pj][:],
                        in_=scr["xc_sp"][dt_i * 128:(dt_i + 1) * 128, c0:c0 + TC2]),
                        S["d_t"])
                    ctx.dinc(sy.dma_start(
                        out=szt[pj][:],
                        in_=scr["sz_sp"][dt_i * 128:(dt_i + 1) * 128, c0:c0 + TC2]),
                        S["d_t"])
                    ctx.wait(sc, S["t_del"])
                    if gd >= 1:
                        ctx.wait(sc, S["v_del"])
                    bcol = g["bdt_c"][:, layer * NDT + dt_i:layer * NDT + dt_i + 1]
                    sc.activation(sp_t[:], del_ps[:], AF.Abs, bias=bcol)
                    sc.activation(sp_t[:], sp_t[:], AF.Exp, scale=-1.0)
                    sc.activation(sp_t[:], sp_t[:], AF.Ln, bias=1.0)
                    ctx.inc(sc.activation(delta[pj][:], del_ps[:], AF.Relu,
                                          bias=bcol), S["sc_sp"])
                    ctx.wait(v, S["sc_sp"])
                    ctx.wait(v, S["d_t"])
                    if dt_i == 0:
                        ctx.wait(v, S["d_bc"])
                        if ch > 0:
                            v.wait_ge(S["t_p3"], b_tp3 + ch * NKD * 2)
                    ctx.inc(v.tensor_tensor(delta[pj][:], delta[pj][:], sp_t[:],
                                            ALU.add), S["v_del"])
                    ctx.inc(v.tensor_tensor(du_t[:], delta[pj][:], xct[pj][:],
                                            ALU.mult), S["v_du"])
                    if DEBUG_TAPS and layer == 0 and dt_i == 0:
                        ctx.inc(v.tensor_copy(taps_sb["del"][:], delta[pj][:]),
                                S["v_misc"])
                        ctx.wait(gp, S["v_misc"])
                        ctx.dinc(gp.dma_start(
                            out=taps["t_del0"][:, c0:c0 + TC2],
                            in_=taps_sb["del"][:]), S["d_spc"])
                        ctx.wait(v, S["d_spc"])
                        ctx.inc(v.tensor_copy(taps_sb["du"][:], du_t[:]), S["v_misc"])
                        ctx.wait(gp, S["v_misc"])
                        if ch == 0:
                            ctx.dinc(gp.dma_start(out=taps["t_du0"],
                                                  in_=taps_sb["du"][:]), S["d_spc"])
                            ctx.dinc(gp.dma_start(out=taps["t_B0"], in_=Ball[:]),
                                     S["d_spc"])
                        ctx.wait(v, S["d_spc"])
                    for n in range(NST):
                        j = n % 2
                        if n == 0:
                            ctx.wait(sc, S["v_del"])
                        if svi[0] >= 2:
                            ctx.wait_back(sc, S["v_scf"], 1)
                        acol = g["a_c"][:, (layer * NDT + dt_i) * NST + n:
                                        (layer * NDT + dt_i) * NST + n + 1]
                        ctx.inc(sc.activation(dA[j][:], delta[pj][:], AF.Exp,
                                              scale=acol), S["sc_dA"])
                        v.tensor_tensor(wbuf[j][:], du_t[:],
                                        Ball[:, n * TC2:(n + 1) * TC2],
                                        ALU.mult)
                        ctx.wait(v, S["sc_dA"])
                        if svi[0] >= 2:
                            ctx.wait_back(v, S["t_yac"], 1)
                        init = (0.0 if ch == 0 else
                                w2["carry"][:, dt_i * NST + n:
                                            dt_i * NST + n + 1])
                        ctx.inc(v.tensor_tensor_scan(hbuf[j][:], dA[j][:],
                                                     wbuf[j][:], init, ALU.mult,
                                                     ALU.add), S["v_scf"])
                        if DEBUG_TAPS and layer == 0 and ch == 0 \
                                and dt_i == 0 and n == 0:
                            for tnm, tsrc in [("t_w00", wbuf[j]),
                                              ("t_h00", hbuf[j]),
                                              ("t_dA00", dA[j])]:
                                ctx.inc(v.tensor_copy(taps_sb["w"][:],
                                                      tsrc[:]), S["v_misc"])
                                ctx.wait(gp, S["v_misc"])
                                ctx.dinc(gp.dma_start(out=taps[tnm],
                                                      in_=taps_sb["w"][:]),
                                         S["d_spc"])
                                ctx.wait(v, S["d_spc"])
                        if ch < NCH2 - 1:
                            v.tensor_copy(w2["carry"][:, dt_i * NST + n:
                                                      dt_i * NST + n + 1],
                                          hbuf[j][:, TC2 - 1:TC2])
                        ctx.inc(v.tensor_tensor(hbuf[j][:], hbuf[j][:],
                                                Call[:, n * TC2:(n + 1) * TC2],
                                                ALU.mult), S["v_cm"])
                        ctx.wait(t, S["v_cm"])
                        if n == 0:
                            ctx.wait(t, S["v_dt"])
                        last_acc = None
                        for nn in range(TC2 // 512):
                            last_acc = t.matmul(
                                y_ps[:, nn * 512:(nn + 1) * 512], g["identb"][:],
                                hbuf[j][:, nn * 512:(nn + 1) * 512],
                                start=(n == 0), stop=(n == NST - 1))
                        ctx.inc(last_acc, S["t_yac"])
                        svi[0] += 1
                    dcol = g["dp_c"][:, layer * NDT + dt_i:layer * NDT + dt_i + 1]
                    ctx.wait(v, S["t_yac"])
                    v.scalar_tensor_tensor(yacc[:], xct[pj][:], dcol, y_ps[:],
                                           ALU.mult, ALU.add)
                    ctx.inc(v.tensor_tensor(yout[:, dt_i * TC2:(dt_i + 1) * TC2],
                                            yacc[:], szt[pj][:], ALU.mult), S["v_dt"])
                if DEBUG_TAPS and layer == 0 and ch in (0, 3):
                    ctx.wait(gp, S["v_dt"])
                    ctx.dinc(gp.dma_start(out=taps["t_y0" if ch == 0 else "t_y1"],
                                          in_=yout[:]), S["d_spc"])
                for mt in range(NKD):
                    gm3 = ch * NKD + mt
                    if gm3 >= 2:
                        ctx.wait_back(sy, S["d_spc"], 1, step=16)
                    ctx.dinc(sy.dma_start(
                        out=xrt[mt % 2][:],
                        in_=scr["x_resid"][mt * 128:(mt + 1) * 128, c0:c0 + TC2]),
                        S["d_x"])
                    for nn in range(TC2 // 512):
                        if mt == 0 and nn == 0:
                            ctx.wait(t, S["v_dt"])
                        if gm3 * 2 + nn >= 1:
                            ctx.wait_back(t, S["v_p3"], 1)
                        last = None
                        for kd in range(NDT):
                            last = t.matmul(
                                x_ps[nn][:],
                                w2["wblk_w"][:, kd * D + mt * 128:
                                             kd * D + (mt + 1) * 128],
                                yout[:, kd * TC2 + nn * 512:
                                     kd * TC2 + (nn + 1) * 512],
                                start=(kd == 0), stop=(kd == NDT - 1))
                        ctx.inc(last, S["t_p3"])
                        ctx.wait(v, S["t_p3"])
                        ctx.wait(v, S["d_x"])
                        ctx.inc(v.tensor_tensor(
                            xrt[mt % 2][:, nn * 512:(nn + 1) * 512],
                            xrt[mt % 2][:, nn * 512:(nn + 1) * 512], x_ps[nn][:],
                            ALU.add), S["v_p3"])
                    ctx.wait(gp, S["v_p3"])
                    ctx.dinc(gp.dma_start(
                        out=scr["x_resid"][mt * 128:(mt + 1) * 128, c0:c0 + TC2],
                        in_=xrt[mt % 2][:]), S["d_spc"])
            ctx.wait(gp, S["d_spc"])
            if DEBUG_TAPS and layer in (0, NL - 1):
                ctx.wait(sy, S["d_spc"])
                tap = taps["t_res1"] if layer == 0 else taps["t_res4"]
                for k in range(NKD):
                    for cq in range(NCH2):
                        ctx.dinc(sy.dma_start(
                            out=xrt[0][:],
                            in_=scr["x_resid"][k * 128:(k + 1) * 128,
                                               cq * TC2:(cq + 1) * TC2]), S["d_x"])
                        ctx.wait(sy, S["d_x"])
                        ctx.dinc(sy.dma_start(
                            out=tap[k * 128:(k + 1) * 128, cq * TC2:(cq + 1) * TC2],
                            in_=xrt[0][:]), S["d_spH"])
                        ctx.wait(sy, S["d_spH"])
        ctx.phase(body)


def phase_wout(nc, ctx, S, di, g, scr, taps, **_):
    with ExitStack() as es:
        E = es.enter_context
        h4b = E(_sb(nc, "wo_h4", [128, NKD * L], BF16))
        h4f = [E(_sb(nc, f"wo_hf{j}", [128, L], F32)) for j in range(2)]
        wout_w = E(_sb(nc, "wo_w", [128, NKD * D], BF16))
        o_ps = [E(_pt(nc, f"wo_ps{j}", [128, 512], F32)) for j in range(2)]
        xtt = [E(_sb(nc, f"wo_xt{j}", [128, D], F32)) for j in range(2)]
        x1t = [E(_sb(nc, f"wo_x1{j}", [128, D], F32)) for j in range(2)]
        ga_b = E(_sb(nc, "wo_ga", [128, D], F32))
        bout_b = E(_sb(nc, "wo_bo", [128, D], F32))

        def body(e):
            sy, sc, v, t, gp = e.sync, e.scalar, e.vector, e.tensor, e.gpsimd
            for nm, j in [(ga_b, 2), (bout_b, 6)]:
                ctx.dinc(sy.dma_start(
                    out=nm[:],
                    in_=scr["row_sp"][0:1, j * D:(j + 1) * D].partition_broadcast(128)),
                    S["d_bcH"])
            for k in range(NKD):
                ctx.dinc(gp.dma_start(out=wout_w[:, k * D:(k + 1) * D],
                                      in_=di["Wout"][k * 128:(k + 1) * 128, :]),
                         S["d_w"])
                if k >= 2:
                    ctx.wait_back(sy, S["v_misc"], 1)
                ctx.dinc(sy.dma_start(out=h4f[k % 2][:],
                                      in_=scr["x_resid"][k * 128:(k + 1) * 128, :]),
                         S["d_x"])
                ctx.wait(v, S["d_x"])
                ctx.inc(v.tensor_copy(h4b[:, k * L:(k + 1) * L], h4f[k % 2][:]),
                        S["v_misc"])
            for tt in range(NTT):
                if tt >= 2:
                    ctx.wait_back(sy, S["d_spa"], 2 if DEBUG_TAPS else 1, step=16)
                ctx.dinc(sy.dma_start(out=xtt[tt % 2][:],
                                      in_=di["x_T"][tt * 128:(tt + 1) * 128, :]),
                         S["d_x"])
                for nn in range(2):
                    if tt == 0 and nn == 0:
                        ctx.wait(t, S["v_misc"])
                        ctx.wait(t, S["d_w"])
                    if tt * 2 + nn >= 1:
                        ctx.wait_back(t, S["v_p3"], 1)
                    last = None
                    for k in range(NKD):
                        last = t.matmul(
                            o_ps[nn][:],
                            h4b[:, k * L + tt * 128:k * L + (tt + 1) * 128],
                            wout_w[:, k * D + nn * 512:k * D + (nn + 1) * 512],
                            start=(k == 0), stop=(k == NKD - 1))
                    ctx.inc(last, S["t_p3"])
                    ctx.wait(v, S["t_p3"])
                    if nn == 0:
                        ctx.wait(v, S["d_x"])
                        ctx.wait(v, S["d_bcH"])
                    sl = slice(nn * 512, (nn + 1) * 512)
                    v.tensor_tensor(x1t[tt % 2][:, sl], o_ps[nn][:], bout_b[:, sl],
                                    ALU.add)
                    v.tensor_tensor(x1t[tt % 2][:, sl], x1t[tt % 2][:, sl],
                                    ga_b[:, sl], ALU.mult)
                    ctx.inc(v.tensor_tensor(x1t[tt % 2][:, sl], x1t[tt % 2][:, sl],
                                            xtt[tt % 2][:, sl], ALU.add), S["v_p3"])
                ctx.wait(gp, S["v_p3"])
                ctx.dinc(gp.dma_start(out=scr["x1_sp"][tt * 128:(tt + 1) * 128, :],
                                      in_=x1t[tt % 2][:]), S["d_spa"])
                if DEBUG_TAPS:
                    ctx.dinc(gp.dma_start(out=taps["t_x1"][tt * 128:(tt + 1) * 128, :],
                                          in_=x1t[tt % 2][:]), S["d_spa"])
            ctx.wait(gp, S["d_spa"])
        ctx.phase(body)


QTHR = [(4.0, 1.0), (1.0 / 0.75, 1.0), (0.8, 1.0), (1.0 / 1.75, 1.0),
        (0.4, 2.0), (1.0 / 3.5, 2.0), (0.2, 4.0)]


def emit_quant(ctx, S, sc, v, gp, src, dst, q, F, sem_done):
    NB = F // 16
    FW = max(F, 512)
    NBW = FW // 16
    sc.activation(q["a2"][:, 0:F], src, AF.Abs)
    ctx.inc(sc.activation(q["sgn"][:, 0:F], src, AF.Sign), S["sc_q"])
    ctx.inc(v.reduce_max(q["bm"][:, 0:NB],
                         src.rearrange("p (b s) -> p b s", s=16), AX.X,
                         apply_absolute_value=True), S["v_bm"])
    ctx.wait(gp, S["v_bm"])
    ctx.inc(gp.tensor_scalar(q["scl"][:, 0:NB], q["bm"][:, 0:NB], 1.0 / 6.0,
                             1e-12, ALU.mult, ALU.max), S["g_scl"])
    sclb = q["scl"][:, 0:NBW].rearrange("p (b o) -> p b o", o=1) \
        .broadcast_to([128, NBW, 16])
    a2v = q["a2"][:, 0:FW].rearrange("p (b s) -> p b s", s=16)
    r2v = q["r2"][:, 0:FW].rearrange("p (b s) -> p b s", s=16)
    r3v = q["r3"][:, 0:FW].rearrange("p (b s) -> p b s", s=16)
    qav = q["qa"][:, 0:FW].rearrange("p (b s) -> p b s", s=16)
    sgv = q["sgn"][:, 0:FW].rearrange("p (b s) -> p b s", s=16)
    ctx.wait(v, S["sc_q"])
    ctx.wait(v, S["g_scl"])
    v.scalar_tensor_tensor(qav, a2v, QTHR[0][0], sclb, ALU.mult, ALU.is_gt)
    v.scalar_tensor_tensor(r2v, a2v, QTHR[1][0], sclb, ALU.mult, ALU.is_gt)
    v.scalar_tensor_tensor(r3v, a2v, QTHR[2][0], sclb, ALU.mult, ALU.is_gt)
    v.scalar_tensor_tensor(qav, r2v, QTHR[1][1], qav, ALU.mult, ALU.add)
    v.scalar_tensor_tensor(r2v, a2v, QTHR[3][0], sclb, ALU.mult, ALU.is_gt)
    v.scalar_tensor_tensor(qav, r3v, QTHR[2][1], qav, ALU.mult, ALU.add)
    v.scalar_tensor_tensor(r3v, a2v, QTHR[4][0], sclb, ALU.mult, ALU.is_gt)
    v.scalar_tensor_tensor(qav, r2v, QTHR[3][1], qav, ALU.mult, ALU.add)
    v.scalar_tensor_tensor(r2v, a2v, QTHR[5][0], sclb, ALU.mult, ALU.is_gt)
    v.scalar_tensor_tensor(qav, r3v, QTHR[4][1], qav, ALU.mult, ALU.add)
    v.scalar_tensor_tensor(r3v, a2v, QTHR[6][0], sclb, ALU.mult, ALU.is_gt)
    v.scalar_tensor_tensor(qav, r2v, QTHR[5][1], qav, ALU.mult, ALU.add)
    v.scalar_tensor_tensor(sgv, sgv, 0.5, sclb, ALU.mult, ALU.mult)
    v.scalar_tensor_tensor(qav, r3v, QTHR[6][1], qav, ALU.mult, ALU.add)
    ctx.inc(v.tensor_tensor(dst, q["qa"][:, 0:F], q["sgn"][:, 0:F], ALU.mult),
            sem_done)


def phase_ffn_wq(nc, ctx, S, di, g, scr, **_):
    with ExitStack() as es:
        E = es.enter_context
        wrow = [E(_sb(nc, f"fq_w{j}", [128, IFF], F32)) for j in range(2)]
        wq = E(_sb(nc, "fq_q", [128, IFF], BF16))
        tr_ps = E(_pt(nc, "fq_tr", [128, 128], BF16))
        trs = [E(_sb(nc, f"fq_t{j}", [128, 128], BF16)) for j in range(2)]
        q = {k: E(_sb(nc, f"fq_{k}", [128, IFF], F32))
             for k in ["a2", "qa", "r2", "r3", "sgn"]}
        q.update({k: E(_sb(nc, f"fq_{k}", [128, IFF // 16], F32))
                  for k in ["bm", "scl"]})

        def body(e):
            sy, sc, v, t, gp = e.sync, e.scalar, e.vector, e.tensor, e.gpsimd
            jobs = [(di["wg"], IFF // 128, D, scr["wgqT_sp"]),
                    (di["wu"], IFF // 128, D, scr["wuqT_sp"]),
                    (di["wdn"], D // 128, IFF, scr["wdqT_sp"])]
            si = 0
            gi = [0]
            for (src, nsl, rl, dst) in jobs:
                for s in range(nsl):
                    if si >= 2:
                        ctx.wait_back(sy, S["sc_q"], 1)
                        ctx.wait_back(sy, S["v_q"], 1)
                    ctx.dinc(sy.dma_start(out=wrow[si % 2][:, 0:rl],
                                          in_=src[s * 128:(s + 1) * 128, :]), S["d_x"])
                    ctx.wait(sc, S["d_x"])
                    ctx.wait(v, S["d_x"])
                    if si >= 1:
                        ctx.wait(v, S["t_tr"])
                        ctx.wait(sc, S["v_q"])
                    emit_quant(ctx, S, sc, v, gp, wrow[si % 2][:, 0:rl],
                               wq[:, 0:rl], q, rl, S["v_q"])
                    ctx.wait(t, S["v_q"])
                    for kb in range(rl // 128):
                        ctx.wait(t, S["v_tr"])
                        ctx.inc(t.transpose(tr_ps[:], wq[:, kb * 128:(kb + 1) * 128],
                                            g["identb"][:]), S["t_tr"])
                        ctx.wait(v, S["t_tr"])
                        if gi[0] >= 2:
                            ctx.wait_back(v, S["d_spa"], 1, step=16)
                        ctx.inc(v.tensor_copy(trs[gi[0] % 2][:], tr_ps[:]), S["v_tr"])
                        ctx.wait(gp, S["v_tr"])
                        ctx.dinc(gp.dma_start(
                            out=dst[kb * 128:(kb + 1) * 128, s * 128:(s + 1) * 128],
                            in_=trs[gi[0] % 2][:]), S["d_spa"])
                        gi[0] += 1
                    si += 1
            ctx.wait(gp, S["d_spa"])
        ctx.phase(body)


def phase_ffn_half(nc, ctx, S, di, g, scr, out_t, taps, half, **_):
    TH = L // 4
    NTH = TH // 128
    h0 = half * TH
    ICH = [(0, 512), (512, 512), (1024, 512), (1536, 512), (2048, 512), (2560, 256)]
    with ExitStack() as es:
        E = es.enter_context
        x1t = [E(_sb(nc, f"ff_x1{j}", [128, D], F32)) for j in range(2)]
        hf = E(_sb(nc, "ff_hf", [128, D], F32))
        mcol = E(_sb(nc, "ff_m", [128, 4], F32))
        xq = E(_sb(nc, "ff_xq", [128, D], BF16))
        tr_ps = E(_pt(nc, "ff_tr", [128, 128], BF16))
        q = {k: E(_sb(nc, f"ffq_{k}", [128, D], F32))
             for k in ["a2", "qa", "r2", "r3", "sgn"]}
        q.update({k: E(_sb(nc, f"ffq_{k}", [128, D // 16], F32))
                  for k in ["bm", "scl"]})
        h2qT = E(_sb(nc, "ff_h2T", [128, NIC * TH], BF16))
        g_ps = E(_pt(nc, "ff_g", [128, 512], F32))
        u_ps = E(_pt(nc, "ff_u", [128, 512], F32))
        gsb = E(_sb(nc, "ff_gs", [128, 512], F32))
        h2 = E(_sb(nc, "ff_h2", [128, 512], F32))
        h2q = E(_sb(nc, "ff_h2q", [128, 512], BF16))
        d_ps = [E(_pt(nc, f"ff_d{j}", [128, 512], F32)) for j in range(2)]
        osb = [E(_sb(nc, f"ff_o{j}", [128, D], F32)) for j in range(2)]
        cb1_b = E(_sb(nc, "ff_cb1", [128, D], F32))
        sb_b = E(_sb(nc, "ff_sb", [128, D], F32))
        gb_b = E(_sb(nc, "ff_gb", [128, D], F32))

        def phase1(e, xqT, wgc, wuc):
            sy, sc, v, t, gp = e.sync, e.scalar, e.vector, e.tensor, e.gpsimd
            for nm, j in [(cb1_b, 4), (sb_b, 3), (gb_b, 5)]:
                ctx.dinc(sy.dma_start(
                    out=nm[:],
                    in_=scr["row_sp"][0:1, j * D:(j + 1) * D].partition_broadcast(128)),
                    S["d_bcH"])
            for tt in range(NTH):
                r0 = h0 + tt * 128
                if tt >= 2:
                    ctx.wait_back(sy, S["v_q"], 1)
                ctx.dinc(sy.dma_start(out=x1t[tt % 2][:],
                                      in_=scr["x1_sp"][r0:r0 + 128, :]), S["d_x"])
                ctx.wait(sc, S["d_x"])
                if tt >= 1:
                    ctx.wait(sc, S["v_q"])
                ctx.inc(sc.activation(q["a2"][:, 0:D], x1t[tt % 2][:], AF.Square),
                        S["sc_q"])
                ctx.wait(v, S["d_x"])
                if tt == 0:
                    ctx.wait(v, S["d_bcH"])
                v.reduce_sum(mcol[:, 0:1], x1t[tt % 2][:], AX.X)
                ctx.wait(v, S["sc_q"])
                ctx.inc(v.reduce_sum(mcol[:, 1:2], q["a2"][:, 0:D], AX.X),
                        S["v_stat"])
                ctx.wait(gp, S["v_stat"])
                ctx.inc(gp.tensor_scalar_mul(mcol[:, 2:3], mcol[:, 0:1], 1.0 / D),
                        S["g_a"])
                ctx.wait(sc, S["g_a"])
                ctx.inc(sc.activation(mcol[:, 3:4], mcol[:, 2:3], AF.Square),
                        S["sc_m2"])
                ctx.wait(v, S["sc_m2"])
                ctx.inc(v.scalar_tensor_tensor(mcol[:, 1:2], mcol[:, 1:2], 1.0 / D,
                                               mcol[:, 3:4], ALU.mult, ALU.subtract),
                        S["v_var"])
                ctx.wait(sc, S["v_var"])
                ctx.inc(sc.activation(mcol[:, 1:2], mcol[:, 1:2], AF.Ln,
                                      bias=g["cst"][:, 0:1]), S["sc_ln"])
                ctx.wait(gp, S["sc_ln"])
                ctx.inc(gp.tensor_copy(mcol[:, 3:4], mcol[:, 1:2]), S["g_b"])
                ctx.wait(sc, S["g_b"])
                ctx.inc(sc.activation(mcol[:, 1:2], mcol[:, 3:4], AF.Exp,
                                      scale=g["cst"][:, 1:2]), S["sc_rstd"])
                ctx.wait(v, S["sc_rstd"])
                v.tensor_scalar(hf[:], x1t[tt % 2][:], mcol[:, 2:3], mcol[:, 1:2],
                                ALU.subtract, ALU.mult)
                v.tensor_tensor(hf[:], hf[:], cb1_b[:], ALU.mult)
                ctx.inc(v.tensor_tensor(hf[:], hf[:], sb_b[:], ALU.add), S["v_misc"])
                ctx.wait(sc, S["v_misc"])
                ctx.wait(v, S["t_tr"])
                emit_quant(ctx, S, sc, v, gp, hf[:], xq[:], q, D, S["v_q"])
                ctx.wait(t, S["v_q"])
                for kb in range(NKD):
                    ctx.wait(t, S["v_tr"])
                    ctx.inc(t.transpose(tr_ps[:], xq[:, kb * 128:(kb + 1) * 128],
                                        g["identb"][:]), S["t_tr"])
                    ctx.wait(v, S["t_tr"])
                    ctx.inc(v.tensor_copy(
                        xqT[:, kb * TH + tt * 128:kb * TH + (tt + 1) * 128],
                        tr_ps[:]), S["v_tr"])
            for ci, (i0, iw) in enumerate(ICH):
                cj = ci % 2
                if ci >= 2:
                    ctx.wait_back(sy, S["t_gu"], NTH)
                for k in range(NKD):
                    ctx.dinc(sy.dma_start(
                        out=wgc[cj][:, k * 512:k * 512 + iw],
                        in_=scr["wgqT_sp"][k * 128:(k + 1) * 128, i0:i0 + iw]),
                        S["d_q"])
                    ctx.dinc(sy.dma_start(
                        out=wuc[cj][:, k * 512:k * 512 + iw],
                        in_=scr["wuqT_sp"][k * 128:(k + 1) * 128, i0:i0 + iw]),
                        S["d_q"])
                ctx.wait(t, S["d_q"])
                if ci == 0:
                    ctx.wait(t, S["v_tr"])
                for tt in range(NTH):
                    gti = ci * NTH + tt
                    if gti >= 1:
                        ctx.wait(t, S["sc_cv"])
                        ctx.wait(t, S["v_misc"])
                    last = None
                    for k in range(NKD):
                        last = t.matmul(
                            g_ps[:, 0:iw],
                            xqT[:, k * TH + tt * 128:k * TH + (tt + 1) * 128],
                            wgc[cj][:, k * 512:k * 512 + iw],
                            start=(k == 0), stop=(k == NKD - 1))
                    for k in range(NKD):
                        last = t.matmul(
                            u_ps[:, 0:iw],
                            xqT[:, k * TH + tt * 128:k * TH + (tt + 1) * 128],
                            wuc[cj][:, k * 512:k * 512 + iw],
                            start=(k == 0), stop=(k == NKD - 1))
                    ctx.inc(last, S["t_gu"])
                    ctx.wait(sc, S["t_gu"])
                    ctx.inc(sc.activation(gsb[:, 0:iw], g_ps[:, 0:iw], AF.Silu),
                            S["sc_cv"])
                    ctx.wait(v, S["sc_cv"])
                    v.tensor_copy(h2[:, 0:iw], u_ps[:, 0:iw])
                    ctx.inc(v.tensor_tensor(h2[:, 0:iw], h2[:, 0:iw], gsb[:, 0:iw],
                                            ALU.mult), S["v_misc"])
                    ctx.wait(sc, S["v_misc"])
                    ctx.wait(v, S["t_tr"])
                    emit_quant(ctx, S, sc, v, gp, h2[:, 0:iw], h2q[:, 0:iw], q,
                               iw, S["v_q"])
                    ctx.wait(t, S["v_q"])
                    for kb in range(iw // 128):
                        ic = (i0 // 128) + kb
                        ctx.wait(t, S["v_tr"])
                        ctx.inc(t.transpose(tr_ps[:], h2q[:, kb * 128:(kb + 1) * 128],
                                            g["identb"][:]), S["t_tr"])
                        ctx.wait(v, S["t_tr"])
                        ctx.inc(v.tensor_copy(
                            h2qT[:, ic * TH + tt * 128:ic * TH + (tt + 1) * 128],
                            tr_ps[:]), S["v_tr"])

        def phase2(e, wdc):
            sy, sc, v, t, gp = e.sync, e.scalar, e.vector, e.tensor, e.gpsimd
            ctx.wait(sy, S["t_gu"])
            for ic in range(NIC):
                ctx.dinc(sy.dma_start(out=wdc[:, ic * D:(ic + 1) * D],
                                      in_=scr["wdqT_sp"][ic * 128:(ic + 1) * 128, :]),
                         S["d_q"])
            ctx.wait(t, S["d_q"])
            ctx.wait(t, S["v_tr"])
            for tt in range(NTH):
                r0 = h0 + tt * 128
                if tt >= 2:
                    ctx.wait_back(sy, S["d_st"], 1, step=16)
                ctx.dinc(sy.dma_start(out=x1t[tt % 2][:],
                                      in_=scr["x1_sp"][r0:r0 + 128, :]), S["d_x"])
                for nn in range(2):
                    if tt * 2 + nn >= 1:
                        ctx.wait_back(t, S["v_p3"], 1)
                    last = None
                    for ic in range(NIC):
                        last = t.matmul(
                            d_ps[nn][:],
                            h2qT[:, ic * TH + tt * 128:ic * TH + (tt + 1) * 128],
                            wdc[:, ic * D + nn * 512:ic * D + (nn + 1) * 512],
                            start=(ic == 0), stop=(ic == NIC - 1))
                    ctx.inc(last, S["t_dn"])
                    ctx.wait(v, S["t_dn"])
                    if nn == 0:
                        ctx.wait(v, S["d_x"])
                    sl = slice(nn * 512, (nn + 1) * 512)
                    v.tensor_tensor(osb[tt % 2][:, sl], d_ps[nn][:], gb_b[:, sl],
                                    ALU.mult)
                    ctx.inc(v.tensor_tensor(osb[tt % 2][:, sl], osb[tt % 2][:, sl],
                                            x1t[tt % 2][:, sl], ALU.add), S["v_p3"])
                ctx.wait(gp, S["v_p3"])
                ctx.dinc(gp.dma_start(out=out_t[r0:r0 + 128, :], in_=osb[tt % 2][:]),
                         S["d_st"])
            ctx.wait(gp, S["d_st"])

        with ExitStack() as es1:
            E1 = es1.enter_context
            xqT = E1(_sb(nc, "ff_xqT", [128, NKD * TH], BF16))
            wgc = [E1(_sb(nc, f"ff_wg{j}", [128, NKD * 512], BF16))
                   for j in range(2)]
            wuc = [E1(_sb(nc, f"ff_wu{j}", [128, NKD * 512], BF16))
                   for j in range(2)]
            ctx.phase(lambda e: phase1(e, xqT, wgc, wuc))
        with ExitStack() as es2:
            E2 = es2.enter_context
            wdc = E2(_sb(nc, "ff_wd", [128, NIC * D], BF16))
            ctx.phase(lambda e: phase2(e, wdc))


def kernel(L0_attn_x, L0_attn_norm_w, L0_attn_Win, L0_attn_conv_w, L0_attn_conv_b,
           L0_attn_Wx, L0_attn_Wdt, L0_attn_bdt, L0_attn_A_log, L0_attn_Dp,
           L0_attn_Wblk, L0_attn_Wout, L0_attn_bout, L0_ffn_w_gate, L0_ffn_w_up,
           L0_ffn_w_down, L0_c, L0_wmod):
    f32 = np.float32
    x = np.asarray(L0_attn_x, f32)
    cw = np.asarray(L0_attn_conv_w, f32)
    convdiag = np.zeros((NL, NDT, KC, 128, 128), f32)
    idx = np.arange(128)
    for l in range(NL):
        for dt_i in range(NDT):
            for kk in range(KC):
                convdiag[l, dt_i, kk, idx, idx] = cw[l, dt_i * 128:(dt_i + 1) * 128, kk]

    common = {
        "Win": np.ascontiguousarray(np.asarray(L0_attn_Win, f32)),
        "Wx": np.ascontiguousarray(np.asarray(L0_attn_Wx, f32)),
        "Wdt": np.ascontiguousarray(np.asarray(L0_attn_Wdt, f32)),
        "Wblk": np.ascontiguousarray(np.asarray(L0_attn_Wblk, f32)),
        "A_logP": np.ascontiguousarray(
            np.asarray(L0_attn_A_log, f32).reshape(NL, NDT, 128, NST)),
        "convdiag": convdiag,
        "convb_cols": np.ascontiguousarray(
            np.asarray(L0_attn_conv_b, f32).reshape(NL, NDT, 128)),
        "bdt_cols": np.ascontiguousarray(
            np.asarray(L0_attn_bdt, f32).reshape(NL, NDT, 128)),
        "Dp_cols": np.ascontiguousarray(
            np.asarray(L0_attn_Dp, f32).reshape(NL, NDT, 128)),
        "normw_cols": np.ascontiguousarray(
            np.asarray(L0_attn_norm_w, f32).reshape(NL, NKD, 128)),
        "Wout": np.ascontiguousarray(np.asarray(L0_attn_Wout, f32)),
        "bout": np.ascontiguousarray(np.asarray(L0_attn_bout, f32).reshape(1, D)),
        "wmod": np.ascontiguousarray(np.asarray(L0_wmod, f32)),
        "wg": np.ascontiguousarray(np.asarray(L0_ffn_w_gate, f32)),
        "wu": np.ascontiguousarray(np.asarray(L0_ffn_w_up, f32)),
        "wdn": np.ascontiguousarray(np.asarray(L0_ffn_w_down, f32)),
        "ident": np.eye(128, dtype=ml_dtypes.bfloat16),
        "ones_colf": np.ones((128, 1), f32),
        "ones_rowf": np.ones((1, 128), f32),
    }
    cc = np.asarray(L0_c, f32)
    in_maps = []
    for b in range(NCORES):
        m = dict(common)
        xb = np.ascontiguousarray(x[b])
        m["x_T"] = xb
        m["x_D"] = np.ascontiguousarray(xb.T)
        m["c_cols"] = np.ascontiguousarray(cc[b].reshape(NKD, 128).T)
        in_maps.append(m)

    nc = build_program()
    res = run_bass_kernel_spmd(nc, in_maps, core_ids=list(range(NCORES)))
    out = np.stack([np.asarray(res.results[b]["out"], f32) for b in range(NCORES)])
    if DEBUG_TAPS:
        kernel._last = res
    return out


# version 10
# speedup vs baseline: 1.1938x; 1.0194x over previous
import os
import numpy as np
import ml_dtypes
import concourse.bass as bass
import concourse.mybir as mybir
from concourse.bass_utils import run_bass_kernel_spmd
from contextlib import ExitStack

F32 = mybir.dt.float32
BF16 = mybir.dt.bfloat16
AF = mybir.ActivationFunctionType
ALU = mybir.AluOpType
AX = mybir.AxisListType

NCORES = 8
L = 4096
D = 1024
DI = 2048
NST = 16
DTR = 64
KC = 4
NL = 4
IFF = 2816
TC1 = 1024
NCH1 = L // TC1
TC2 = 1024
NCH2 = L // TC2
NKD = D // 128
NDT = DI // 128
NTT = L // 128
NIC = IFF // 128
NSV = 8
MAGIC = float(np.float32(1.5 * 2 ** 23))
EPS_LN = 1e-5
EPS_RMS = 1e-5
HALO = KC - 1

DEBUG_TAPS = bool(int(os.environ.get("MK_DEBUG_TAPS", "0")))

ENGS = ("sync", "scalar", "vector", "tensor", "gpsimd")


class _Pending:
    __slots__ = ("incs",)

    def __init__(self):
        self.incs = []

    def then_inc(self, sem, by=1):
        self.incs.append((sem, by))
        return self


class _Rec:
    def __init__(self, prog, name):
        self._prog = prog
        self._name = name

    def __getattr__(self, op):
        def call(*a, **k):
            p = _Pending()
            self._prog[self._name].append((op, a, k, p))
            return p
        return call


class Ctx:
    def __init__(self, nc):
        self.nc = nc
        self.cnt = {}

    def phase(self, body):
        prog = {n: [] for n in ENGS}

        class E:
            pass
        e = E()
        for n in ENGS:
            setattr(e, n, _Rec(prog, n))
        body(e)
        with self.nc.Block() as blk:
            for n in ENGS:
                items = prog[n]
                if not items:
                    continue

                def make(items):
                    def f(eng):
                        for (op, a, k, p) in items:
                            ins = getattr(eng, op)(*a, **k)
                            for (sem, by) in p.incs:
                                ins.then_inc(sem, by)
                    return f
                getattr(blk, n)(make(items))

    def inc(self, pending, sem, by=1):
        self.cnt[sem] = self.cnt.get(sem, 0) + by
        pending.then_inc(sem, by)

    def dinc(self, pending, sem):
        self.inc(pending, sem, 16)

    def at(self, sem):
        return self.cnt.get(sem, 0)

    def wait(self, rec, sem, val=None):
        v = self.at(sem) if val is None else val
        if v > 0:
            rec.wait_ge(sem, v)

    def wait_back(self, rec, sem, n_back, step=1):
        v = self.at(sem) - n_back * step
        if v > 0:
            rec.wait_ge(sem, v)



_UNIQ = [0]


def _sb(nc, name, shape, dt):
    _UNIQ[0] += 1
    return nc.sbuf_tensor(f"{name}_{_UNIQ[0]}", shape, dt)


def _pt(nc, name, shape, dt):
    _UNIQ[0] += 1
    return nc.psum_tensor(f"{name}_{_UNIQ[0]}", shape, dt)


def build_program():
    nc = bass.Bass("TRN2", target_bir_lowering=False, debug=False,
                   detect_race_conditions=False)
    ctx = Ctx(nc)
    di = {}

    def inp(name, shape, dt=F32):
        di[name] = nc.dram_tensor(name, shape, dt, kind="ExternalInput").ap()

    inp("x_D", [D, L])
    inp("x_T", [L, D])
    inp("Win", [NL, D, 2 * DI])
    inp("Wx", [NL, DI, DTR + 2 * NST])
    inp("Wdt", [NL, DTR, DI])
    inp("Wblk", [NL, DI, D])
    inp("A_logP", [NL, NDT, 128, NST])
    inp("convdiag", [NL, NDT, KC, 128, 128])
    inp("convb_cols", [NL, NDT, 128])
    inp("bdt_cols", [NL, NDT, 128])
    inp("Dp_cols", [NL, NDT, 128])
    inp("normw_cols", [NL, NKD, 128])
    inp("Wout", [D, D])
    inp("bout", [1, D])
    inp("wmod", [D, 6 * D])
    inp("c_cols", [128, NKD])
    inp("wg", [IFF, D])
    inp("wu", [IFF, D])
    inp("wdn", [D, IFF])
    inp("ident", [128, 128], BF16)
    inp("ones_colf", [128, 1])
    inp("ones_rowf", [1, 128])

    out_t = nc.dram_tensor("out", [L, D], F32, kind="ExternalOutput").ap()
    scr = {
        "x_resid": nc.dram_tensor("x_resid", [D, L], F32).ap(),
        "xc_sp": nc.dram_tensor("xc_sp", [DI, L], BF16).ap(),
        "sz_sp": nc.dram_tensor("sz_sp", [DI, L], BF16).ap(),
        "bc_sp": nc.dram_tensor("bc_sp", [2 * NST, L], F32).ap(),
        "x1_sp": nc.dram_tensor("x1_sp", [L, D], F32).ap(),
        "row_sp": nc.dram_tensor("row_sp", [1, 8 * D], F32).ap(),
        "wgqT_sp": nc.dram_tensor("wgqT_sp", [D, IFF], BF16).ap(),
        "wuqT_sp": nc.dram_tensor("wuqT_sp", [D, IFF], BF16).ap(),
        "wdqT_sp": nc.dram_tensor("wdqT_sp", [IFF, D], BF16).ap(),
    }
    taps = {}
    if DEBUG_TAPS:
        for nm, shp in [("t_h0", [D, L]), ("t_dbc0", [96, L]), ("t_res1", [D, L]),
                        ("t_res4", [D, L]), ("t_x1", [L, D]), ("t_del0", [128, L]),
                        ("t_y0", [128, NDT * TC2]), ("t_y1", [128, NDT * TC2]), ("t_w00", [128, TC2]),
                        ("t_h00", [128, TC2]), ("t_dA00", [128, TC2]),
                        ("t_du0", [128, TC2]), ("t_B0", [128, NST * TC2])]:
            taps[nm] = nc.dram_tensor(nm, shp, F32, kind="ExternalOutput").ap()

    with ExitStack() as es:
        E = es.enter_context
        g = {}
        for nm, shape, dt in [
            ("ca1_c", [128, NKD], F32), ("sa_c", [128, NKD], F32),
            ("identb", [128, 128], BF16),
            ("ones_cf", [128, 1], F32), ("ones_rf", [1, 128], F32),
            ("convb_c", [128, NL * NDT], F32), ("bdt_c", [128, NL * NDT], F32),
            ("dp_c", [128, NL * NDT], F32), ("nw_c", [128, NL * NKD], F32),
            ("a_c", [128, NL * NDT * NST], F32),
            ("cst", [128, 4], F32),
        ]:
            g[nm] = E(_sb(nc, nm, shape, dt))

        SN = ["d_x", "d_xG", "d_w", "d_spa", "d_spb", "d_spc", "d_bc", "d_bcH",
              "d_t", "d_q", "d_st", "d_c", "d_spH",
              "sc_sq", "sc_rstd", "sc_sp", "sc_dA", "sc_dAg", "sc_xz", "sc_cv",
              "sc_q", "sc_a", "sc_m2", "sc_ln", "sc_ln2", "sc_rs",
              "v_rb", "v_xn", "v_del", "v_scf", "v_dt", "v_p3", "v_evac",
              "v_halo", "v_q", "v_tr", "v_misc", "v_bm", "v_stat", "v_var",
              "v_du", "v_cm",
              "g_scl", "g_a", "g_b", "g_b2", "g_scf", "g_yd", "g_wb", "g_acc",
              "t_ssq", "t_bc", "t_xz", "t_conv", "t_wx", "t_del", "t_p3",
              "t_tr", "t_mm", "t_gu", "t_dn", "t_yac"]
        S = {n: E(nc.semaphore(n)) for n in SN}

        a = dict(nc=nc, ctx=ctx, S=S, di=di, g=g, scr=scr, out_t=out_t, taps=taps)
        phase_consts(**a)
        phase_mod(**a)
        phase_ln_pro(**a)
        for layer in range(NL):
            with ExitStack() as esl:
                El = esl.enter_context
                lay = {
                    "dt_full": El(_sb(nc, "dt_full", [64, L], BF16)),
                }
                with ExitStack() as es1:
                    E1 = es1.enter_context
                    w1 = {
                        "win_w": E1(_sb(nc, "win_w", [128, NKD * 2 * DI], BF16)),
                        "wx_w": E1(_sb(nc, "wx_w", [128, NDT * 96], BF16)),
                        "cdiag_w": E1(_sb(nc, "cdiag_w",
                                                     [128, NDT * KC * 128], BF16)),
                        "xp_buf": E1(_sb(nc, "xp_buf",
                                                    [128, NDT * (HALO + TC1)], BF16)),
                        "dbc_sb": E1(_sb(nc, "dbc_sb", [96, L], F32)),
                    }
                    phase_pass1(layer=layer, lay=lay, w1=w1, **a)
                with ExitStack() as es2:
                    E2 = es2.enter_context
                    w2 = {
                        "wblk_w": E2(_sb(nc, "wblk_w", [128, NDT * D], BF16)),
                        "wdt_w": E2(_sb(nc, "wdt_w", [64, DI], BF16)),
                        "carry": E2(_sb(nc, "carry", [128, NDT * NST], F32)),
                    }
                    phase_pass23(layer=layer, lay=lay, w2=w2, **a)
        phase_wout(**a)
        phase_ffn_wq(**a)
        for half in range(4):
            phase_ffn_half(half=half, **a)
    return nc


def phase_consts(nc, ctx, S, di, g, **_):
    def body(e):
        sy, sc = e.sync, e.scalar
        for dst, src in [
            (g["identb"][:], di["ident"]), (g["ones_cf"][:], di["ones_colf"]),
            (g["ones_rf"][:], di["ones_rowf"]),
        ]:
            ctx.dinc(sy.dma_start(out=dst, in_=src), S["d_c"])
        for nm, key, w in [("convb_c", "convb_cols", NDT), ("bdt_c", "bdt_cols", NDT),
                           ("dp_c", "Dp_cols", NDT), ("nw_c", "normw_cols", NKD)]:
            ctx.dinc(sy.dma_start(
                out=g[nm][:].rearrange("p (l t) -> p l t", t=w),
                in_=di[key].rearrange("l t p -> p l t"),
                allow_slow_non_contiguous=True), S["d_c"])
        ctx.dinc(sy.dma_start(
            out=g["a_c"][:].rearrange("p (l t n) -> p l t n", t=NDT, n=NST),
            in_=di["A_logP"].rearrange("l t p n -> p l t n"),
            allow_slow_non_contiguous=True), S["d_c"])
        v = e.vector
        v.memset(g["cst"][:, 0:1], EPS_LN)
        v.memset(g["cst"][:, 1:2], -0.5)
        ctx.inc(v.memset(g["cst"][:, 2:3], 1.0 / D), S["v_misc"])
        ctx.wait(sc, S["d_c"])
        ctx.wait(sc, S["v_misc"])
        sc.activation(g["a_c"][:], g["a_c"][:], AF.Exp)
        ctx.inc(sc.mul(g["a_c"][:], g["a_c"][:], -1.0), S["sc_a"])
    ctx.phase(body)


def phase_mod(nc, ctx, S, di, g, scr, **_):
    NMOD = (6 * D) // 512
    with ExitStack() as es:
        E = es.enter_context
        cc = E(_sb(nc, "cc", [128, NKD], F32))
        ccb = E(_sb(nc, "ccb", [128, NKD], BF16))
        wmod_sb = [E(_sb(nc, f"wmod{j}", [128, 512], BF16)) for j in range(2)]
        mod_ps = E(_pt(nc, "mod_ps", [1, 512], F32))
        modv = E(_sb(nc, "modv", [1, 6 * D], F32))
        gm = E(_sb(nc, "gm", [1, 2 * D], F32))
        gsg = E(_sb(nc, "gsg", [1, 2 * D], F32))

        def body(e):
            sy, sc, v, t, gp = e.sync, e.scalar, e.vector, e.tensor, e.gpsimd
            b_dw = ctx.at(S["d_w"])
            b_mm = ctx.at(S["t_mm"])
            b_ve = ctx.at(S["v_misc"])
            ctx.dinc(sy.dma_start(out=cc[:], in_=di["c_cols"]), S["d_x"])
            ctx.wait(sc, S["d_x"])
            ctx.inc(sc.activation(ccb[:], cc[:], AF.Silu), S["sc_a"])
            for nchi in range(NMOD):
                for k in range(NKD):
                    idx = nchi * NKD + k
                    if idx >= 2:
                        gp.wait_ge(S["t_mm"], b_mm + idx - 1)
                    ctx.dinc(gp.dma_start(
                        out=wmod_sb[idx % 2][:],
                        in_=di["wmod"][k * 128:(k + 1) * 128,
                                       nchi * 512:(nchi + 1) * 512]), S["d_w"])
                    if idx == 0:
                        ctx.wait(t, S["sc_a"])
                    t.wait_ge(S["d_w"], b_dw + 16 * (idx + 1))
                    if k == 0 and nchi > 0:
                        t.wait_ge(S["v_misc"], b_ve + nchi)
                    ctx.inc(t.matmul(mod_ps[:], ccb[:, k:k + 1], wmod_sb[idx % 2][:],
                                     start=(k == 0), stop=(k == NKD - 1)), S["t_mm"])
                v.wait_ge(S["t_mm"], b_mm + (nchi + 1) * NKD)
                ctx.inc(v.tensor_copy(modv[0:1, nchi * 512:(nchi + 1) * 512],
                                      mod_ps[0:1, :]), S["v_misc"])
            v.tensor_copy(gm[0:1, 0:D], modv[0:1, 2 * D:3 * D])
            v.tensor_copy(gm[0:1, D:2 * D], modv[0:1, 5 * D:6 * D])
            v.tensor_scalar(gsg[0:1, :], gm[0:1, :], 0.0, None, ALU.is_ge)
            v.tensor_scalar(gsg[0:1, :], gsg[0:1, :], 2.0, -1.0, ALU.mult, ALU.add)
            v.tensor_tensor(gm[0:1, :], gm[0:1, :], gsg[0:1, :], ALU.mult)
            v.tensor_scalar_max(gm[0:1, :], gm[0:1, :], 0.25)
            v.tensor_tensor(gm[0:1, :], gm[0:1, :], gsg[0:1, :], ALU.mult)
            v.tensor_copy(modv[0:1, 2 * D:3 * D], gm[0:1, 0:D])
            v.tensor_copy(modv[0:1, 5 * D:6 * D], gm[0:1, D:2 * D])
            v.tensor_scalar_add(modv[0:1, 1 * D:2 * D], modv[0:1, 1 * D:2 * D], 1.0)
            ctx.inc(v.tensor_scalar_add(modv[0:1, 4 * D:5 * D],
                                        modv[0:1, 4 * D:5 * D], 1.0), S["v_misc"])
            ctx.wait(sy, S["v_misc"])
            ctx.dinc(sy.dma_start(out=scr["row_sp"][0:1, 0:6 * D], in_=modv[0:1, :]),
                     S["d_spH"])
            ctx.dinc(sy.dma_start(out=scr["row_sp"][0:1, 6 * D:7 * D], in_=di["bout"]),
                     S["d_spH"])
            ctx.wait(sy, S["d_spH"])
            ctx.dinc(sy.dma_start(
                out=g["ca1_c"][:].rearrange("p (k o) -> p k o", o=1),
                in_=scr["row_sp"][0:1, D:2 * D].rearrange("o (k p) -> p k o", p=128),
                allow_slow_non_contiguous=True), S["d_bcH"])
            ctx.dinc(sy.dma_start(
                out=g["sa_c"][:].rearrange("p (k o) -> p k o", o=1),
                in_=scr["row_sp"][0:1, 0:D].rearrange("o (k p) -> p k o", p=128),
                allow_slow_non_contiguous=True), S["d_bcH"])
            ctx.wait(sy, S["d_bcH"])
        ctx.phase(body)


def phase_ln_pro(nc, ctx, S, di, g, scr, taps, **_):
    PT = 1024
    NCH = L // PT
    NBC = PT // 512
    spk = 2 if DEBUG_TAPS else 1
    with ExitStack() as es:
        E = es.enter_context
        xt = E(_sb(nc, "lp_x", [128, NKD * PT], F32))
        sqb = [E(_sb(nc, f"lp_sq{j}", [128, PT], F32)) for j in range(2)]
        s1_ps = E(_pt(nc, "lp_s1", [1, PT], F32))
        s2_ps = E(_pt(nc, "lp_s2", [1, PT], F32))
        mrow = E(_sb(nc, "lp_m", [1, PT], F32))
        rrow = E(_sb(nc, "lp_r", [1, PT], F32))
        b_ps = E(_pt(nc, "lp_bps", [128, 512], F32))
        m_b = E(_sb(nc, "lp_mb", [128, PT], F32))
        r_b = E(_sb(nc, "lp_rb", [128, PT], F32))
        ho = [E(_sb(nc, f"lp_h{j}", [128, PT], F32)) for j in range(2)]

        def body(e):
            sy, sc, v, t, gp = e.sync, e.scalar, e.vector, e.tensor, e.gpsimd
            for ch in range(NCH):
                c0 = ch * PT
                if ch > 0:
                    ctx.wait(sy, S["v_misc"])
                for k in range(NKD):
                    ctx.dinc(sy.dma_start(
                        out=xt[:, k * PT:(k + 1) * PT],
                        in_=di["x_D"][k * 128:(k + 1) * 128, c0:c0 + PT]), S["d_x"])
                ctx.wait(sc, S["d_x"])
                ctx.wait(sc, S["d_bcH"])
                for k in range(NKD):
                    if k >= 2:
                        ctx.wait_back(sc, S["t_ssq"], 1)
                    ctx.inc(sc.activation(sqb[k % 2][:], xt[:, k * PT:(k + 1) * PT],
                                          AF.Square), S["sc_sq"])
                    ctx.wait(t, S["sc_sq"])
                    if k == 0 and ch > 0:
                        ctx.wait(t, S["v_misc"])
                    last = None
                    for nn in range(NBC):
                        t.matmul(s2_ps[0:1, nn * 512:(nn + 1) * 512], g["ones_cf"][:],
                                 sqb[k % 2][:, nn * 512:(nn + 1) * 512],
                                 start=(k == 0), stop=(k == NKD - 1))
                        last = t.matmul(
                            s1_ps[0:1, nn * 512:(nn + 1) * 512], g["ones_cf"][:],
                            xt[:, k * PT + nn * 512:k * PT + (nn + 1) * 512],
                            start=(k == 0), stop=(k == NKD - 1))
                    ctx.inc(last, S["t_ssq"])
                ctx.wait(v, S["t_ssq"])
                v.tensor_scalar_mul(mrow[:], s1_ps[0:1, :], 1.0 / D)
                v.tensor_tensor(rrow[:], mrow[:], mrow[:], ALU.mult)
                ctx.inc(v.scalar_tensor_tensor(rrow[:], s2_ps[0:1, :], 1.0 / D,
                                               rrow[:], ALU.mult, ALU.subtract),
                        S["v_misc"])
                ctx.wait(sc, S["v_misc"])
                sc.activation(rrow[:], rrow[:], AF.Ln, bias=g["cst"][0:1, 0:1])
                ctx.inc(sc.activation(rrow[:], rrow[:], AF.Exp,
                                      scale=g["cst"][0:1, 1:2]), S["sc_rstd"])
                ctx.wait(t, S["sc_rstd"])
                for nn in range(NBC):
                    if not (ch == 0 and nn == 0):
                        ctx.wait(t, S["v_rb"])
                    ctx.inc(t.matmul(b_ps[:], g["ones_rf"][:],
                                     mrow[0:1, nn * 512:(nn + 1) * 512],
                                     start=True, stop=True), S["t_bc"])
                    ctx.wait(v, S["t_bc"])
                    ctx.inc(v.tensor_copy(m_b[:, nn * 512:(nn + 1) * 512], b_ps[:]),
                            S["v_rb"])
                    ctx.wait(t, S["v_rb"])
                    ctx.inc(t.matmul(b_ps[:], g["ones_rf"][:],
                                     rrow[0:1, nn * 512:(nn + 1) * 512],
                                     start=True, stop=True), S["t_bc"])
                    ctx.wait(v, S["t_bc"])
                    ctx.inc(v.tensor_copy(r_b[:, nn * 512:(nn + 1) * 512], b_ps[:]),
                            S["v_rb"])
                for k in range(NKD):
                    hb = ho[k % 2]
                    if k >= 2:
                        ctx.wait_back(v, S["d_spa"], spk, step=16)
                    v.tensor_tensor(hb[:], xt[:, k * PT:(k + 1) * PT], m_b[:],
                                    ALU.subtract)
                    v.tensor_tensor(hb[:], hb[:], r_b[:], ALU.mult)
                    ctx.inc(v.tensor_scalar(hb[:], hb[:], g["ca1_c"][:, k:k + 1],
                                            g["sa_c"][:, k:k + 1], ALU.mult, ALU.add),
                            S["v_misc"])
                    ctx.wait(gp, S["v_misc"])
                    ctx.dinc(gp.dma_start(
                        out=scr["x_resid"][k * 128:(k + 1) * 128, c0:c0 + PT],
                        in_=hb[:]), S["d_spa"])
                    if DEBUG_TAPS:
                        ctx.dinc(gp.dma_start(
                            out=taps["t_h0"][k * 128:(k + 1) * 128, c0:c0 + PT],
                            in_=hb[:]), S["d_spa"])
            ctx.wait(gp, S["d_spa"])
        ctx.phase(body)


def phase_pass1(nc, ctx, S, di, g, scr, taps, layer, lay, w1, **_):
    NBC = TC1 // 512
    with ExitStack() as es:
        E = es.enter_context
        xt = E(_sb(nc, "p1_x", [128, NKD * TC1], BF16))
        sqb = [E(_sb(nc, f"p1_sq{j}", [128, TC1], F32)) for j in range(2)]
        s2_ps = E(_pt(nc, "p1_s2", [1, TC1], F32))
        rrow = E(_sb(nc, "p1_r", [1, TC1], F32))
        b_ps = E(_pt(nc, "p1_b", [128, 512], F32))
        r_b = E(_sb(nc, "p1_rb", [128, TC1], F32))
        xn = E(_sb(nc, "p1_xn", [128, NKD * TC1], BF16))
        xz_ps = [E(_pt(nc, f"p1_xz{j}", [128, 512], F32)) for j in range(2)]
        cv_ps = E(_pt(nc, "p1_cv", [128, 512], F32))
        dbc_ps = E(_pt(nc, "p1_dbc", [96, TC1], F32))
        sz_t = [E(_sb(nc, f"p1_sz{j}", [128, 512], BF16)) for j in range(2)]
        xc_t = [E(_sb(nc, f"p1_xc{j}", [128, TC1], BF16)) for j in range(2)]

        def body(e):
            sy, sc, v, t, gp = e.sync, e.scalar, e.vector, e.tensor, e.gpsimd
            b_tcv = ctx.at(S["t_conv"])
            b_vh = ctx.at(S["v_halo"])
            b_txz = ctx.at(S["t_xz"])
            for k in range(NKD):
                ctx.dinc(gp.dma_start(
                    out=w1["win_w"][:, k * (2 * DI):(k + 1) * (2 * DI)],
                    in_=di["Win"][layer, k * 128:(k + 1) * 128, :]), S["d_w"])
            for kd in range(NDT):
                ctx.dinc(gp.dma_start(
                    out=w1["wx_w"][:, kd * 96:(kd + 1) * 96],
                    in_=di["Wx"][layer, kd * 128:(kd + 1) * 128, :]), S["d_w"])
            ctx.dinc(gp.dma_start(
                out=w1["cdiag_w"][:].rearrange("p (t k m) -> p t k m", t=NDT, k=KC),
                in_=di["convdiag"][layer].rearrange("t k p m -> p t k m")), S["d_w"])
            hv = w1["xp_buf"][:].rearrange("p (t c) -> p t c",
                                           c=HALO + TC1)[:, :, 0:HALO]
            ctx.inc(e.vector.memset(hv, 0.0), S["v_halo"])
            for ch in range(NCH1):
                c0 = ch * TC1
                if ch > 0:
                    ctx.wait(gp, S["v_xn"])
                for k in range(NKD):
                    ctx.dinc(gp.dma_start(
                        out=xt[:, k * TC1:(k + 1) * TC1],
                        in_=scr["x_resid"][k * 128:(k + 1) * 128, c0:c0 + TC1]),
                        S["d_xG"])
                ctx.wait(sc, S["d_xG"])
                for k in range(NKD):
                    if k >= 2:
                        ctx.wait_back(sc, S["t_ssq"], 1)
                    ctx.inc(sc.activation(sqb[k % 2][:], xt[:, k * TC1:(k + 1) * TC1],
                                          AF.Square), S["sc_sq"])
                    ctx.wait(t, S["sc_sq"])
                    if k == 0:
                        ctx.wait(t, S["sc_rstd"])
                        ctx.wait(t, S["d_w"])
                    last = None
                    for nn in range(NBC):
                        last = t.matmul(
                            s2_ps[0:1, nn * 512:(nn + 1) * 512], g["ones_cf"][:],
                            sqb[k % 2][:, nn * 512:(nn + 1) * 512],
                            start=(k == 0), stop=(k == NKD - 1))
                    ctx.inc(last, S["t_ssq"])
                ctx.wait(sc, S["t_ssq"])
                sc.activation(rrow[:], s2_ps[0:1, :], AF.Ln,
                              scale=g["cst"][0:1, 2:3], bias=g["cst"][0:1, 0:1])
                ctx.inc(sc.activation(rrow[:], rrow[:], AF.Exp,
                                      scale=g["cst"][0:1, 1:2]), S["sc_rstd"])
                ctx.wait(t, S["sc_rstd"])
                for nn in range(NBC):
                    if not (ch == 0 and layer == 0 and nn == 0):
                        ctx.wait(t, S["v_rb"])
                    ctx.inc(t.matmul(b_ps[:], g["ones_rf"][:],
                                     rrow[0:1, nn * 512:(nn + 1) * 512],
                                     start=True, stop=True), S["t_bc"])
                    ctx.wait(v, S["t_bc"])
                    ctx.inc(v.tensor_copy(r_b[:, nn * 512:(nn + 1) * 512], b_ps[:]),
                            S["v_rb"])
                if ch > 0:
                    v.wait_ge(S["t_xz"], b_txz + ch * 64)
                for k in range(NKD):
                    ctx.inc(v.scalar_tensor_tensor(
                        xn[:, k * TC1:(k + 1) * TC1], xt[:, k * TC1:(k + 1) * TC1],
                        g["nw_c"][:, layer * NKD + k:layer * NKD + k + 1], r_b[:],
                        ALU.mult, ALU.mult), S["v_xn"])
                for mt in range(32):
                    for nn in range(NBC):
                        gl = mt * NBC + nn
                        pj = gl % 2
                        if gl == 0:
                            ctx.wait(t, S["v_xn"])
                        if gl >= 2:
                            ctx.wait_back(t, S["sc_xz"], 1)
                        last = None
                        for k in range(NKD):
                            last = t.matmul(
                                xz_ps[pj][:],
                                w1["win_w"][:, k * 2 * DI + mt * 128:
                                            k * 2 * DI + (mt + 1) * 128],
                                xn[:, k * TC1 + nn * 512:k * TC1 + (nn + 1) * 512],
                                start=(k == 0), stop=(k == NKD - 1))
                        ctx.inc(last, S["t_xz"])
                        ctx.wait(sc, S["t_xz"])
                        if mt < NDT:
                            if ch > 0 and nn == 0:
                                sc.wait_ge(S["t_conv"],
                                           b_tcv + (ch - 1) * 32 + (mt + 1) * 2)
                                sc.wait_ge(S["v_halo"],
                                           b_vh + 1 + (ch - 1) * NDT + mt + 1)
                            dst = w1["xp_buf"][:,
                                               mt * (HALO + TC1) + HALO + nn * 512:
                                               mt * (HALO + TC1) + HALO +
                                               (nn + 1) * 512]
                            ctx.inc(sc.copy(dst, xz_ps[pj][:]), S["sc_xz"])
                        else:
                            zi = (mt - NDT) * NBC + nn
                            if zi >= 2:
                                ctx.wait_back(sc, S["d_spb"], 1, step=16)
                            ctx.inc(sc.activation(sz_t[zi % 2][:], xz_ps[pj][:],
                                                  AF.Silu), S["sc_xz"])
                            ctx.wait(gp, S["sc_xz"])
                            dt_i = mt - NDT
                            ctx.dinc(gp.dma_start(
                                out=scr["sz_sp"][dt_i * 128:(dt_i + 1) * 128,
                                                 c0 + nn * 512:c0 + (nn + 1) * 512],
                                in_=sz_t[zi % 2][:]), S["d_spb"])
                    if mt < NDT:
                        base = mt * (HALO + TC1)
                        ctx.wait(t, S["sc_xz"])
                        for nn in range(NBC):
                            if not (layer == 0 and ch == 0 and mt == 0 and nn == 0):
                                ctx.wait(t, S["sc_cv"])
                            last = None
                            for kk in range(KC):
                                last = t.matmul(
                                    cv_ps[:],
                                    w1["cdiag_w"][:, (mt * KC + kk) * 128:
                                                  (mt * KC + kk + 1) * 128],
                                    w1["xp_buf"][:, base + nn * 512 + kk:
                                                 base + nn * 512 + kk + 512],
                                    start=(kk == 0), stop=(kk == KC - 1))
                            ctx.inc(last, S["t_conv"])
                            ctx.wait(sc, S["t_conv"])
                            if mt >= 2 and nn == 0:
                                ctx.wait_back(sc, S["t_wx"], 1)
                                ctx.wait_back(sc, S["d_spa"], 1, step=16)
                            ctx.inc(sc.activation(
                                xc_t[mt % 2][:, nn * 512:(nn + 1) * 512], cv_ps[:],
                                AF.Silu,
                                bias=g["convb_c"][:, layer * NDT + mt:
                                                  layer * NDT + mt + 1]), S["sc_cv"])
                        ctx.wait(gp, S["sc_cv"])
                        ctx.dinc(gp.dma_start(
                            out=scr["xc_sp"][mt * 128:(mt + 1) * 128, c0:c0 + TC1],
                            in_=xc_t[mt % 2][:]), S["d_spa"])
                        ctx.wait(t, S["sc_cv"])
                        if mt == 0 and not (ch == 0 and layer == 0):
                            ctx.wait(t, S["v_evac"])
                        last = None
                        for nn in range(NBC):
                            last = t.matmul(dbc_ps[:, nn * 512:(nn + 1) * 512],
                                            w1["wx_w"][:, mt * 96:(mt + 1) * 96],
                                            xc_t[mt % 2][:, nn * 512:(nn + 1) * 512],
                                            start=(mt == 0), stop=(mt == NDT - 1))
                        ctx.inc(last, S["t_wx"])
                        ctx.wait(v, S["t_conv"])
                        ctx.inc(v.tensor_copy(
                            w1["xp_buf"][:, base:base + HALO],
                            w1["xp_buf"][:, base + TC1:base + TC1 + HALO]),
                            S["v_halo"])
                ctx.wait(v, S["t_wx"])
                last = None
                for nn in range(NBC):
                    last = v.tensor_copy(
                        w1["dbc_sb"][:, c0 + nn * 512:c0 + (nn + 1) * 512],
                        dbc_ps[:, nn * 512:(nn + 1) * 512])
                ctx.inc(last, S["v_evac"])
            ctx.inc(v.tensor_copy(lay["dt_full"][:], w1["dbc_sb"][0:64, :]),
                    S["v_misc"])
            ctx.wait(gp, S["v_misc"])
            ctx.dinc(gp.dma_start(out=scr["bc_sp"], in_=w1["dbc_sb"][64:96, :]),
                     S["d_spc"])
            if DEBUG_TAPS and layer == 0:
                ctx.dinc(gp.dma_start(out=taps["t_dbc0"], in_=w1["dbc_sb"][:]),
                         S["d_spc"])
            ctx.wait(gp, S["d_spa"])
            ctx.wait(gp, S["d_spb"])
            ctx.wait(gp, S["d_spc"])
        ctx.phase(body)


def phase_pass23(nc, ctx, S, di, g, scr, taps, layer, lay, w2, **_):
    with ExitStack() as es:
        E = es.enter_context
        Ball = E(_sb(nc, "p2_B", [128, NST * TC2], BF16))
        Call = E(_sb(nc, "p2_C", [128, NST * TC2], BF16))
        del_ps = E(_pt(nc, "p2_dps", [128, TC2], F32))
        delta = [E(_sb(nc, f"p2_del{j}", [128, TC2], F32)) for j in range(2)]
        sp_t = E(_sb(nc, "p2_spt", [128, TC2], F32))
        xct = [E(_sb(nc, f"p2_xc{j}", [128, TC2], BF16)) for j in range(2)]
        szt = [E(_sb(nc, f"p2_sz{j}", [128, TC2], BF16)) for j in range(2)]
        du_t = E(_sb(nc, "p2_du", [128, TC2], BF16))
        dA = [E(_sb(nc, f"p2_dA{j}", [128, TC2], F32)) for j in range(2)]
        wbuf = [E(_sb(nc, f"p2_w{j}", [128, TC2], BF16)) for j in range(2)]
        hbuf = [E(_sb(nc, f"p2_h{j}", [128, TC2], BF16)) for j in range(2)]
        yacc = E(_sb(nc, "p2_y", [128, TC2], F32))
        y_ps = E(_pt(nc, "p2_yps", [128, TC2], F32))
        yout = E(_sb(nc, "p2_yo", [128, NDT * TC2], BF16))
        x_ps = [E(_pt(nc, f"p3_x{j}", [128, 512], F32)) for j in range(2)]
        xrt = [E(_sb(nc, f"p3_xr{j}", [128, TC2], F32)) for j in range(2)]
        taps_sb = {}
        if DEBUG_TAPS:
            shared = E(_sb(nc, "tap_shared", [128, TC2], F32))
            for nm2 in ["del", "du", "w", "h", "dA"]:
                taps_sb[nm2] = shared

        def body(e):
            sy, sc, v, t, gp = e.sync, e.scalar, e.vector, e.tensor, e.gpsimd
            b_vdt = ctx.at(S["v_dt"])
            b_tp3 = ctx.at(S["t_p3"])
            svi = [0]
            sgi = [0]
            for kd in range(NDT):
                ctx.dinc(gp.dma_start(
                    out=w2["wblk_w"][:, kd * D:(kd + 1) * D],
                    in_=di["Wblk"][layer, kd * 128:(kd + 1) * 128, :]), S["d_w"])
            ctx.dinc(gp.dma_start(out=w2["wdt_w"][:], in_=di["Wdt"][layer]), S["d_w"])
            for ch in range(NCH2):
                c0 = ch * TC2
                if ch > 0:
                    gp.wait_ge(S["v_dt"], b_vdt + ch * NDT)
                for q in range(4):
                    ctx.dinc(gp.dma_start(
                        out=Ball[:].rearrange("p (n t) -> p n t", n=NST)
                        [:, q * 4:(q + 1) * 4, :],
                        in_=scr["bc_sp"][q * 4:(q + 1) * 4, c0:c0 + TC2]
                        .partition_broadcast(128)), S["d_bc"])
                    ctx.dinc(gp.dma_start(
                        out=Call[:].rearrange("p (n t) -> p n t", n=NST)
                        [:, q * 4:(q + 1) * 4, :],
                        in_=scr["bc_sp"][NST + q * 4:NST + (q + 1) * 4, c0:c0 + TC2]
                        .partition_broadcast(128)), S["d_bc"])
                for dt_i in range(NDT):
                    gd = ch * NDT + dt_i
                    pj = gd % 2
                    if gd == 0:
                        ctx.wait(t, S["d_w"])
                    if gd >= 1:
                        ctx.wait(t, S["sc_sp"])
                    if ch > 0 and dt_i == 0:
                        t.wait_ge(S["t_p3"], b_tp3 + ch * NKD * 2)
                    last = None
                    for nn in range(TC2 // 512):
                        last = t.matmul(
                            del_ps[:, nn * 512:(nn + 1) * 512],
                            w2["wdt_w"][:, dt_i * 128:(dt_i + 1) * 128],
                            lay["dt_full"][:, c0 + nn * 512:c0 + (nn + 1) * 512],
                            start=True, stop=True)
                    ctx.inc(last, S["t_del"])
                    if gd >= 2:
                        ctx.wait_back(sy, S["v_dt"], 1)
                    ctx.dinc(sy.dma_start(
                        out=xct[pj][:],
                        in_=scr["xc_sp"][dt_i * 128:(dt_i + 1) * 128, c0:c0 + TC2]),
                        S["d_t"])
                    ctx.dinc(sy.dma_start(
                        out=szt[pj][:],
                        in_=scr["sz_sp"][dt_i * 128:(dt_i + 1) * 128, c0:c0 + TC2]),
                        S["d_t"])
                    ctx.wait(sc, S["t_del"])
                    if gd >= 1:
                        ctx.wait(sc, S["v_del"])
                    bcol = g["bdt_c"][:, layer * NDT + dt_i:layer * NDT + dt_i + 1]
                    sc.activation(sp_t[:], del_ps[:], AF.Abs, bias=bcol)
                    sc.activation(sp_t[:], sp_t[:], AF.Exp, scale=-1.0)
                    sc.activation(sp_t[:], sp_t[:], AF.Ln, bias=1.0)
                    ctx.inc(sc.activation(delta[pj][:], del_ps[:], AF.Relu,
                                          bias=bcol), S["sc_sp"])
                    ctx.wait(v, S["sc_sp"])
                    ctx.wait(v, S["d_t"])
                    if dt_i == 0:
                        ctx.wait(v, S["d_bc"])
                        if ch > 0:
                            v.wait_ge(S["t_p3"], b_tp3 + ch * NKD * 2)
                    ctx.inc(v.tensor_tensor(delta[pj][:], delta[pj][:], sp_t[:],
                                            ALU.add), S["v_del"])
                    ctx.inc(v.tensor_tensor(du_t[:], delta[pj][:], xct[pj][:],
                                            ALU.mult), S["v_du"])
                    if DEBUG_TAPS and layer == 0 and dt_i == 0:
                        ctx.inc(v.tensor_copy(taps_sb["del"][:], delta[pj][:]),
                                S["v_misc"])
                        ctx.wait(gp, S["v_misc"])
                        ctx.dinc(gp.dma_start(
                            out=taps["t_del0"][:, c0:c0 + TC2],
                            in_=taps_sb["del"][:]), S["d_spc"])
                        ctx.wait(v, S["d_spc"])
                        ctx.inc(v.tensor_copy(taps_sb["du"][:], du_t[:]), S["v_misc"])
                        ctx.wait(gp, S["v_misc"])
                        if ch == 0:
                            ctx.dinc(gp.dma_start(out=taps["t_du0"],
                                                  in_=taps_sb["du"][:]), S["d_spc"])
                            ctx.dinc(gp.dma_start(out=taps["t_B0"], in_=Ball[:]),
                                     S["d_spc"])
                        ctx.wait(v, S["d_spc"])
                    for n in range(NST):
                        j = n % 2
                        if n == 0:
                            ctx.wait(sc, S["v_del"])
                        if svi[0] >= 2:
                            ctx.wait_back(sc, S["v_scf"], 1)
                        acol = g["a_c"][:, (layer * NDT + dt_i) * NST + n:
                                        (layer * NDT + dt_i) * NST + n + 1]
                        ctx.inc(sc.activation(dA[j][:], delta[pj][:], AF.Exp,
                                              scale=acol), S["sc_dA"])
                        v.tensor_tensor(wbuf[j][:], du_t[:],
                                        Ball[:, n * TC2:(n + 1) * TC2],
                                        ALU.mult)
                        ctx.wait(v, S["sc_dA"])
                        if svi[0] >= 2:
                            ctx.wait_back(v, S["t_yac"], 1)
                        init = (0.0 if ch == 0 else
                                w2["carry"][:, dt_i * NST + n:
                                            dt_i * NST + n + 1])
                        ctx.inc(v.tensor_tensor_scan(hbuf[j][:], dA[j][:],
                                                     wbuf[j][:], init, ALU.mult,
                                                     ALU.add), S["v_scf"])
                        if DEBUG_TAPS and layer == 0 and ch == 0 \
                                and dt_i == 0 and n == 0:
                            for tnm, tsrc in [("t_w00", wbuf[j]),
                                              ("t_h00", hbuf[j]),
                                              ("t_dA00", dA[j])]:
                                ctx.inc(v.tensor_copy(taps_sb["w"][:],
                                                      tsrc[:]), S["v_misc"])
                                ctx.wait(gp, S["v_misc"])
                                ctx.dinc(gp.dma_start(out=taps[tnm],
                                                      in_=taps_sb["w"][:]),
                                         S["d_spc"])
                                ctx.wait(v, S["d_spc"])
                        if ch < NCH2 - 1:
                            v.tensor_copy(w2["carry"][:, dt_i * NST + n:
                                                      dt_i * NST + n + 1],
                                          hbuf[j][:, TC2 - 1:TC2])
                        ctx.inc(v.tensor_tensor(hbuf[j][:], hbuf[j][:],
                                                Call[:, n * TC2:(n + 1) * TC2],
                                                ALU.mult), S["v_cm"])
                        ctx.wait(t, S["v_cm"])
                        if n == 0:
                            ctx.wait(t, S["v_dt"])
                        last_acc = None
                        for nn in range(TC2 // 512):
                            last_acc = t.matmul(
                                y_ps[:, nn * 512:(nn + 1) * 512], g["identb"][:],
                                hbuf[j][:, nn * 512:(nn + 1) * 512],
                                start=(n == 0), stop=(n == NST - 1))
                        ctx.inc(last_acc, S["t_yac"])
                        svi[0] += 1
                    dcol = g["dp_c"][:, layer * NDT + dt_i:layer * NDT + dt_i + 1]
                    ctx.wait(v, S["t_yac"])
                    v.scalar_tensor_tensor(yacc[:], xct[pj][:], dcol, y_ps[:],
                                           ALU.mult, ALU.add)
                    ctx.inc(v.tensor_tensor(yout[:, dt_i * TC2:(dt_i + 1) * TC2],
                                            yacc[:], szt[pj][:], ALU.mult), S["v_dt"])
                if DEBUG_TAPS and layer == 0 and ch in (0, 3):
                    ctx.wait(gp, S["v_dt"])
                    ctx.dinc(gp.dma_start(out=taps["t_y0" if ch == 0 else "t_y1"],
                                          in_=yout[:]), S["d_spc"])
                for mt in range(NKD):
                    gm3 = ch * NKD + mt
                    if gm3 >= 2:
                        ctx.wait_back(sy, S["d_spc"], 1, step=16)
                    ctx.dinc(sy.dma_start(
                        out=xrt[mt % 2][:],
                        in_=scr["x_resid"][mt * 128:(mt + 1) * 128, c0:c0 + TC2]),
                        S["d_x"])
                    for nn in range(TC2 // 512):
                        if mt == 0 and nn == 0:
                            ctx.wait(t, S["v_dt"])
                        if gm3 * 2 + nn >= 1:
                            ctx.wait_back(t, S["v_p3"], 1)
                        last = None
                        for kd in range(NDT):
                            last = t.matmul(
                                x_ps[nn][:],
                                w2["wblk_w"][:, kd * D + mt * 128:
                                             kd * D + (mt + 1) * 128],
                                yout[:, kd * TC2 + nn * 512:
                                     kd * TC2 + (nn + 1) * 512],
                                start=(kd == 0), stop=(kd == NDT - 1))
                        ctx.inc(last, S["t_p3"])
                        ctx.wait(v, S["t_p3"])
                        ctx.wait(v, S["d_x"])
                        ctx.inc(v.tensor_tensor(
                            xrt[mt % 2][:, nn * 512:(nn + 1) * 512],
                            xrt[mt % 2][:, nn * 512:(nn + 1) * 512], x_ps[nn][:],
                            ALU.add), S["v_p3"])
                    ctx.wait(gp, S["v_p3"])
                    ctx.dinc(gp.dma_start(
                        out=scr["x_resid"][mt * 128:(mt + 1) * 128, c0:c0 + TC2],
                        in_=xrt[mt % 2][:]), S["d_spc"])
            ctx.wait(gp, S["d_spc"])
            if DEBUG_TAPS and layer in (0, NL - 1):
                ctx.wait(sy, S["d_spc"])
                tap = taps["t_res1"] if layer == 0 else taps["t_res4"]
                for k in range(NKD):
                    for cq in range(NCH2):
                        ctx.dinc(sy.dma_start(
                            out=xrt[0][:],
                            in_=scr["x_resid"][k * 128:(k + 1) * 128,
                                               cq * TC2:(cq + 1) * TC2]), S["d_x"])
                        ctx.wait(sy, S["d_x"])
                        ctx.dinc(sy.dma_start(
                            out=tap[k * 128:(k + 1) * 128, cq * TC2:(cq + 1) * TC2],
                            in_=xrt[0][:]), S["d_spH"])
                        ctx.wait(sy, S["d_spH"])
        ctx.phase(body)


def phase_wout(nc, ctx, S, di, g, scr, taps, **_):
    with ExitStack() as es:
        E = es.enter_context
        h4b = E(_sb(nc, "wo_h4", [128, NKD * L], BF16))
        h4f = [E(_sb(nc, f"wo_hf{j}", [128, L], F32)) for j in range(2)]
        wout_w = E(_sb(nc, "wo_w", [128, NKD * D], BF16))
        o_ps = [E(_pt(nc, f"wo_ps{j}", [128, 512], F32)) for j in range(2)]
        xtt = [E(_sb(nc, f"wo_xt{j}", [128, D], F32)) for j in range(2)]
        x1t = [E(_sb(nc, f"wo_x1{j}", [128, D], F32)) for j in range(2)]
        ga_b = E(_sb(nc, "wo_ga", [128, D], F32))
        bout_b = E(_sb(nc, "wo_bo", [128, D], F32))

        def body(e):
            sy, sc, v, t, gp = e.sync, e.scalar, e.vector, e.tensor, e.gpsimd
            for nm, j in [(ga_b, 2), (bout_b, 6)]:
                ctx.dinc(sy.dma_start(
                    out=nm[:],
                    in_=scr["row_sp"][0:1, j * D:(j + 1) * D].partition_broadcast(128)),
                    S["d_bcH"])
            for k in range(NKD):
                ctx.dinc(gp.dma_start(out=wout_w[:, k * D:(k + 1) * D],
                                      in_=di["Wout"][k * 128:(k + 1) * 128, :]),
                         S["d_w"])
                if k >= 2:
                    ctx.wait_back(sy, S["v_misc"], 1)
                ctx.dinc(sy.dma_start(out=h4f[k % 2][:],
                                      in_=scr["x_resid"][k * 128:(k + 1) * 128, :]),
                         S["d_x"])
                ctx.wait(v, S["d_x"])
                ctx.inc(v.tensor_copy(h4b[:, k * L:(k + 1) * L], h4f[k % 2][:]),
                        S["v_misc"])
            for tt in range(NTT):
                if tt >= 2:
                    ctx.wait_back(sy, S["d_spa"], 2 if DEBUG_TAPS else 1, step=16)
                ctx.dinc(sy.dma_start(out=xtt[tt % 2][:],
                                      in_=di["x_T"][tt * 128:(tt + 1) * 128, :]),
                         S["d_x"])
                for nn in range(2):
                    if tt == 0 and nn == 0:
                        ctx.wait(t, S["v_misc"])
                        ctx.wait(t, S["d_w"])
                    if tt * 2 + nn >= 1:
                        ctx.wait_back(t, S["v_p3"], 1)
                    last = None
                    for k in range(NKD):
                        last = t.matmul(
                            o_ps[nn][:],
                            h4b[:, k * L + tt * 128:k * L + (tt + 1) * 128],
                            wout_w[:, k * D + nn * 512:k * D + (nn + 1) * 512],
                            start=(k == 0), stop=(k == NKD - 1))
                    ctx.inc(last, S["t_p3"])
                    ctx.wait(v, S["t_p3"])
                    if nn == 0:
                        ctx.wait(v, S["d_x"])
                        ctx.wait(v, S["d_bcH"])
                    sl = slice(nn * 512, (nn + 1) * 512)
                    v.tensor_tensor(x1t[tt % 2][:, sl], o_ps[nn][:], bout_b[:, sl],
                                    ALU.add)
                    v.tensor_tensor(x1t[tt % 2][:, sl], x1t[tt % 2][:, sl],
                                    ga_b[:, sl], ALU.mult)
                    ctx.inc(v.tensor_tensor(x1t[tt % 2][:, sl], x1t[tt % 2][:, sl],
                                            xtt[tt % 2][:, sl], ALU.add), S["v_p3"])
                ctx.wait(gp, S["v_p3"])
                ctx.dinc(gp.dma_start(out=scr["x1_sp"][tt * 128:(tt + 1) * 128, :],
                                      in_=x1t[tt % 2][:]), S["d_spa"])
                if DEBUG_TAPS:
                    ctx.dinc(gp.dma_start(out=taps["t_x1"][tt * 128:(tt + 1) * 128, :],
                                          in_=x1t[tt % 2][:]), S["d_spa"])
            ctx.wait(gp, S["d_spa"])
        ctx.phase(body)


QTHR = [(4.0, 1.0), (1.0 / 0.75, 1.0), (0.8, 1.0), (1.0 / 1.75, 1.0),
        (0.4, 2.0), (1.0 / 3.5, 2.0), (0.2, 4.0)]


def emit_quant(ctx, S, sc, v, gp, src, dst, q, F, sem_done):
    NB = F // 16
    FW = max(F, 512)
    NBW = FW // 16
    sc.activation(q["a2"][:, 0:F], src, AF.Abs)
    ctx.inc(sc.activation(q["sgn"][:, 0:F], src, AF.Sign), S["sc_q"])
    ctx.inc(v.reduce_max(q["bm"][:, 0:NB],
                         src.rearrange("p (b s) -> p b s", s=16), AX.X,
                         apply_absolute_value=True), S["v_bm"])
    ctx.wait(gp, S["v_bm"])
    ctx.inc(gp.tensor_scalar(q["scl"][:, 0:NB], q["bm"][:, 0:NB], 1.0 / 6.0,
                             1e-12, ALU.mult, ALU.max), S["g_scl"])
    ctx.wait(sc, S["g_scl"])
    ctx.inc(sc.activation(q["lnt"][:, 0:NB], q["scl"][:, 0:NB], AF.Ln),
            S["sc_ln2"])
    ctx.wait(gp, S["sc_ln2"])
    ctx.inc(gp.tensor_copy(q["lnt2"][:, 0:NB], q["lnt"][:, 0:NB]), S["g_b2"])
    ctx.wait(sc, S["g_b2"])
    ctx.inc(sc.activation(q["rs"][:, 0:NB], q["lnt2"][:, 0:NB], AF.Exp,
                          scale=-1.0), S["sc_rs"])
    rsb = q["rs"][:, 0:NBW].rearrange("p (b o) -> p b o", o=1) \
        .broadcast_to([128, NBW, 16])
    sclb = q["scl"][:, 0:NBW].rearrange("p (b o) -> p b o", o=1) \
        .broadcast_to([128, NBW, 16])
    a2v = q["a2"][:, 0:FW].rearrange("p (b s) -> p b s", s=16)
    sgv = q["sgn"][:, 0:FW].rearrange("p (b s) -> p b s", s=16)
    ctx.wait(v, S["sc_q"])
    ctx.wait(v, S["sc_rs"])
    v.tensor_tensor(a2v, a2v, rsb, ALU.mult)
    a2f = q["a2"][:, 0:FW]
    v.tensor_scalar(q["qa"][:, 0:FW], a2f, 2.0, 2.0, ALU.min, ALU.mult)
    v.tensor_scalar(q["qa"][:, 0:FW], q["qa"][:, 0:FW], MAGIC, MAGIC, ALU.add,
                    ALU.subtract)
    v.tensor_scalar(q["r2"][:, 0:FW], a2f, 2.0, 2.0, ALU.subtract, ALU.min)
    v.tensor_scalar(q["r2"][:, 0:FW], q["r2"][:, 0:FW], MAGIC, MAGIC, ALU.add,
                    ALU.subtract)
    v.tensor_relu(q["r2"][:, 0:FW], q["r2"][:, 0:FW])
    v.scalar_tensor_tensor(q["qa"][:, 0:FW], q["qa"][:, 0:FW], 0.5,
                           q["r2"][:, 0:FW], ALU.mult, ALU.add)
    v.tensor_scalar(q["r2"][:, 0:FW], a2f, 5.0, 2.0, ALU.is_gt, ALU.mult)
    v.tensor_tensor(q["qa"][:, 0:FW], q["qa"][:, 0:FW], q["r2"][:, 0:FW],
                    ALU.add)
    v.tensor_tensor(sgv, sgv, sclb, ALU.mult)
    ctx.inc(v.tensor_tensor(dst, q["qa"][:, 0:F], q["sgn"][:, 0:F], ALU.mult),
            sem_done)


def phase_ffn_wq(nc, ctx, S, di, g, scr, **_):
    with ExitStack() as es:
        E = es.enter_context
        wrow = [E(_sb(nc, f"fq_w{j}", [128, IFF], F32)) for j in range(2)]
        wq = E(_sb(nc, "fq_q", [128, IFF], BF16))
        tr_ps = E(_pt(nc, "fq_tr", [128, 128], BF16))
        trs = [E(_sb(nc, f"fq_t{j}", [128, 128], BF16)) for j in range(2)]
        q = {k: E(_sb(nc, f"fq_{k}", [128, IFF], F32))
             for k in ["a2", "qa", "r2", "sgn"]}
        q.update({k: E(_sb(nc, f"fq_{k}", [128, IFF // 16], F32))
                  for k in ["bm", "scl", "rs", "lnt", "lnt2"]})

        def body(e):
            sy, sc, v, t, gp = e.sync, e.scalar, e.vector, e.tensor, e.gpsimd
            jobs = [(di["wg"], IFF // 128, D, scr["wgqT_sp"]),
                    (di["wu"], IFF // 128, D, scr["wuqT_sp"]),
                    (di["wdn"], D // 128, IFF, scr["wdqT_sp"])]
            si = 0
            gi = [0]
            for (src, nsl, rl, dst) in jobs:
                for s in range(nsl):
                    if si >= 2:
                        ctx.wait_back(sy, S["sc_q"], 1)
                        ctx.wait_back(sy, S["v_q"], 1)
                    ctx.dinc(sy.dma_start(out=wrow[si % 2][:, 0:rl],
                                          in_=src[s * 128:(s + 1) * 128, :]), S["d_x"])
                    ctx.wait(sc, S["d_x"])
                    ctx.wait(v, S["d_x"])
                    if si >= 1:
                        ctx.wait(v, S["t_tr"])
                        ctx.wait(sc, S["v_q"])
                    emit_quant(ctx, S, sc, v, gp, wrow[si % 2][:, 0:rl],
                               wq[:, 0:rl], q, rl, S["v_q"])
                    ctx.wait(t, S["v_q"])
                    for kb in range(rl // 128):
                        ctx.wait(t, S["v_tr"])
                        ctx.inc(t.transpose(tr_ps[:], wq[:, kb * 128:(kb + 1) * 128],
                                            g["identb"][:]), S["t_tr"])
                        ctx.wait(v, S["t_tr"])
                        if gi[0] >= 2:
                            ctx.wait_back(v, S["d_spa"], 1, step=16)
                        ctx.inc(v.tensor_copy(trs[gi[0] % 2][:], tr_ps[:]), S["v_tr"])
                        ctx.wait(gp, S["v_tr"])
                        ctx.dinc(gp.dma_start(
                            out=dst[kb * 128:(kb + 1) * 128, s * 128:(s + 1) * 128],
                            in_=trs[gi[0] % 2][:]), S["d_spa"])
                        gi[0] += 1
                    si += 1
            ctx.wait(gp, S["d_spa"])
        ctx.phase(body)


def phase_ffn_half(nc, ctx, S, di, g, scr, out_t, taps, half, **_):
    TH = L // 4
    NTH = TH // 128
    h0 = half * TH
    ICH = [(0, 512), (512, 512), (1024, 512), (1536, 512), (2048, 512), (2560, 256)]
    with ExitStack() as es:
        E = es.enter_context
        x1t = [E(_sb(nc, f"ff_x1{j}", [128, D], F32)) for j in range(2)]
        hf = E(_sb(nc, "ff_hf", [128, D], F32))
        mcol = E(_sb(nc, "ff_m", [128, 4], F32))
        xq = E(_sb(nc, "ff_xq", [128, D], BF16))
        tr_ps = E(_pt(nc, "ff_tr", [128, 128], BF16))
        q = {k: E(_sb(nc, f"ffq_{k}", [128, D], F32))
             for k in ["a2", "qa", "r2", "sgn"]}
        q.update({k: E(_sb(nc, f"ffq_{k}", [128, D // 16], F32))
                  for k in ["bm", "scl", "rs", "lnt", "lnt2"]})
        h2qT = E(_sb(nc, "ff_h2T", [128, NIC * TH], BF16))
        g_ps = E(_pt(nc, "ff_g", [128, 512], F32))
        u_ps = E(_pt(nc, "ff_u", [128, 512], F32))
        gsb = E(_sb(nc, "ff_gs", [128, 512], F32))
        h2 = E(_sb(nc, "ff_h2", [128, 512], F32))
        h2q = E(_sb(nc, "ff_h2q", [128, 512], BF16))
        d_ps = [E(_pt(nc, f"ff_d{j}", [128, 512], F32)) for j in range(2)]
        osb = [E(_sb(nc, f"ff_o{j}", [128, D], F32)) for j in range(2)]
        cb1_b = E(_sb(nc, "ff_cb1", [128, D], F32))
        sb_b = E(_sb(nc, "ff_sb", [128, D], F32))
        gb_b = E(_sb(nc, "ff_gb", [128, D], F32))

        def phase1(e, xqT, wgc, wuc):
            sy, sc, v, t, gp = e.sync, e.scalar, e.vector, e.tensor, e.gpsimd
            for nm, j in [(cb1_b, 4), (sb_b, 3), (gb_b, 5)]:
                ctx.dinc(sy.dma_start(
                    out=nm[:],
                    in_=scr["row_sp"][0:1, j * D:(j + 1) * D].partition_broadcast(128)),
                    S["d_bcH"])
            for tt in range(NTH):
                r0 = h0 + tt * 128
                if tt >= 2:
                    ctx.wait_back(sy, S["v_q"], 1)
                ctx.dinc(sy.dma_start(out=x1t[tt % 2][:],
                                      in_=scr["x1_sp"][r0:r0 + 128, :]), S["d_x"])
                ctx.wait(sc, S["d_x"])
                if tt >= 1:
                    ctx.wait(sc, S["v_q"])
                ctx.inc(sc.activation(q["a2"][:, 0:D], x1t[tt % 2][:], AF.Square),
                        S["sc_q"])
                ctx.wait(v, S["d_x"])
                if tt == 0:
                    ctx.wait(v, S["d_bcH"])
                v.reduce_sum(mcol[:, 0:1], x1t[tt % 2][:], AX.X)
                ctx.wait(v, S["sc_q"])
                ctx.inc(v.reduce_sum(mcol[:, 1:2], q["a2"][:, 0:D], AX.X),
                        S["v_stat"])
                ctx.wait(gp, S["v_stat"])
                ctx.inc(gp.tensor_scalar_mul(mcol[:, 2:3], mcol[:, 0:1], 1.0 / D),
                        S["g_a"])
                ctx.wait(sc, S["g_a"])
                ctx.inc(sc.activation(mcol[:, 3:4], mcol[:, 2:3], AF.Square),
                        S["sc_m2"])
                ctx.wait(v, S["sc_m2"])
                ctx.inc(v.scalar_tensor_tensor(mcol[:, 1:2], mcol[:, 1:2], 1.0 / D,
                                               mcol[:, 3:4], ALU.mult, ALU.subtract),
                        S["v_var"])
                ctx.wait(sc, S["v_var"])
                ctx.inc(sc.activation(mcol[:, 1:2], mcol[:, 1:2], AF.Ln,
                                      bias=g["cst"][:, 0:1]), S["sc_ln"])
                ctx.wait(gp, S["sc_ln"])
                ctx.inc(gp.tensor_copy(mcol[:, 3:4], mcol[:, 1:2]), S["g_b"])
                ctx.wait(sc, S["g_b"])
                ctx.inc(sc.activation(mcol[:, 1:2], mcol[:, 3:4], AF.Exp,
                                      scale=g["cst"][:, 1:2]), S["sc_rstd"])
                ctx.wait(v, S["sc_rstd"])
                v.tensor_scalar(hf[:], x1t[tt % 2][:], mcol[:, 2:3], mcol[:, 1:2],
                                ALU.subtract, ALU.mult)
                v.tensor_tensor(hf[:], hf[:], cb1_b[:], ALU.mult)
                ctx.inc(v.tensor_tensor(hf[:], hf[:], sb_b[:], ALU.add), S["v_misc"])
                ctx.wait(sc, S["v_misc"])
                ctx.wait(v, S["t_tr"])
                emit_quant(ctx, S, sc, v, gp, hf[:], xq[:], q, D, S["v_q"])
                ctx.wait(t, S["v_q"])
                for kb in range(NKD):
                    ctx.wait(t, S["v_tr"])
                    ctx.inc(t.transpose(tr_ps[:], xq[:, kb * 128:(kb + 1) * 128],
                                        g["identb"][:]), S["t_tr"])
                    ctx.wait(v, S["t_tr"])
                    ctx.inc(v.tensor_copy(
                        xqT[:, kb * TH + tt * 128:kb * TH + (tt + 1) * 128],
                        tr_ps[:]), S["v_tr"])
            for ci, (i0, iw) in enumerate(ICH):
                cj = ci % 2
                if ci >= 2:
                    ctx.wait_back(sy, S["t_gu"], NTH)
                for k in range(NKD):
                    ctx.dinc(sy.dma_start(
                        out=wgc[cj][:, k * 512:k * 512 + iw],
                        in_=scr["wgqT_sp"][k * 128:(k + 1) * 128, i0:i0 + iw]),
                        S["d_q"])
                    ctx.dinc(sy.dma_start(
                        out=wuc[cj][:, k * 512:k * 512 + iw],
                        in_=scr["wuqT_sp"][k * 128:(k + 1) * 128, i0:i0 + iw]),
                        S["d_q"])
                ctx.wait(t, S["d_q"])
                if ci == 0:
                    ctx.wait(t, S["v_tr"])
                for tt in range(NTH):
                    gti = ci * NTH + tt
                    if gti >= 1:
                        ctx.wait(t, S["sc_cv"])
                        ctx.wait(t, S["v_misc"])
                    last = None
                    for k in range(NKD):
                        last = t.matmul(
                            g_ps[:, 0:iw],
                            xqT[:, k * TH + tt * 128:k * TH + (tt + 1) * 128],
                            wgc[cj][:, k * 512:k * 512 + iw],
                            start=(k == 0), stop=(k == NKD - 1))
                    for k in range(NKD):
                        last = t.matmul(
                            u_ps[:, 0:iw],
                            xqT[:, k * TH + tt * 128:k * TH + (tt + 1) * 128],
                            wuc[cj][:, k * 512:k * 512 + iw],
                            start=(k == 0), stop=(k == NKD - 1))
                    ctx.inc(last, S["t_gu"])
                    ctx.wait(sc, S["t_gu"])
                    ctx.inc(sc.activation(gsb[:, 0:iw], g_ps[:, 0:iw], AF.Silu),
                            S["sc_cv"])
                    ctx.wait(v, S["sc_cv"])
                    v.tensor_copy(h2[:, 0:iw], u_ps[:, 0:iw])
                    ctx.inc(v.tensor_tensor(h2[:, 0:iw], h2[:, 0:iw], gsb[:, 0:iw],
                                            ALU.mult), S["v_misc"])
                    ctx.wait(sc, S["v_misc"])
                    ctx.wait(v, S["t_tr"])
                    emit_quant(ctx, S, sc, v, gp, h2[:, 0:iw], h2q[:, 0:iw], q,
                               iw, S["v_q"])
                    ctx.wait(t, S["v_q"])
                    for kb in range(iw // 128):
                        ic = (i0 // 128) + kb
                        ctx.wait(t, S["v_tr"])
                        ctx.inc(t.transpose(tr_ps[:], h2q[:, kb * 128:(kb + 1) * 128],
                                            g["identb"][:]), S["t_tr"])
                        ctx.wait(v, S["t_tr"])
                        ctx.inc(v.tensor_copy(
                            h2qT[:, ic * TH + tt * 128:ic * TH + (tt + 1) * 128],
                            tr_ps[:]), S["v_tr"])

        def phase2(e, wdc):
            sy, sc, v, t, gp = e.sync, e.scalar, e.vector, e.tensor, e.gpsimd
            ctx.wait(sy, S["t_gu"])
            for ic in range(NIC):
                ctx.dinc(sy.dma_start(out=wdc[:, ic * D:(ic + 1) * D],
                                      in_=scr["wdqT_sp"][ic * 128:(ic + 1) * 128, :]),
                         S["d_q"])
            ctx.wait(t, S["d_q"])
            ctx.wait(t, S["v_tr"])
            for tt in range(NTH):
                r0 = h0 + tt * 128
                if tt >= 2:
                    ctx.wait_back(sy, S["d_st"], 1, step=16)
                ctx.dinc(sy.dma_start(out=x1t[tt % 2][:],
                                      in_=scr["x1_sp"][r0:r0 + 128, :]), S["d_x"])
                for nn in range(2):
                    if tt * 2 + nn >= 1:
                        ctx.wait_back(t, S["v_p3"], 1)
                    last = None
                    for ic in range(NIC):
                        last = t.matmul(
                            d_ps[nn][:],
                            h2qT[:, ic * TH + tt * 128:ic * TH + (tt + 1) * 128],
                            wdc[:, ic * D + nn * 512:ic * D + (nn + 1) * 512],
                            start=(ic == 0), stop=(ic == NIC - 1))
                    ctx.inc(last, S["t_dn"])
                    ctx.wait(v, S["t_dn"])
                    if nn == 0:
                        ctx.wait(v, S["d_x"])
                    sl = slice(nn * 512, (nn + 1) * 512)
                    v.tensor_tensor(osb[tt % 2][:, sl], d_ps[nn][:], gb_b[:, sl],
                                    ALU.mult)
                    ctx.inc(v.tensor_tensor(osb[tt % 2][:, sl], osb[tt % 2][:, sl],
                                            x1t[tt % 2][:, sl], ALU.add), S["v_p3"])
                ctx.wait(gp, S["v_p3"])
                ctx.dinc(gp.dma_start(out=out_t[r0:r0 + 128, :], in_=osb[tt % 2][:]),
                         S["d_st"])
            ctx.wait(gp, S["d_st"])

        with ExitStack() as es1:
            E1 = es1.enter_context
            xqT = E1(_sb(nc, "ff_xqT", [128, NKD * TH], BF16))
            wgc = [E1(_sb(nc, f"ff_wg{j}", [128, NKD * 512], BF16))
                   for j in range(2)]
            wuc = [E1(_sb(nc, f"ff_wu{j}", [128, NKD * 512], BF16))
                   for j in range(2)]
            ctx.phase(lambda e: phase1(e, xqT, wgc, wuc))
        with ExitStack() as es2:
            E2 = es2.enter_context
            wdc = E2(_sb(nc, "ff_wd", [128, NIC * D], BF16))
            ctx.phase(lambda e: phase2(e, wdc))


def kernel(L0_attn_x, L0_attn_norm_w, L0_attn_Win, L0_attn_conv_w, L0_attn_conv_b,
           L0_attn_Wx, L0_attn_Wdt, L0_attn_bdt, L0_attn_A_log, L0_attn_Dp,
           L0_attn_Wblk, L0_attn_Wout, L0_attn_bout, L0_ffn_w_gate, L0_ffn_w_up,
           L0_ffn_w_down, L0_c, L0_wmod):
    f32 = np.float32
    x = np.asarray(L0_attn_x, f32)
    cw = np.asarray(L0_attn_conv_w, f32)
    convdiag = np.zeros((NL, NDT, KC, 128, 128), f32)
    idx = np.arange(128)
    for l in range(NL):
        for dt_i in range(NDT):
            for kk in range(KC):
                convdiag[l, dt_i, kk, idx, idx] = cw[l, dt_i * 128:(dt_i + 1) * 128, kk]

    common = {
        "Win": np.ascontiguousarray(np.asarray(L0_attn_Win, f32)),
        "Wx": np.ascontiguousarray(np.asarray(L0_attn_Wx, f32)),
        "Wdt": np.ascontiguousarray(np.asarray(L0_attn_Wdt, f32)),
        "Wblk": np.ascontiguousarray(np.asarray(L0_attn_Wblk, f32)),
        "A_logP": np.ascontiguousarray(
            np.asarray(L0_attn_A_log, f32).reshape(NL, NDT, 128, NST)),
        "convdiag": convdiag,
        "convb_cols": np.ascontiguousarray(
            np.asarray(L0_attn_conv_b, f32).reshape(NL, NDT, 128)),
        "bdt_cols": np.ascontiguousarray(
            np.asarray(L0_attn_bdt, f32).reshape(NL, NDT, 128)),
        "Dp_cols": np.ascontiguousarray(
            np.asarray(L0_attn_Dp, f32).reshape(NL, NDT, 128)),
        "normw_cols": np.ascontiguousarray(
            np.asarray(L0_attn_norm_w, f32).reshape(NL, NKD, 128)),
        "Wout": np.ascontiguousarray(np.asarray(L0_attn_Wout, f32)),
        "bout": np.ascontiguousarray(np.asarray(L0_attn_bout, f32).reshape(1, D)),
        "wmod": np.ascontiguousarray(np.asarray(L0_wmod, f32)),
        "wg": np.ascontiguousarray(np.asarray(L0_ffn_w_gate, f32)),
        "wu": np.ascontiguousarray(np.asarray(L0_ffn_w_up, f32)),
        "wdn": np.ascontiguousarray(np.asarray(L0_ffn_w_down, f32)),
        "ident": np.eye(128, dtype=ml_dtypes.bfloat16),
        "ones_colf": np.ones((128, 1), f32),
        "ones_rowf": np.ones((1, 128), f32),
    }
    cc = np.asarray(L0_c, f32)
    in_maps = []
    for b in range(NCORES):
        m = dict(common)
        xb = np.ascontiguousarray(x[b])
        m["x_T"] = xb
        m["x_D"] = np.ascontiguousarray(xb.T)
        m["c_cols"] = np.ascontiguousarray(cc[b].reshape(NKD, 128).T)
        in_maps.append(m)

    nc = build_program()
    res = run_bass_kernel_spmd(nc, in_maps, core_ids=list(range(NCORES)))
    out = np.stack([np.asarray(res.results[b]["out"], f32) for b in range(NCORES)])
    if DEBUG_TAPS:
        kernel._last = res
    return out
